# version 06
# baseline (speedup 1.0000x reference)
import math
from contextlib import ExitStack

import numpy as np

import concourse.bass as bass
import concourse.mybir as mybir
from concourse.bass_utils import run_bass_kernel_spmd

F32 = mybir.dt.float32
BF16 = mybir.dt.bfloat16
AF = mybir.ActivationFunctionType
ALU = mybir.AluOpType
AX = mybir.AxisListType

N_CORES = 8
B, S, D, H, DH, MLP = 16, 2048, 1024, 16, 64, 4096
BC = B // N_CORES
T = BC * S
NT = T // 128
NE = 16
IM = 512
I1 = 2816
EPS0 = 1e-5
EPS1 = 1e-6
KEPS = 1e-3
RMS_EPS = 1e-6
FP4_MIDS = [0.25, 0.75, 1.25, 1.75, 2.5, 3.5, 5.0]
FP4_STEP = [0.5, 0.5, 0.5, 0.5, 1.0, 1.0, 2.0]
GELU_C = 0.7978845608028654
GELU_A = 0.044715
TEST_NO_B2 = False
TEST_NO_GELU = False



_ENGINES = {
    "pe": "tensor",
    "ve": "vector",
    "se": "scalar",
    "po": "gpsimd",
    "sy": "sync",
}


class _Op:
    __slots__ = ("fn", "args", "kwargs", "kind", "marked", "waits", "sub",
                 "emitted", "tag", "grp", "mval", "grp_sem")

    def __init__(self, fn, args, kwargs, kind, sub, tag=""):
        self.fn = fn
        self.args = args
        self.kwargs = kwargs
        self.kind = kind
        self.marked = kind == "d"
        self.waits = []
        self.sub = sub
        self.emitted = False
        self.tag = tag


class Tok:
    __slots__ = ("stream", "kind", "sub", "op", "clock", "grp")

    def __init__(self, stream, kind, sub, op, clock, grp=None):
        self.stream = stream
        self.kind = kind
        self.sub = sub
        self.op = op
        self.clock = clock
        self.grp = grp


class _Stream:
    def __init__(self, name):
        self.name = name
        self.ops = []
        self.nsub = {"c": 0, "d": 0}
        self.clock = {}
        self.last_tok = None
        self.last_by_kind = {}
        self.marked_prefix = {"c": [0], "d": [0]}
        self.emitted_sub = {"c": 0, "d": 0}


def _key(x):
    if isinstance(x, tuple):
        base, sub = x
        return (_key(base)[0], sub)
    if isinstance(x, bass.AP):
        return (x.tensor.name, None)
    return (x.name, None)


class BK:

    def __init__(self, nc, ctx):
        self.nc = nc
        self.ctx = ctx
        self.streams = {n: _Stream(n) for n in _ENGINES}
        self.sems = {}
        for n in _ENGINES:
            self.sems[(n, "c")] = ctx.enter_context(nc.semaphore(f"p_{n}"))
        self.grp_sem = {}
        self.grp_count = {}
        self.grp_ck = {}
        self.grp_queue = {}
        self.last_w = {}
        self.readers = {}
        self.psum_acc = {}

    SEM_ROLL = 30000

    def _grp(self, grp, inc):
        if grp not in self.grp_sem:
            self.grp_sem[grp] = self.ctx.enter_context(
                self.nc.semaphore(f"g_{grp}"))
            self.grp_count[grp] = 0
            self.grp_raw = getattr(self, "grp_raw", {})
            self.grp_gen = getattr(self, "grp_gen", {})
            self.grp_raw[grp] = 0
            self.grp_gen[grp] = 0
            self.grp_inc = getattr(self, "grp_inc", {})
            self.grp_inc[grp] = inc
        return self.grp_sem[grp]

    def _make_ck(self, grp):
        qn = self.grp_queue[grp]
        qst = self.streams[qn]
        cur = self.grp_count[grp]
        wo = _Op(None, (), {}, "w", 0, f"ck_{grp}")
        wo.grp = None
        wo.waits.append(("S", self.grp_sem[grp],
                         self.grp_inc[grp] * self.grp_raw[grp]))
        qst.ops.append(wo)
        cktok = self.op(qn, "nop", kind="c", mark=True, tag=f"ck_{grp}{cur}")
        cktok.clock[("D", grp)] = cur
        qst.clock[("D", grp)] = cur
        self.grp_ck[grp] = (cur, cktok)
        return self.grp_ck[grp]

    def _maybe_roll(self, grp):
        inc = self.grp_inc[grp]
        if (self.grp_raw[grp] + 1) * inc > self.SEM_ROLL:
            self._make_ck(grp)
            self.grp_gen[grp] += 1
            self.grp_sem[grp] = self.ctx.enter_context(
                self.nc.semaphore(f"g_{grp}_{self.grp_gen[grp]}"))
            self.grp_raw[grp] = 0

    def _dep(self, op, st, tok):
        if tok is None or tok.op is op:
            return
        cl = st.clock
        if tok.grp is not None:
            grp = tok.grp
            if cl.get(("D", grp), 0) >= tok.sub:
                return
            ck = self.grp_ck.get(grp)
            if ck is None or ck[0] < tok.sub:
                ck = self._make_ck(grp)
            self._dep(op, st, ck[1])
            cl[("D", tok.grp)] = max(cl.get(("D", tok.grp), 0), ck[0])
            return
        k = (tok.stream, tok.kind)
        if cl.get(k, 0) >= tok.sub:
            return
        if tok.op.emitted and not tok.op.marked:
            raise RuntimeError(
                f"late dependency on already-emitted unmarked op "
                f"{tok.stream}/{tok.kind}/{tok.sub} {tok.op.fn}")
        tok.op.marked = True
        op.waits.append(tok)
        for kk, vv in tok.clock.items():
            if cl.get(kk, 0) < vv:
                cl[kk] = vv
        cl[k] = max(cl.get(k, 0), tok.sub)

    def op(self, stream, fn, *args, reads=(), writes=(), kind="c",
           accum=False, mark=False, tag="", grp=None, **kwargs):
        st = self.streams[stream]
        st.nsub[kind] += 1
        if kind == "d":
            grp = grp or stream
            self._grp(grp, 16)
            self.grp_queue[grp] = stream
            self._maybe_roll(grp)
        elif stream == "po" and fn != "nop":
            grp = grp or "poQ"
            self._grp(grp, 1)
            self.grp_queue[grp] = stream
            self._maybe_roll(grp)
        else:
            grp = None
        o = _Op(fn, args, kwargs, kind, st.nsub[kind], tag)
        o.grp = grp
        if mark:
            o.marked = True
        psum_names = set()
        for r in list(reads) + list(writes):
            if isinstance(r, bass.AP) and isinstance(
                    r.tensor, bass.PSumTensorHandle):
                psum_names.add(r.tensor.name)
            elif isinstance(r, tuple) and isinstance(
                    r[0], bass.PSumTensorHandle):
                psum_names.add(r[0].name)
            elif isinstance(r, bass.PSumTensorHandle):
                psum_names.add(r.name)
        for r in reads:
            name, sub = _key(r)
            d = self.last_w.get(name)
            if d:
                if sub is None:
                    for tok in d.values():
                        self._dep(o, st, tok)
                else:
                    for s2 in (sub, "*"):
                        if s2 in d:
                            self._dep(o, st, d[s2])
        for w in writes:
            name, sub = _key(w)
            d = self.last_w.get(name)
            if d and not accum:
                if sub is None:
                    for tok in d.values():
                        self._dep(o, st, tok)
                else:
                    for s2 in (sub, "*"):
                        if s2 in d:
                            self._dep(o, st, d[s2])
            for rsub, tok in self.readers.get(name, []):
                if accum:
                    continue
                if sub is None or rsub is None or rsub == sub:
                    self._dep(o, st, tok)
        if psum_names and stream != "pe":
            for name in psum_names:
                for (astream, tok2) in self.psum_acc.get(name, []):
                    if astream != stream:
                        self._dep(o, st, tok2)
        tok = Tok(stream, kind, o.sub, o, dict(st.clock), grp)
        if grp is not None:
            self.grp_count[grp] += 1
            self.grp_raw[grp] += 1
            tok.sub = self.grp_count[grp]
            o.grp_sem = self.grp_sem[grp]
        else:
            tok.clock[(stream, kind)] = o.sub
        for r in reads:
            name, sub = _key(r)
            self.readers.setdefault(name, []).append((sub, tok))
        for w in writes:
            name, sub = _key(w)
            sub = "*" if sub is None else sub
            d = self.last_w.setdefault(name, {})
            if sub == "*":
                d.clear()
            d[sub] = tok
            if not accum:
                self.readers[name] = [
                    (rs, rt) for (rs, rt) in self.readers.get(name, [])
                    if not (sub == "*" or rs is None or rs == sub)]
        if psum_names and stream != "pe":
            for name in psum_names:
                lst = [e for e in self.psum_acc.get(name, [])
                       if e[0] == stream]
                lst.append((stream, tok))
                self.psum_acc[name] = lst
        st.ops.append(o)
        st.last_tok = tok
        st.last_by_kind[kind] = tok
        return tok

    def dma(self, queue, out, in_, reads=(), writes=(), tag="", grp=None, **kw):
        return self.op(queue, "dma_start", out, in_, kind="d",
                       reads=list(reads) + [in_], writes=list(writes) + [out],
                       tag=tag, grp=grp, **kw)

    def barrier(self):
        for grp in list(self.grp_sem):
            ck = self.grp_ck.get(grp)
            if ck is None or ck[0] < self.grp_count[grp]:
                self._make_ck(grp)
        toks = [t for st in self.streams.values()
                for t in st.last_by_kind.values()]
        for n, st in self.streams.items():
            o = _Op(None, (), {}, "w", 0, "barrier")
            o.grp = None
            for t in toks:
                self._dep(o, st, t)
            if o.waits:
                st.ops.append(o)

    def flush(self, blk):
        for st in self.streams.values():
            base = st.emitted_sub.get("mc", 0)
            for o in st.ops:
                if (o.kind == "c" and o.marked and o.grp is None):
                    base += 1
                o.mval = base
            st.emitted_sub["mc"] = base
            assert base < self.SEM_ROLL, f"compute sem overflow on {st.name}" 
        for name, st in self.streams.items():
            if not st.ops:
                continue
            ops = st.ops
            sems = self.sems

            def emit(eng, ops=ops, st=st, sems=sems, self=self):
                def wsem(t):
                    if isinstance(t, tuple):
                        return t[1], t[2]
                    return sems[(t.stream, "c")], t.op.mval
                for o in ops:
                    attach = None
                    if o.kind == "w":
                        for t in o.waits:
                            eng.wait_ge(*wsem(t))
                        o.emitted = True
                        continue
                    if o.waits:
                        attach = o.waits[0]
                        for t in o.waits[1:]:
                            eng.wait_ge(*wsem(t))
                    inst = getattr(eng, o.fn)(*o.args, **o.kwargs)
                    if attach is not None:
                        inst._wait_ge(*wsem(attach))
                    if o.grp is not None:
                        inst.then_inc(o.grp_sem, self.grp_inc[o.grp])
                    elif o.marked:
                        inst.then_inc(sems[(st.name, "c")], 1)
                    o.emitted = True

            getattr(blk, _ENGINES[name])(emit)
            st.ops = []



class Ctx:

    def __init__(self, nc, bk, blk, g):
        self.nc = nc
        self.bk = bk
        self.blk = blk
        self.g = g
        self.dram = {}
        self.pers = {}


def _declare_dram(nc, dbg):
    dd = {}

    def ext(name, shape, dt=F32):
        dd[name] = nc.dram_tensor(name, shape, dt, kind="ExternalInput")

    def scratch(name, shape, dt):
        kind = "ExternalOutput" if name in dbg else "Internal"
        dd[name] = nc.dram_tensor(name, shape, dt, kind=kind)

    ext("x_in", [T, D])
    ext("input_ids", [2, 512], mybir.dt.int32)
    ext("ln1_s", [D]); ext("ln1_b", [D]); ext("ln2_s", [D]); ext("ln2_b", [D])
    ext("wq", [D, D]); ext("wk", [D, D]); ext("wv", [D, D]); ext("wo", [D, D])
    ext("w1", [D, MLP]); ext("b1", [MLP]); ext("w2", [MLP, D]); ext("b2", [D])
    ext("gate_w", [NE, D]); ext("gate_b", [NE])
    ext("w13", [NE, 2 * IM, D]); ext("w2m", [NE, D, IM])
    ext("w13s", [2 * IM, D]); ext("w2s", [D, IM])
    ext("wqkv", [3 * D, D]); ext("wout", [D, D])
    ext("qnw", [DH]); ext("knw", [DH])
    ext("wg", [I1, D]); ext("wu", [I1, D]); ext("wd", [D, I1])
    dd["y_out"] = nc.dram_tensor("y_out", [T, D], F32, kind="ExternalOutput")

    scratch("wq_h", [D, D], BF16); scratch("wk_h", [D, D], BF16)
    scratch("wv_h", [D, D], BF16); scratch("wo_h", [D, D], BF16)
    scratch("w1_h", [D, MLP], BF16); scratch("w2_h", [MLP, D], BF16)
    scratch("w13_h", [NE, 2 * IM, D], BF16); scratch("w2m_h", [NE, D, IM], BF16)
    scratch("w13s_h", [2 * IM, D], BF16); scratch("w2s_h", [D, IM], BF16)
    scratch("wqkv_h", [3 * D, D], BF16); scratch("wout_h", [D, D], BF16)
    scratch("gate_w_h", [NE, D], BF16)
    scratch("wgq_h", [I1, D], BF16); scratch("wuq_h", [I1, D], BF16)
    scratch("wdq_h", [D, I1], BF16)

    scratch("xn0", [T, D], F32)
    scratch("xn0_h", [T, D], BF16)
    scratch("xln1_h", [T, D], BF16)
    scratch("x1", [T, D], F32)
    scratch("xln2_h", [T, D], BF16)
    scratch("g0", [T, D], F32)
    scratch("hm_h", [T, IM], BF16)
    scratch("x_res", [T, D], F32)
    scratch("xn1_h", [T, D], BF16)
    scratch("xq_h", [T, D], BF16)
    scratch("qro_h", [T, D], BF16)
    scratch("kro_h", [T, D], BF16)
    scratch("ao_h", [T, D], BF16)
    scratch("vro_h", [T, H * 65], BF16)
    scratch("hq_h", [T, I1], BF16)
    scratch("g1", [T, D], F32)
    return dd


def build_program(dbg=(), stages=("prep", "A", "D", "B", "C", "E", "G", "F")):
    nc = bass.Bass("TRN2", target_bir_lowering=False)
    dd = _declare_dram(nc, set(dbg))
    g = ExitStack()
    with g:
        bk = BK(nc, g)
        blk = g.enter_context(nc.Block())
        cx = Ctx(nc, bk, blk, g)
        cx.dram = dd
        stage_prep(cx, "prep" in stages)
        if "A" in stages:
            stage_A(cx)
        if "D" in stages:
            stage_D(cx)
        if "B" in stages:
            stage_B(cx)
        if "C" in stages:
            stage_C(cx)
        if "E" in stages:
            stage_E(cx)
        if "G" in stages:
            stage_G(cx)
        if "F" in stages:
            stage_F(cx)
        bk.barrier()
        bk.flush(blk)
    return nc


def _end_stage(cx, label=None):
    cx.bk.barrier()
    if label:
        with cx.nc.named_scope(label):
            cx.bk.flush(cx.blk)
    else:
        cx.bk.flush(cx.blk)
    cx.bk.last_w.clear()
    cx.bk.readers.clear()



def stage_prep(cx, do_casts):
    bk, nc, dd, g = cx.bk, cx.nc, cx.dram, cx.g

    if do_casts:
        casts = [("wq", "wq_h"), ("wk", "wk_h"), ("wv", "wv_h"),
                 ("wo", "wo_h"), ("w1", "w1_h"), ("w2", "w2_h"),
                 ("w13s", "w13s_h"), ("w2s", "w2s_h"),
                 ("wqkv", "wqkv_h"), ("wout", "wout_h"), ("gate_w", "gate_w_h")]
        for e in range(NE):
            casts.append((("w13", e), ("w13_h", e)))
            casts.append((("w2m", e), ("w2m_h", e)))
        for src, dst in casts:
            if isinstance(src, tuple):
                s_ap = dd[src[0]][src[1]]
                d_ap = dd[dst[0]][dst[1]]
                rd, wr = (dd[src[0]], src[1]), (dd[dst[0]], dst[1])
            else:
                s_ap, d_ap = dd[src][:], dd[dst][:]
                rd, wr = dd[src], dd[dst]
            bk.op("po", "dma_start", d_ap, s_ap, kind="d",
                  reads=[rd], writes=[wr], tag=f"cast_{src}")

    pers = cx.pers
    ident1 = g.enter_context(nc.sbuf_tensor("ident1", [1, 1], F32))
    bk.op("po", "memset", ident1[:], 1.0, writes=[ident1])
    vecs = [("ln1_s", D), ("ln1_b", D), ("ln2_s", D), ("ln2_b", D),
            ("b1", MLP), ("b2", D)]
    for vec, n in vecs:
        pers[vec] = g.enter_context(
            nc.sbuf_tensor(f"pp_{vec}", [128, n // 128], F32))
    with nc.sbuf_tensor("vrow", [1, MLP], F32) as vrow, \
         nc.psum_tensor("vps", [128, MLP // 128], F32) as vps:
        for vec, n in vecs:
            a = n // 128
            t = pers[vec]
            bk.dma("sy", vrow[:, 0:n], dd[vec][None, :], writes=[vrow],
                   tag=f"vec_{vec}")
            for c in range(a):
                bk.op("pe", "transpose", vps[:, c:c + 1],
                      vrow[:, c * 128:(c + 1) * 128], ident1[:],
                      reads=[vrow, ident1], writes=[(vps, c)], tag=f"vec_{vec}")
            bk.op("ve", "tensor_copy", t[:, 0:a], vps[:, 0:a],
                  reads=[(vps, cc) for cc in range(a)], writes=[t],
                  mark=True, tag=f"vec_{vec}")
        bk.barrier()
        bk.flush(cx.blk)
    ones1 = g.enter_context(nc.sbuf_tensor("ones1", [1, 128], BF16))
    bk.op("po", "memset", ones1[:], 1.0, writes=[ones1])
    brows = [("gate_b", NE), ("qnw", DH), ("knw", DH)]
    for vec, n in brows:
        pers[vec] = g.enter_context(nc.sbuf_tensor(f"rep_{vec}", [128, n], F32))
    with nc.sbuf_tensor("brow_b", [1, DH], BF16) as row, \
         nc.sbuf_tensor("brow_f", [1, DH], F32) as rowf, \
         nc.psum_tensor("brow_ps", [128, DH], F32) as ps:
        for vec, n in brows:
            bk.dma("sy", rowf[:, 0:n], dd[vec][None, :], writes=[rowf],
                   tag=f"row_{vec}", grp="p_vec")
            bk.op("ve", "tensor_copy", row[:, 0:n], rowf[:, 0:n],
                  reads=[rowf], writes=[row])
            bk.op("pe", "matmul", ps[:, 0:n], ones1[:], row[:, 0:n],
                  start=True, stop=True, reads=[ones1, row], writes=[ps])
            bk.op("ve", "tensor_copy", pers[vec][:, 0:n], ps[:, 0:n],
                  reads=[ps], writes=[pers[vec]], mark=True)
        bk.barrier()
        bk.flush(cx.blk)
    mask = g.enter_context(nc.sbuf_tensor("mask_incl", [128, 128], F32))
    bk.op("po", "iota", mask[:], [[1, 128]], base=0, channel_multiplier=-1,
          allow_small_or_imprecise_dtypes=True, writes=[mask])
    bk.op("po", "tensor_scalar", mask[:], mask[:], 0.0, None, ALU.is_ge,
          reads=[mask], writes=[mask])
    pers["mask_incl"] = mask
    identB = g.enter_context(nc.sbuf_tensor("identB", [128, 128], BF16))
    iotaB = g.enter_context(nc.sbuf_tensor("iotaB", [128, 128], F32))
    bk.op("po", "iota", iotaB[:], [[1, 128]], base=0, channel_multiplier=-1,
          allow_small_or_imprecise_dtypes=True, writes=[iotaB])
    bk.op("po", "tensor_scalar", identB[:], iotaB[:], 0.0, None, ALU.is_equal,
          reads=[iotaB], writes=[identB])
    pers["identB"] = identB
    onec = g.enter_context(nc.sbuf_tensor("onec", [128, 1], BF16))
    bk.op("po", "memset", onec[:], 1.0, writes=[onec])
    pers["onec"] = onec
    onesP = g.enter_context(nc.sbuf_tensor("onesP", [128, 64], BF16))
    bk.op("po", "memset", onesP[:], 1.0, writes=[onesP])
    pers["onesP"] = onesP
    onesF = g.enter_context(nc.sbuf_tensor("onesF", [1, 128], F32))
    bk.op("po", "memset", onesF[:], 1.0, writes=[onesF])
    pers["onesF"] = onesF
    b2row2 = g.enter_context(nc.sbuf_tensor("b2row2", [1, D], F32))
    with nc.sbuf_tensor("b2row", [1, D], F32) as b2row:
        bk.dma("sy", b2row[:], dd["b2"][None, :], writes=[b2row], tag="b2row",
               grp="p_vec")
        bk.op("ve", "tensor_scalar_mul", b2row2[:], b2row[:], 2.0,
              reads=[b2row], writes=[b2row2], mark=True)
        bk.barrier()
        bk.flush(cx.blk)
    pers["b2row2"] = b2row2
    _end_stage(cx)



def _ln_tile(bk, sb, se_eps, x_ap, out_f32, out_bf, bufs, tag):
    negm, sq, vs, v1, rstd = bufs
    ve, se = "ve", "se"
    bk.op(ve, "reduce_sum", negm[:], x_ap, axis=AX.X, negate=True,
          reads=[x_ap], writes=[negm], tag=tag)
    bk.op(ve, "tensor_scalar_mul", negm[:], negm[:], 1.0 / D,
          reads=[negm], writes=[negm], tag=tag)
    bk.op(se, "activation", sq[:], x_ap, AF.Square, bias=negm[:, 0:1],
          accum_out=vs[:, 0:1], reads=[x_ap, negm], writes=[sq, vs], tag=tag)
    bk.op(ve, "tensor_scalar", v1[:], vs[:], 1.0 / D, se_eps, ALU.mult, ALU.add,
          reads=[vs], writes=[v1], tag=tag)
    bk.op(se, "activation", rstd[:], v1[:], AF.Ln, reads=[v1], writes=[rstd],
          tag=tag)
    bk.op(se, "activation", rstd[:], rstd[:], AF.Exp, scale=-0.5,
          reads=[rstd], writes=[rstd], tag=tag)
    outs = []
    if out_f32 is not None:
        bk.op(ve, "tensor_scalar", out_f32[:], x_ap, negm[:, 0:1], rstd[:, 0:1],
              ALU.add, ALU.mult, reads=[x_ap, negm, rstd], writes=[out_f32],
              tag=tag)
    if out_bf is not None:
        src = out_f32 if out_f32 is not None else None
        if src is not None:
            bk.op(ve, "tensor_copy", out_bf[:], src[:], reads=[src],
                  writes=[out_bf], tag=tag)
        else:
            bk.op(ve, "tensor_scalar", out_bf[:], x_ap, negm[:, 0:1],
                  rstd[:, 0:1], ALU.add, ALU.mult,
                  reads=[x_ap, negm, rstd], writes=[out_bf], tag=tag)


def stage_A(cx):
    bk, nc, dd = cx.bk, cx.nc, cx.dram
    with ExitStack() as st:
        sb = lambda n, sh, d: st.enter_context(nc.sbuf_tensor(n, sh, d))
        NB = 3
        xt = [sb(f"a_x{i}", [128, D], F32) for i in range(NB)]
        xn = [sb(f"a_xn{i}", [128, D], F32) for i in range(NB)]
        xnb = [sb(f"a_xnb{i}", [128, D], BF16) for i in range(NB)]
        xlb = [sb(f"a_xlb{i}", [128, D], BF16) for i in range(NB)]
        sq = [sb(f"a_sq{i}", [128, D], F32) for i in range(NB)]
        cols = [[sb(f"a_c{i}_{j}", [128, 1], F32) for j in range(4)]
                for i in range(NB)]
        for t in range(NT):
            i = t % NB
            x_ap = dd["x_in"][t * 128:(t + 1) * 128, :]
            bk.dma("sy", xt[i][:], x_ap, tag=f"A{t}", grp="a_in")
            negm, vs, v1, rstd = cols[i]
            _ln_tile(bk, sb, EPS0, xt[i][:], xn[i], xnb[i],
                     (negm, sq[i], vs, v1, rstd), f"A{t}")
            bk.dma("sy", dd["xn0"][t * 128:(t + 1) * 128, :], xn[i][:],
                   reads=[xn[i]], writes=[(dd["xn0"], t)], tag=f"A{t}",
                   grp="a_out")
            bk.dma("sy", dd["xn0_h"][t * 128:(t + 1) * 128, :], xnb[i][:],
                   reads=[xnb[i]], writes=[(dd["xn0_h"], t)], tag=f"A{t}",
                   grp="a_out")
            _ln_tile(bk, sb, EPS1, xn[i][:], None, xlb[i],
                     (negm, sq[i], vs, v1, rstd), f"A{t}l")
            bk.dma("sy", dd["xln1_h"][t * 128:(t + 1) * 128, :], xlb[i][:],
                   reads=[xlb[i]], writes=[(dd["xln1_h"], t)], tag=f"A{t}",
                   grp="a_out")
        _end_stage(cx, "stA")


def stage_D(cx):
    bk, nc, dd, pers = cx.bk, cx.nc, cx.dram, cx.pers
    TH = T // 2
    NK = TH // 128
    with ExitStack() as st:
        sb = lambda n, sh, d: st.enter_context(nc.sbuf_tensor(n, sh, d))
        ps_ = lambda n, sh: st.enter_context(nc.psum_tensor(n, sh, F32))
        gwn = sb("d_gwn", [16, D], BF16)
        gwT = sb("d_gwT", [128, 8 * NE], BF16)
        psg = st.enter_context(nc.psum_tensor("d_psg", [128, NE], BF16))
        bk.dma("sy", gwn[:], dd["gate_w_h"][:], tag="gwT", grp="d_x")
        for c in range(8):
            bk.op("pe", "transpose", psg[:], gwn[:, c * 128:(c + 1) * 128],
                  pers["identB"][0:16, 0:16], reads=[gwn, pers["identB"]],
                  writes=[psg], tag="gwT")
            bk.op("ve", "tensor_copy", gwT[:, c * NE:(c + 1) * NE], psg[:],
                  reads=[psg], writes=[(gwT, c)], tag="gwT")
        xT = [sb(f"d_xT{c}", [128, TH], BF16) for c in range(8)]
        w13T = [[sb(f"d_w13T{p}_{c}", [128, 2 * IM], BF16) for c in range(8)]
                for p in range(2)]
        w2T = [[sb(f"d_w2T{p}_{c}", [128, D], BF16) for c in range(4)]
               for p in range(2)]
        hT = [[sb(f"d_hT{pp_}_{c}", [128, 512], BF16) for c in range(4)]
              for pp_ in range(2)]
        y_sb = sb("d_y", [128, NK * D], F32)
        comb = sb("d_comb", [128, NK * NE], F32)
        S_sb = sb("d_S", [128, NE], F32)
        biased = sb("d_bias", [128, NE], F32)
        eq = sb("d_eq", [128, NE], F32)
        tmp = sb("d_tmp", [128, NE], F32)
        mx = sb("d_mx", [128, 1], F32)
        ssum = sb("d_ssum", [128, 1], F32)
        sg = [sb(f"d_sg{p}", [128, 512], F32) for p in range(2)]
        hbf = [sb(f"d_hbf{p}", [128, 512], BF16) for p in range(2)]
        g0t = sb("d_g0t", [128, D], F32)
        psc = ps_("d_psc", [128, NE])
        pm = [ps_(f"d_pm{i}", [128, 512]) for i in range(4)]
        py = [ps_(f"d_py{i}", [128, 512]) for i in range(2)]
        for hf in range(2):
            t0 = hf * TH
            for c in range(8):
                bk.dma("sy", xT[c][:],
                       dd["xn0_h"][t0:t0 + TH, c * 128:(c + 1) * 128],
                       transpose=True,
                       reads=[(dd["xn0_h"], t) for t in range(hf * NK, (hf + 1) * NK)],
                       writes=[xT[c]], tag=f"D{hf}xT", grp="d_x")
            for k in range(NK):
                kk = slice(k * 128, (k + 1) * 128)
                for c in range(8):
                    bk.op("pe", "matmul", psc[:], xT[c][:, kk],
                          gwT[:, c * NE:(c + 1) * NE],
                          start=(c == 0), stop=(c == 7),
                          reads=[xT[c], (gwT, c)], writes=[psc],
                          accum=(c > 0), tag=f"D{hf}r{k}")
                bk.op("se", "activation", S_sb[:], psc[:], AF.Sigmoid,
                      reads=[psc], writes=[S_sb], tag=f"D{hf}r{k}")
                bk.op("ve", "tensor_add", biased[:], S_sb[:], pers["gate_b"][:, 0:NE],
                      reads=[S_sb, pers["gate_b"]], writes=[biased], tag=f"D{hf}r{k}")
                cslice = slice(k * NE, (k + 1) * NE)
                for r in range(4):
                    bk.op("ve", "reduce_max", mx[:], biased[:], axis=AX.X,
                          reads=[biased], writes=[mx], tag=f"D{hf}r{k}")
                    bk.op("ve", "tensor_single_scalar", eq[:], biased[:],
                          mx[:, 0:1], ALU.is_equal,
                          reads=[biased, mx], writes=[eq], tag=f"D{hf}r{k}")
                    if r == 0:
                        bk.op("ve", "tensor_mul", comb[:, cslice], eq[:], S_sb[:],
                              reads=[eq, S_sb], writes=[(comb, k)], tag=f"D{hf}r{k}")
                    else:
                        bk.op("ve", "tensor_mul", tmp[:], eq[:], S_sb[:],
                              reads=[eq, S_sb], writes=[tmp], tag=f"D{hf}r{k}")
                        bk.op("ve", "tensor_add", comb[:, cslice], comb[:, cslice],
                              tmp[:], reads=[(comb, k), tmp], writes=[(comb, k)],
                              tag=f"D{hf}r{k}")
                    if r < 3:
                        bk.op("ve", "tensor_scalar_mul", tmp[:], eq[:], -1e9,
                              reads=[eq], writes=[tmp], tag=f"D{hf}r{k}")
                        bk.op("ve", "tensor_add", biased[:], biased[:], tmp[:],
                              reads=[biased, tmp], writes=[biased], tag=f"D{hf}r{k}")
                bk.op("ve", "reduce_sum", ssum[:], comb[:, cslice], axis=AX.X,
                      reads=[(comb, k)], writes=[ssum], tag=f"D{hf}r{k}")
                bk.op("ve", "reciprocal", ssum[:], ssum[:],
                      reads=[ssum], writes=[ssum], tag=f"D{hf}r{k}")
                bk.op("ve", "tensor_scalar_mul", ssum[:], ssum[:], 2.5,
                      reads=[ssum], writes=[ssum], tag=f"D{hf}r{k}")
                bk.op("ve", "tensor_single_scalar", comb[:, cslice], comb[:, cslice],
                      ssum[:, 0:1], ALU.mult,
                      reads=[(comb, k), ssum], writes=[(comb, k)], tag=f"D{hf}r{k}")
            pend = [None]
            for e in range(NE + 1):
                p = e % 2
                shared = e == NE
                w13src = dd["w13s_h"][:] if shared else dd["w13_h"][e]
                w2src = dd["w2s_h"][:] if shared else dd["w2m_h"][e]
                w13key = dd["w13s_h"] if shared else (dd["w13_h"], e)
                w2key = dd["w2s_h"] if shared else (dd["w2m_h"], e)
                for c in range(8):
                    bk.dma("sy", w13T[p][c][:], w13src[:, c * 128:(c + 1) * 128],
                           transpose=True, reads=[w13key], writes=[w13T[p][c]],
                           tag=f"D{hf}e{e}w", grp=f"d_w{p}")
                for c in range(4):
                    bk.dma("sy", w2T[p][c][:], w2src[:, c * 128:(c + 1) * 128],
                           transpose=True, reads=[w2key], writes=[w2T[p][c]],
                           tag=f"D{hf}e{e}w", grp=f"d_w{p}")
                def emit_mm1(q):
                    for kq in range(4):
                        k = q * 4 + kq
                        kk = slice(k * 128, (k + 1) * 128)
                        pg, pu = pm[2 * (k % 2)], pm[2 * (k % 2) + 1]
                        tg = f"D{hf}e{e}k{k}"
                        for half_o in range(2):
                            dst = pg if half_o == 0 else pu
                            for c in range(8):
                                bk.op("pe", "matmul", dst[:],
                                      xT[c][:, kk],
                                      w13T[p][c][:, half_o * IM:(half_o + 1) * IM],
                                      start=(c == 0), stop=(c == 7),
                                      reads=[xT[c], w13T[p][c]], writes=[dst],
                                      accum=(c > 0), tag=tg)
                        bk.op("se", "activation", sg[k % 2][:], pg[:], AF.Silu,
                              reads=[pg], writes=[sg[k % 2]], tag=tg)
                        if shared:
                            bk.op("ve", "scalar_tensor_tensor", hbf[k % 2][:],
                                  sg[k % 2][:], 1.0, pu[:], ALU.mult, ALU.mult,
                                  reads=[sg[k % 2], pu], writes=[hbf[k % 2]],
                                  tag=tg)
                        else:
                            bk.op("ve", "scalar_tensor_tensor", hbf[k % 2][:],
                                  sg[k % 2][:], comb[:, k * NE + e:k * NE + e + 1],
                                  pu[:], ALU.mult, ALU.mult,
                                  reads=[sg[k % 2], pu, (comb, k)],
                                  writes=[hbf[k % 2]], tag=tg)
                        bk.dma("sy", dd["hm_h"][t0 + k * 128:t0 + (k + 1) * 128, :],
                               hbf[k % 2][:], reads=[hbf[k % 2]],
                               writes=[(dd["hm_h"], (hf, k))],
                               tag=tg, grp="d_hs")

                def emit_hload(q):
                    q0 = t0 + q * 512
                    for c in range(4):
                        bk.dma("sy", hT[q % 2][c][:],
                               dd["hm_h"][q0:q0 + 512, c * 128:(c + 1) * 128],
                               transpose=True,
                               reads=[(dd["hm_h"], (hf, q * 4 + j)) for j in range(4)],
                               writes=[hT[q % 2][c]], tag=f"D{hf}e{e}q{q}",
                               grp="d_ht")

                def emit_mm2(q, e2, p2):
                    for kq in range(4):
                        k = q * 4 + kq
                        tg = f"D{hf}e{e2}y{k}"
                        for b in range(2):
                            pyb = py[b]
                            for c in range(4):
                                bk.op("pe", "matmul", pyb[:],
                                      hT[q % 2][c][:, kq * 128:(kq + 1) * 128],
                                      w2T[p2][c][:, b * 512:(b + 1) * 512],
                                      start=(c == 0), stop=(c == 3),
                                      reads=[hT[q % 2][c], w2T[p2][c]], writes=[pyb],
                                      accum=(c > 0), tag=tg)
                            ys = y_sb[:, k * D + b * 512:k * D + (b + 1) * 512]
                            if e2 == 0:
                                bk.op("ve", "tensor_copy", ys, pyb[:],
                                      reads=[pyb], writes=[(y_sb, (k, b))],
                                      tag=tg)
                            else:
                                bk.op("ve", "tensor_add", ys, ys, pyb[:],
                                      reads=[pyb, (y_sb, (k, b))],
                                      writes=[(y_sb, (k, b))], tag=tg)

                for q in range(NK // 4):
                    emit_mm1(q)
                    emit_hload(q)
                    if pend[0] is not None:
                        emit_mm2(*pend[0])
                    pend[0] = (q, e, p)
            if pend[0] is not None:
                emit_mm2(*pend[0])
                pend[0] = None
            for k in range(NK):
                bk.op("se", "activation", g0t[:], y_sb[:, k * D:(k + 1) * D],
                      AF.Silu, reads=[(y_sb, (k, 0)), (y_sb, (k, 1))],
                      writes=[g0t], tag=f"D{hf}g{k}")
                bk.dma("sy", dd["g0"][t0 + k * 128:t0 + (k + 1) * 128, :], g0t[:],
                       reads=[g0t], writes=[(dd["g0"], hf * NK + k)],
                       tag=f"D{hf}g{k}", grp="d_gs")
        _end_stage(cx, "stD")


def stage_B(cx):
    bk, nc, dd, pers = cx.bk, cx.nc, cx.dram, cx.pers
    TH = T // 2
    NK = TH // 128
    identB = pers["identB"]
    with ExitStack() as st:
        sb = lambda n, sh, d: st.enter_context(nc.sbuf_tensor(n, sh, d))
        ps_ = lambda n, sh, d=F32: st.enter_context(nc.psum_tensor(n, sh, d))
        phiq = [sb(f"b_phiq{c}", [128, TH], BF16) for c in range(8)]
        phik = [sb(f"b_phik{c}", [128, TH], BF16) for c in range(8)]
        v_sb = sb("b_v", [128, NK * D], BF16)
        atn = [sb(f"b_atn{c}", [128, TH], BF16) for c in range(8)]
        state_f = sb("b_stf", [128, 65], F32)
        state_b = sb("b_stb", [128, 65], BF16)
        Am = [sb(f"b_Am{i}", [128, 128], BF16) for i in range(2)]
        kj = [sb(f"b_kj{i}", [128, 64], BF16) for i in range(2)]
        recip_f = sb("b_recf", [128, 128], F32)
        recip_b = sb("b_recb", [128, 128], BF16)
        bb = sb("b_bb", [128, 128], F32)

        for hf in range(2):
            t0 = hf * TH
            with ExitStack() as stq:
                psq = [stq.enter_context(nc.psum_tensor(
                    f"b1_psq{hf}_{i}", [128, 512], F32)) for i in range(2)]
                xlT = [stq.enter_context(nc.sbuf_tensor(f"b_xlT{hf}_{c}", [128, 512], BF16))
                       for c in range(8)]
                wq_sb = [stq.enter_context(nc.sbuf_tensor(f"b_wq{hf}_{c}", [128, D], BF16))
                         for c in range(8)]
                wk_sb = [stq.enter_context(nc.sbuf_tensor(f"b_wk{hf}_{c}", [128, D], BF16))
                         for c in range(8)]
                wv_sb = [stq.enter_context(nc.sbuf_tensor(f"b_wv{hf}_{c}", [128, D], BF16))
                         for c in range(8)]
                for c in range(8):
                    bk.dma("sy", wq_sb[c][:], dd["wq_h"][c * 128:(c + 1) * 128, :],
                           reads=[dd["wq_h"]], tag="Bw", grp="b_w")
                    bk.dma("sy", wk_sb[c][:], dd["wk_h"][c * 128:(c + 1) * 128, :],
                           reads=[dd["wk_h"]], tag="Bw", grp="b_w")
                    bk.dma("sy", wv_sb[c][:], dd["wv_h"][c * 128:(c + 1) * 128, :],
                           reads=[dd["wv_h"]], tag="Bw", grp="b_w")
                for qtr in range(4):
                    q0 = t0 + qtr * 512
                    for c in range(8):
                        bk.dma("sy", xlT[c][:],
                               dd["xln1_h"][q0:q0 + 512, c * 128:(c + 1) * 128],
                               transpose=True,
                               reads=[(dd["xln1_h"], q0 // 128 + j) for j in range(4)],
                               writes=[xlT[c]], tag=f"B1x{hf}{qtr}", grp="b_x")
                        bk.op("ve", "tensor_scalar", xlT[c][:], xlT[c][:],
                              pers["ln1_s"][:, c:c + 1],
                              pers["ln1_b"][:, c:c + 1], ALU.mult, ALU.add,
                              reads=[xlT[c], pers["ln1_s"], pers["ln1_b"]],
                              writes=[xlT[c]], tag=f"B1a{hf}{qtr}")
                    for dst, wsb, nm in ((phiq, wq_sb, "q"), (phik, wk_sb, "k")):
                        for dc in range(8):
                            pq = psq[dc % 2]
                            for c in range(8):
                                bk.op("pe", "matmul", pq[:],
                                      wsb[c][:, dc * 128:(dc + 1) * 128],
                                      xlT[c][:], start=(c == 0), stop=(c == 7),
                                      reads=[wsb[c], xlT[c]], writes=[pq],
                                      accum=(c > 0), tag=f"B1{nm}{hf}{qtr}{dc}")
                            bk.op("ve", "tensor_scalar",
                                  dst[dc][:, qtr * 512:(qtr + 1) * 512], pq[:],
                                  0.0, KEPS, ALU.max, ALU.add,
                                  reads=[pq], writes=[(dst[dc], qtr)],
                                  tag=f"B1{nm}e{hf}{qtr}{dc}")
                    for j in range(4):
                        tt = qtr * 4 + j
                        for b2 in range(2):
                            pv = psq[b2]
                            for c in range(8):
                                bk.op("pe", "matmul", pv[:],
                                      xlT[c][:, j * 128:(j + 1) * 128],
                                      wv_sb[c][:, b2 * 512:(b2 + 1) * 512],
                                      start=(c == 0), stop=(c == 7),
                                      reads=[xlT[c], wv_sb[c]], writes=[pv],
                                      accum=(c > 0), tag=f"B1v{hf}{tt}")
                            bk.op("ve", "tensor_copy",
                                  v_sb[:, tt * D + b2 * 512:tt * D + (b2 + 1) * 512],
                                  pv[:], reads=[pv], writes=[(v_sb, (tt, b2))],
                                  tag=f"B1ve{hf}{tt}")
            bk.barrier()
            st2 = ExitStack()
            psA = [st2.enter_context(nc.psum_tensor(f"b2_psA{i}_{hf}",
                   [128, 128], F32)) for i in range(2)]
            psT = [st2.enter_context(nc.psum_tensor(f"b2_psT{i}_{hf}",
                   [128, 64], BF16)) for i in range(2)]
            psN = st2.enter_context(nc.psum_tensor(f"b2_psN_{hf}",
                                                   [128, 128], F32))
            psDen = st2.enter_context(nc.psum_tensor(f"b2_psDen_{hf}",
                                                     [128, 128], F32))
            psB = st2.enter_context(nc.psum_tensor(f"b2_psB_{hf}",
                                                   [128, 128], F32))
            psD = st2.enter_context(nc.psum_tensor(f"b2_psD_{hf}",
                                                   [128, 65], F32))
            for h in range(H):
                o = (h % 2) * 64
                P2 = h // 2
                os_ = slice(o, o + 64)
                bk.op("ve", "memset", state_f[os_, :], 0.0,
                      writes=[(state_f, o)], tag=f"B2z{hf}{h}")
                bk.op("ve", "memset", state_b[os_, :], 0.0,
                      writes=[(state_b, o)], tag=f"B2z{hf}{h}")
                for c in range(NK):
                    cs = slice(c * 128, (c + 1) * 128)
                    pq_ap = phiq[P2][os_, cs]
                    pk_ap = phik[P2][os_, cs]
                    v_c = v_sb[:, c * D + h * 64:c * D + h * 64 + 64]
                    tg = f"B2{hf}h{h}c{c}"
                    bk.op("pe", "matmul", psA[c % 2][:], pk_ap, pq_ap, start=True,
                          stop=True, reads=[(phik[P2], c), (phiq[P2], c)],
                          writes=[psA[c % 2]], tag=tg)
                    bk.op("ve", "tensor_tensor", Am[c % 2][:], psA[c % 2][:],
                          pers["mask_incl"][:], ALU.mult,
                          reads=[psA[c % 2], pers["mask_incl"]], writes=[Am[c % 2]], tag=tg)
                    bk.op("pe", "transpose", psT[c % 2][:], pk_ap,
                          identB[os_, os_], reads=[(phik[P2], c), identB],
                          writes=[psT[c % 2]], tag=tg)
                    bk.op("ve", "tensor_copy", kj[c % 2][:], psT[c % 2][:],
                          reads=[psT[c % 2]], writes=[kj[c % 2]], tag=tg)
                    bk.op("pe", "matmul", psN[os_, :], state_b[os_, 0:64],
                          pq_ap, start=True, stop=False, tile_position=(o, o),
                          reads=[(state_b, o), (phiq[P2], c)], writes=[psN],
                          tag=tg)
                    bk.op("pe", "matmul", psN[os_, :], v_c, Am[c % 2][:],
                          start=False, stop=True, tile_position=(0, o),
                          reads=[(v_sb, (c, 0)), (v_sb, (c, 1)), Am[c % 2]],
                          writes=[psN], accum=True, tag=tg)
                    bk.op("pe", "matmul", psDen[o:o + 1, :],
                          state_b[os_, 64:65], pq_ap, start=True, stop=False,
                          tile_position=(o, o),
                          reads=[(state_b, o), (phiq[P2], c)], writes=[psDen],
                          tag=tg)
                    bk.op("pe", "matmul", psDen[o:o + 1, :], pers["onec"][:],
                          Am[c % 2][:], start=False, stop=True, tile_position=(0, o),
                          reads=[pers["onec"], Am[c % 2]], writes=[psDen], accum=True,
                          tag=tg)
                    bk.op("pe", "matmul", psD[os_, 0:64], kj[c % 2][:], v_c,
                          start=True, stop=True, tile_position=(0, o),
                          reads=[kj[c % 2], (v_sb, (c, 0)), (v_sb, (c, 1))],
                          writes=[psD], tag=tg)
                    bk.op("pe", "matmul", psD[os_, 64:65], kj[c % 2][:],
                          pers["onec"][:], start=True, stop=True,
                          tile_position=(0, o), reads=[kj[c % 2], pers["onec"]],
                          writes=[psD], accum=True, tag=tg)
                    bk.op("ve", "tensor_add", state_f[os_, :], state_f[os_, :],
                          psD[os_, :], reads=[(state_f, o), psD],
                          writes=[(state_f, o)], tag=tg)
                    bk.op("ve", "tensor_copy", state_b[os_, :], state_f[os_, :],
                          reads=[(state_f, o)], writes=[(state_b, o)], tag=tg)
                    bk.op("ve", "reciprocal", recip_f[o:o + 1, :],
                          psDen[o:o + 1, :], reads=[psDen],
                          writes=[recip_f], tag=tg)
                    bk.op("ve", "tensor_copy", recip_b[o:o + 1, :],
                          recip_f[o:o + 1, :], reads=[recip_f],
                          writes=[recip_b], tag=tg)
                    bk.op("pe", "matmul", psB[os_, :],
                          pers["onesP"][o:o + 1, :], recip_b[o:o + 1, :],
                          start=True, stop=True, tile_position=(o, o),
                          reads=[pers["onesP"], recip_b], writes=[psB], tag=tg)
                    bk.op("ve", "tensor_copy", bb[os_, :], psB[os_, :],
                          reads=[psB], writes=[bb], tag=tg)
                    bk.op("ve", "tensor_tensor", atn[P2][os_, cs], psN[os_, :],
                          bb[os_, :], ALU.mult, reads=[psN, bb],
                          writes=[(atn[P2], (o, c))], tag=tg)
            bk.barrier()
            st2.close()
            with ExitStack() as st3:
                psq = [st3.enter_context(nc.psum_tensor(
                    f"b3_psq{hf}_{i}", [128, 512], F32)) for i in range(2)]
                wo_sb = [st3.enter_context(
                    nc.sbuf_tensor(f"b_wo{hf}_{c}", [128, D], BF16))
                    for c in range(8)]
                for c in range(8):
                    bk.dma("sy", wo_sb[c][:],
                           dd["wo_h"][c * 128:(c + 1) * 128, :],
                           reads=[dd["wo_h"]], tag="Bw", grp="b_w")
                x1sb = st3.enter_context(nc.sbuf_tensor(f"b3_x1_{hf}", [128, D], F32))
                xnsb = st3.enter_context(nc.sbuf_tensor(f"b3_xn_{hf}", [128, D], F32))
                xl2b = st3.enter_context(nc.sbuf_tensor(f"b3_xl2_{hf}", [128, D], BF16))
                sq3 = st3.enter_context(nc.sbuf_tensor(f"b3_sq_{hf}", [128, D], F32))
                cols3 = [st3.enter_context(nc.sbuf_tensor(f"b3_c{hf}_{j}", [128, 1], F32))
                         for j in range(4)]
                for tt in range(NK):
                    tg = f"B3{hf}t{tt}"
                    ta = t0 + tt * 128
                    bk.dma("sy", xnsb[:], dd["xn0"][ta:ta + 128, :],
                           reads=[(dd["xn0"], ta // 128)], writes=[xnsb],
                           tag=tg, grp="b_x")
                    for b2 in range(2):
                        pw = psq[b2]
                        for c in range(8):
                            bk.op("pe", "matmul", pw[:],
                                  atn[c][:, tt * 128:(tt + 1) * 128],
                                  wo_sb[c][:, b2 * 512:(b2 + 1) * 512],
                                  start=(c == 0), stop=(c == 7),
                                  reads=[(atn[c], (0, tt)), (atn[c], (64, tt)),
                                         wo_sb[c]],
                                  writes=[pw], accum=(c > 0), tag=tg)
                        bk.op("ve", "tensor_add",
                              x1sb[:, b2 * 512:(b2 + 1) * 512], pw[:],
                              xnsb[:, b2 * 512:(b2 + 1) * 512],
                              reads=[pw, xnsb], writes=[x1sb], tag=tg)
                    bk.dma("sy", dd["x1"][ta:ta + 128, :], x1sb[:],
                           reads=[x1sb], writes=[(dd["x1"], ta // 128)],
                           tag=tg, grp="b_out")
                    negm, vs, v1c, rstd = cols3
                    _ln_tile(bk, None, EPS1, x1sb[:], None, xl2b,
                             (negm, sq3, vs, v1c, rstd), tg)
                    bk.dma("sy", dd["xln2_h"][ta:ta + 128, :], xl2b[:],
                           reads=[xl2b], writes=[(dd["xln2_h"], ta // 128)],
                           tag=tg, grp="b_out")
            bk.barrier()
        _end_stage(cx, "stB")


def stage_C(cx):
    bk, nc, dd, pers = cx.bk, cx.nc, cx.dram, cx.pers
    NB = T // 512
    NC1, NC2 = MLP // 128, 8
    with ExitStack() as st:
        sb = lambda n, sh, d: st.enter_context(nc.sbuf_tensor(n, sh, d))
        ps_ = lambda n, sh: st.enter_context(nc.psum_tensor(n, sh, F32))
        w1_sb = [sb(f"c_w1{c}", [128, MLP], BF16) for c in range(8)]
        w2_sb = [sb(f"c_w2{c}", [128, D], BF16) for c in range(32)]
        hT = [sb(f"c_hT{c}", [128, 512], BF16) for c in range(32)]
        xl2T = [sb(f"c_xl2T{c}", [128, 512], BF16) for c in range(8)]
        tsb = [sb(f"c_t{i}", [128, 512], F32) for i in range(2)]
        usb = [sb(f"c_u{i}", [128, 512], F32) for i in range(2)]
        csb = [sb(f"c_c{i}", [128, 512], F32) for i in range(2)]
        x1t = sb("c_x1t", [128, D], F32)
        g0t = sb("c_g0t", [128, D], F32)
        xint = sb("c_xint", [128, D], F32)
        pg = [ps_(f"c_pg{i}", [128, 512]) for i in range(2)]
        py = [ps_(f"c_py{i}", [128, 512]) for i in range(2)]
        for c in range(8):
            bk.dma("sy", w1_sb[c][:], dd["w1_h"][c * 128:(c + 1) * 128, :],
                   reads=[dd["w1_h"]], tag="Cw1", grp="c_w")
        for c in range(32):
            bk.dma("sy", w2_sb[c][:], dd["w2_h"][c * 128:(c + 1) * 128, :],
                   reads=[dd["w2_h"]], tag="Cw2", grp="c_w")
        for blk in range(NB):
            t0 = blk * 512
            for c in range(8):
                bk.dma("sy", xl2T[c][:],
                       dd["xln2_h"][t0:t0 + 512, c * 128:(c + 1) * 128],
                       transpose=True,
                       reads=[(dd["xln2_h"], t0 // 128 + j) for j in range(4)],
                       writes=[xl2T[c]], tag=f"C{blk}x", grp="c_x")
                bk.op("ve", "tensor_scalar", xl2T[c][:], xl2T[c][:],
                      pers["ln2_s"][:, c:c + 1], pers["ln2_b"][:, c:c + 1],
                      ALU.mult, ALU.add,
                      reads=[xl2T[c], pers["ln2_s"], pers["ln2_b"]],
                      writes=[xl2T[c]], tag=f"C{blk}x")
            for dc in range(NC1):
                tg = f"C{blk}h{dc}"
                pp = pg[dc % 2]
                for c in range(8):
                    bk.op("pe", "matmul", pp[:],
                          w1_sb[c][:, dc * 128:(dc + 1) * 128], xl2T[c][:],
                          start=(c == 0), stop=(c == 7),
                          reads=[w1_sb[c], xl2T[c]], writes=[pp],
                          accum=(c > 0), tag=tg)
                i = dc % 2
                b1c = pers["b1"][:, dc:dc + 1]
                bk.op("ve", "tensor_scalar", tsb[i][:], pp[:], b1c, None,
                      ALU.add, reads=[pp, pers["b1"]], writes=[tsb[i]], tag=tg)
                bk.op("se", "activation", usb[i][:], tsb[i][:], AF.Square,
                      reads=[tsb[i]], writes=[usb[i]], tag=tg)
                bk.op("ve", "tensor_scalar", usb[i][:], usb[i][:], GELU_A, 1.0,
                      ALU.mult, ALU.add, reads=[usb[i]], writes=[usb[i]], tag=tg)
                bk.op("ve", "tensor_tensor", csb[i][:], tsb[i][:], usb[i][:],
                      ALU.mult, reads=[tsb[i], usb[i]], writes=[csb[i]], tag=tg)
                bk.op("se", "activation", usb[i][:], csb[i][:], AF.Tanh,
                      scale=GELU_C, reads=[csb[i]], writes=[usb[i]], tag=tg)
                bk.op("ve", "scalar_tensor_tensor", hT[dc][:], usb[i][:], 1.0,
                      tsb[i][:], ALU.add, ALU.mult,
                      reads=[usb[i], tsb[i]], writes=[hT[dc]], tag=tg)
            for j in range(4):
                tt = t0 + j * 128
                ti = tt // 128
                tg = f"C{blk}y{j}"
                bk.dma("sy", x1t[:], dd["x1"][tt:tt + 128, :],
                       reads=[(dd["x1"], ti)], writes=[x1t], tag=tg, grp="c_in")
                bk.dma("sy", g0t[:], dd["g0"][tt:tt + 128, :],
                       reads=[(dd["g0"], ti)], writes=[g0t], tag=tg, grp="c_in")
                bk.dma("sy", xint[:], dd["x_in"][tt:tt + 128, :],
                       writes=[xint], tag=tg, grp="c_in")
                for b2 in range(2):
                    pp = py[b2]
                    ds_ = slice(b2 * 512, (b2 + 1) * 512)
                    if TEST_NO_B2:
                        for c in range(32):
                            bk.op("pe", "matmul", pp[:],
                                  hT[c][:, j * 128:(j + 1) * 128],
                                  w2_sb[c][:, ds_], start=(c == 0),
                                  stop=(c == 31),
                                  reads=[hT[c], w2_sb[c]], writes=[pp],
                                  accum=(c > 0), tag=tg)
                    else:
                        bk.op("pe", "matmul", pp[:], pers["onesF"][:],
                              pers["b2row2"][0:1, ds_], start=True, stop=False,
                              reads=[pers["onesF"], pers["b2row2"]],
                              writes=[pp], tag=tg)
                        for c in range(32):
                            bk.op("pe", "matmul", pp[:],
                                  hT[c][:, j * 128:(j + 1) * 128],
                                  w2_sb[c][:, ds_], start=False, stop=(c == 31),
                                  reads=[hT[c], w2_sb[c]], writes=[pp],
                                  accum=True, tag=tg)
                    bk.op("ve", "scalar_tensor_tensor", x1t[:, ds_], pp[:], 0.5,
                          x1t[:, ds_], ALU.mult, ALU.add,
                          reads=[pp, x1t], writes=[x1t], tag=tg)
                    bk.op("ve", "tensor_mul", g0t[:, ds_], x1t[:, ds_],
                          g0t[:, ds_], reads=[x1t, g0t], writes=[g0t], tag=tg)
                    bk.op("ve", "tensor_add", xint[:, ds_], g0t[:, ds_],
                          xint[:, ds_], reads=[g0t, xint], writes=[(xint, b2)],
                          tag=tg)
                bk.dma("sy", dd["x_res"][tt:tt + 128, :], xint[:],
                       reads=[(xint, 0), (xint, 1)],
                       writes=[(dd["x_res"], ti)], tag=tg, grp="c_out")
        _end_stage(cx, "stC")




def emit_fp4(bk, pers, x_ap, out_ap, bufs, N, tag):
    absx, p1, p2, c7, bm, sc = bufs
    NB = N // 16
    a3 = lambda t: t[:, 0:N].rearrange("p (b i) -> p b i", i=16)
    scb = bass.AP(sc, 0, [[sc.shape[1], 128], [1, NB], [0, 16]])
    bk.op("se", "activation", absx[:, 0:N], x_ap, AF.Abs,
          reads=[x_ap], writes=[absx], tag=tag)
    bk.op("ve", "reduce_max", bm[:, 0:NB].rearrange("p (b o) -> p b o", o=1),
          a3(absx), axis=AX.X, reads=[absx], writes=[bm], tag=tag)
    bk.op("ve", "tensor_scalar", sc[:, 0:NB], bm[:, 0:NB], 1.0 / 6.0, 1e-12,
          ALU.mult, ALU.max, reads=[bm], writes=[sc], tag=tag)
    bk.op("ve", "reciprocal", bm[:, 0:NB], sc[:, 0:NB],
          reads=[sc], writes=[bm], tag=tag)
    rsb = bass.AP(bm, 0, [[bm.shape[1], 128], [1, NB], [0, 16]])
    bk.op("ve", "tensor_tensor", a3(absx), a3(absx), rsb, ALU.mult,
          reads=[absx, bm], writes=[absx], tag=tag)
    bk.op("ve", "tensor_scalar", p1[:, 0:N], absx[:, 0:N], 0.25, None,
          ALU.is_gt, reads=[absx], writes=[p1], tag=tag)
    for t in (0.75, 1.25, 1.75):
        bk.op("ve", "scalar_tensor_tensor", p1[:, 0:N], absx[:, 0:N], t,
              p1[:, 0:N], ALU.is_gt, ALU.add, reads=[absx, p1], writes=[p1],
              tag=tag)
    bk.op("ve", "tensor_scalar", p2[:, 0:N], absx[:, 0:N], 2.5, None,
          ALU.is_gt, reads=[absx], writes=[p2], tag=tag)
    bk.op("ve", "scalar_tensor_tensor", p2[:, 0:N], absx[:, 0:N], 3.5,
          p2[:, 0:N], ALU.is_gt, ALU.add, reads=[absx, p2], writes=[p2],
          tag=tag)
    bk.op("ve", "tensor_scalar", c7[:, 0:N], absx[:, 0:N], 5.0, 2.0,
          ALU.is_gt, ALU.mult, reads=[absx], writes=[c7], tag=tag)
    bk.op("ve", "tensor_add", p2[:, 0:N], p2[:, 0:N], c7[:, 0:N],
          reads=[p2, c7], writes=[p2], tag=tag)
    bk.op("ve", "scalar_tensor_tensor", p1[:, 0:N], p1[:, 0:N], 0.5,
          p2[:, 0:N], ALU.mult, ALU.add, reads=[p1, p2], writes=[p1], tag=tag)
    bk.op("se", "sign", c7[:, 0:N], x_ap, reads=[x_ap], writes=[c7], tag=tag)
    bk.op("ve", "tensor_mul", p1[:, 0:N], p1[:, 0:N], c7[:, 0:N],
          reads=[p1, c7], writes=[p1], tag=tag)
    bk.op("ve", "tensor_tensor",
          out_ap.rearrange("p (b i) -> p b i", i=16), a3(p1),
          bass.AP(sc, 0, [[sc.shape[1], 128], [1, NB], [0, 16]]), ALU.mult,
          reads=[p1, sc], writes=[out_ap], tag=tag)


def stage_E(cx):
    bk, nc, dd, pers = cx.bk, cx.nc, cx.dram, cx.pers
    with ExitStack() as st:
        sb = lambda n, sh, d: st.enter_context(nc.sbuf_tensor(n, sh, d))
        NBUF = 2
        xt = [sb(f"e_x{i}", [128, D], F32) for i in range(NBUF)]
        xn = [sb(f"e_xn{i}", [128, D], F32) for i in range(NBUF)]
        xnb = [sb(f"e_xnb{i}", [128, D], BF16) for i in range(NBUF)]
        xqb = [sb(f"e_xqb{i}", [128, D], BF16) for i in range(NBUF)]
        sq = sb("e_sq", [128, D], F32)
        cols = [sb(f"e_c{j}", [128, 1], F32) for j in range(4)]
        qb = (sb("e_qa", [128, D], F32), sb("e_qp1", [128, D], F32),
              sb("e_qp2", [128, D], F32), sb("e_qc7", [128, D], F32),
              sb("e_qbm", [128, D // 16], F32), sb("e_qsc", [128, D // 16], F32))
        for t in range(NT):
            i = t % NBUF
            tg = f"E{t}"
            bk.dma("sy", xt[i][:], dd["x_res"][t * 128:(t + 1) * 128, :],
                   reads=[(dd["x_res"], t)], writes=[xt[i]], tag=tg,
                   grp="e_in")
            negm, vs, v1, rstd = cols
            _ln_tile(bk, None, EPS0, xt[i][:], xn[i], xnb[i],
                     (negm, sq, vs, v1, rstd), tg)
            bk.dma("sy", dd["xn1_h"][t * 128:(t + 1) * 128, :], xnb[i][:],
                   reads=[xnb[i]], writes=[(dd["xn1_h"], t)], tag=tg,
                   grp="e_out")
            emit_fp4(bk, pers, xn[i][:], xqb[i][:], qb, D, tg)
            bk.dma("sy", dd["xq_h"][t * 128:(t + 1) * 128, :], xqb[i][:],
                   reads=[xqb[i]], writes=[(dd["xq_h"], t)], tag=tg,
                   grp="e_out")
        _end_stage(cx, "stE")


def stage_G(cx):
    bk, nc, dd, pers = cx.bk, cx.nc, cx.dram, cx.pers
    TH = T // 2
    NK = TH // 128
    NB6 = (I1 + 511) // 512
    with ExitStack() as st:
        sb = lambda n, sh, d: st.enter_context(nc.sbuf_tensor(n, sh, d))
        ps_ = lambda n, sh: st.enter_context(nc.psum_tensor(n, sh, F32))
        with ExitStack() as stq:
            wt = [stq.enter_context(nc.sbuf_tensor(f"g_wt{i}", [128, I1], F32))
                  for i in range(2)]
            wq = [stq.enter_context(nc.sbuf_tensor(f"g_wq{i}", [128, I1], BF16))
                  for i in range(2)]
            qb = (stq.enter_context(nc.sbuf_tensor("g_qa", [128, I1], F32)),
                  stq.enter_context(nc.sbuf_tensor("g_qp1", [128, I1], F32)),
                  stq.enter_context(nc.sbuf_tensor("g_qp2", [128, I1], F32)),
                  stq.enter_context(nc.sbuf_tensor("g_qc7", [128, I1], F32)),
                  stq.enter_context(nc.sbuf_tensor("g_qbm", [128, I1 // 16], F32)),
                  stq.enter_context(nc.sbuf_tensor("g_qsc", [128, I1 // 16], F32)))
            jobs = []
            for w_, wq_, rows, cols_ in (("wg", "wgq_h", I1, D),
                                         ("wu", "wuq_h", I1, D),
                                         ("wd", "wdq_h", D, I1)):
                for r0 in range(0, rows, 128):
                    jobs.append((w_, wq_, r0, cols_))
            for ji, (w_, wq_, r0, cols_) in enumerate(jobs):
                i = ji % 2
                tg = f"Gq{w_}{r0}"
                bk.dma("sy", wt[i][:, 0:cols_], dd[w_][r0:r0 + 128, :],
                       reads=[(dd[w_], r0)], writes=[wt[i]], tag=tg,
                       grp="g_wq")
                emit_fp4(bk, pers, wt[i][:, 0:cols_], wq[i][:, 0:cols_],
                         qb, cols_, tg)
                bk.dma("sy", dd[wq_][r0:r0 + 128, :], wq[i][:, 0:cols_],
                       reads=[wq[i]], writes=[(dd[wq_], r0)], tag=tg,
                       grp="g_wqs")
            bk.barrier()
        wgT = [sb(f"g_wgT{c}", [128, I1], BF16) for c in range(8)]
        wuT = [sb(f"g_wuT{c}", [128, I1], BF16) for c in range(8)]
        wdT = [sb(f"g_wdT{c}", [128, D], BF16) for c in range(22)]
        for c in range(8):
            bk.dma("sy", wgT[c][:], dd["wgq_h"][:, c * 128:(c + 1) * 128],
                   transpose=True,
                   reads=[(dd["wgq_h"], r0) for r0 in range(0, I1, 128)],
                   writes=[wgT[c]], tag="GwT", grp="g_w")
            bk.dma("sy", wuT[c][:], dd["wuq_h"][:, c * 128:(c + 1) * 128],
                   transpose=True,
                   reads=[(dd["wuq_h"], r0) for r0 in range(0, I1, 128)],
                   writes=[wuT[c]], tag="GwT", grp="g_w")
        for c in range(22):
            bk.dma("sy", wdT[c][:], dd["wdq_h"][:, c * 128:(c + 1) * 128],
                   transpose=True,
                   reads=[(dd["wdq_h"], r0) for r0 in range(0, D, 128)],
                   writes=[wdT[c]], tag="GwT", grp="g_w")
        xqT = [sb(f"g_xqT{c}", [128, 512], BF16) for c in range(8)]
        hT = [[sb(f"g_hT{pp_}_{c}", [128, 512], BF16) for c in range(22)]
              for pp_ in range(2)]
        sg_1 = sb("g_sg", [128, 512], F32)
        sg = [sg_1, sg_1]
        hf_1 = sb("g_hf", [128, 512], F32)
        hf = [hf_1, hf_1]
        hq = [sb(f"g_hq{i}", [128, 512], BF16) for i in range(2)]
        g1t = sb("g_g1t", [128, D], F32)
        qb2_1 = (sb("g_q2a", [128, 512], F32), sb("g_q2p1", [128, 512], F32),
                 sg_1, sb("g_q2c7", [128, 512], F32),
                 sb("g_q2bm", [128, 32], F32), sb("g_q2sc", [128, 32], F32))
        qb2 = [qb2_1, qb2_1]
        pg = [ps_(f"g_pg{i}", [128, 512]) for i in range(2)]
        pu = [ps_(f"g_pu{i}", [128, 512]) for i in range(2)]
        py = [ps_(f"g_py{i}", [128, 512]) for i in range(2)]
        def g_mm1(hf_, q):
            t0 = hf_ * TH
            for kq in range(4):
                k = q * 4 + kq
                kk = slice(kq * 128, (kq + 1) * 128)
                tg = f"G{hf_}k{k}"
                for blkb in range(NB6):
                    bw = min(512, I1 - blkb * 512)
                    bs = slice(blkb * 512, blkb * 512 + bw)
                    for c in range(8):
                        bk.op("pe", "matmul", pg[blkb % 2][:, 0:bw], xqT[c][:, kk],
                              wgT[c][:, bs], start=(c == 0), stop=(c == 7),
                              reads=[xqT[c], wgT[c]], writes=[pg[blkb % 2]],
                              accum=(c > 0), tag=tg)
                    for c in range(8):
                        bk.op("pe", "matmul", pu[blkb % 2][:, 0:bw], xqT[c][:, kk],
                              wuT[c][:, bs], start=(c == 0), stop=(c == 7),
                              reads=[xqT[c], wuT[c]], writes=[pu[blkb % 2]],
                              accum=(c > 0), tag=tg)
                    bk.op("se", "activation", sg[blkb % 2][:, 0:bw],
                          pg[blkb % 2][:, 0:bw],
                          AF.Silu, reads=[pg[blkb % 2]], writes=[sg[blkb % 2]],
                          tag=tg)
                    bk.op("ve", "tensor_mul", hf[blkb % 2][:, 0:bw],
                          sg[blkb % 2][:, 0:bw],
                          pu[blkb % 2][:, 0:bw], reads=[sg[blkb % 2], pu[blkb % 2]],
                          writes=[hf[blkb % 2]], tag=tg)
                    emit_fp4(bk, pers, hf[blkb % 2][:, 0:bw], hq[blkb % 2][:, 0:bw],
                             qb2[blkb % 2], bw, tg)
                    bk.dma("sy", dd["hq_h"][t0 + k * 128:t0 + (k + 1) * 128,
                                            blkb * 512:blkb * 512 + bw],
                           hq[blkb % 2][:, 0:bw], reads=[hq[blkb % 2]],
                           writes=[(dd["hq_h"], (hf_, k, blkb))], tag=tg,
                           grp="g_hs")

        def g_hload(hf_, q):
            t0 = hf_ * TH
            q0 = t0 + q * 512
            for c in range(22):
                bk.dma("sy", hT[q % 2][c][:],
                       dd["hq_h"][q0:q0 + 512, c * 128:(c + 1) * 128],
                       transpose=True,
                       reads=[(dd["hq_h"], (hf_, q * 4 + j, bb))
                              for j in range(4) for bb in range(NB6)],
                       writes=[hT[q % 2][c]], tag=f"G{hf_}q{q}", grp="g_ht")

        def g_mm2(hf_, q):
            t0 = hf_ * TH
            for kq in range(4):
                k = q * 4 + kq
                ti = t0 // 128 + k
                tg = f"G{hf_}y{k}"
                for b2 in range(2):
                    pp = py[b2]
                    for c in range(22):
                        bk.op("pe", "matmul", pp[:],
                              hT[q % 2][c][:, kq * 128:(kq + 1) * 128],
                              wdT[c][:, b2 * 512:(b2 + 1) * 512],
                              start=(c == 0), stop=(c == 21),
                              reads=[hT[q % 2][c], wdT[c]], writes=[pp],
                              accum=(c > 0), tag=tg)
                    bk.op("se", "activation", g1t[:, b2 * 512:(b2 + 1) * 512],
                          pp[:], AF.Silu, reads=[pp], writes=[(g1t, b2)],
                          tag=tg)
                bk.dma("sy", dd["g1"][ti * 128:(ti + 1) * 128, :], g1t[:],
                       reads=[(g1t, 0), (g1t, 1)],
                       writes=[(dd["g1"], ti)], tag=tg, grp="g_gs")

        pendg = None
        for hf_ in range(2):
            t0 = hf_ * TH
            for q in range(NK // 4):
                qq0 = t0 + q * 512
                for c in range(8):
                    bk.dma("sy", xqT[c][:],
                           dd["xq_h"][qq0:qq0 + 512, c * 128:(c + 1) * 128],
                           transpose=True,
                           reads=[(dd["xq_h"], qq0 // 128 + j) for j in range(4)],
                           writes=[xqT[c]], tag=f"G{hf_}x{q}", grp="g_x")
                g_mm1(hf_, q)
                g_hload(hf_, q)
                if pendg is not None:
                    g_mm2(*pendg)
                pendg = (hf_, q)
        if pendg is not None:
            g_mm2(*pendg)
        _end_stage(cx, "stG")


def stage_F(cx):
    bk, nc, dd, pers = cx.bk, cx.nc, cx.dram, cx.pers
    TH = T // 2
    NK = TH // 128
    R2PI = 1.0 / (2.0 * math.pi)
    RND_C = 12582912.0
    with ExitStack() as st:
        sb = lambda n, sh, d: st.enter_context(nc.sbuf_tensor(n, sh, d))
        ps_ = lambda n, sh: st.enter_context(nc.psum_tensor(n, sh, F32))
        cosT = sb("f_cos", [128, NK * 32], F32)
        sinT = sb("f_sin", [128, NK * 32], F32)
        with ExitStack() as s0:
            ivrow = s0.enter_context(nc.sbuf_tensor("f_ivr", [1, 32], F32))
            ivrep = s0.enter_context(nc.sbuf_tensor("f_ivrep", [128, 32], F32))
            ivb = s0.enter_context(nc.sbuf_tensor("f_ivb", [1, 32], BF16))
            poscol = s0.enter_context(nc.sbuf_tensor("f_pos", [128, 1], F32))
            ang = s0.enter_context(nc.sbuf_tensor("f_ang", [128, 32], F32))
            tmpa = s0.enter_context(nc.sbuf_tensor("f_tmpa", [128, 32], F32))
            hpi = s0.enter_context(nc.sbuf_tensor("f_hpi", [128, 1], F32))
            psr = s0.enter_context(nc.psum_tensor("f_psr", [128, 32], F32))
            bk.op("po", "iota", ivrow[:], [[1, 32]], base=0,
                  channel_multiplier=0, allow_small_or_imprecise_dtypes=True,
                  writes=[ivrow], tag="Frt")
            bk.op("po", "memset", hpi[:], float(math.pi / 2.0), writes=[hpi],
                  tag="Frt")
            bk.op("se", "activation", ivrow[:], ivrow[:], AF.Exp,
                  scale=float(-math.log(10000.0) / 32.0),
                  reads=[ivrow], writes=[ivrow], tag="Frt")
            bk.op("ve", "tensor_copy", ivb[:], ivrow[:], reads=[ivrow],
                  writes=[ivb], tag="Frt")
            bk.op("pe", "matmul", psr[:], pers["onesF"][:].rearrange(
                  "a b -> a b"), ivrow[:], start=True, stop=True,
                  reads=[pers["onesF"], ivrow], writes=[psr], tag="Frt")
            bk.op("ve", "tensor_copy", ivrep[:], psr[:], reads=[psr],
                  writes=[ivrep], tag="Frt")
            for t in range(NK):
                tg = f"Frt{t}"
                cs = slice(t * 32, (t + 1) * 32)
                bk.op("po", "iota", poscol[:], [[0, 1]], base=t * 128,
                      channel_multiplier=1,
                      allow_small_or_imprecise_dtypes=True,
                      writes=[poscol], tag=tg)
                bk.op("ve", "tensor_single_scalar", ang[:], ivrep[:],
                      poscol[:, 0:1], ALU.mult, reads=[ivrep, poscol],
                      writes=[ang], tag=tg)
                bk.op("ve", "tensor_scalar", tmpa[:], ang[:], R2PI, RND_C,
                      ALU.mult, ALU.add, reads=[ang], writes=[tmpa], tag=tg)
                bk.op("ve", "tensor_scalar", tmpa[:], tmpa[:], -RND_C, None,
                      ALU.add, reads=[tmpa], writes=[tmpa], tag=tg)
                bk.op("ve", "scalar_tensor_tensor", ang[:], tmpa[:],
                      float(-2.0 * math.pi), ang[:], ALU.mult, ALU.add,
                      reads=[tmpa, ang], writes=[ang], tag=tg)
                bk.op("ve", "tensor_scalar", ang[:], ang[:],
                      float(math.pi), float(-math.pi), ALU.min, ALU.max,
                      reads=[ang], writes=[ang], tag=tg)
                bk.op("se", "activation", sinT[:, cs], ang[:], AF.Sin,
                      reads=[ang], writes=[(sinT, t)], tag=tg)
                bk.op("se", "activation", tmpa[:], ang[:], AF.Abs,
                      reads=[ang], writes=[tmpa], tag=tg)
                bk.op("se", "activation", cosT[:, cs], tmpa[:], AF.Sin,
                      scale=-1.0, bias=hpi[:, 0:1],
                      reads=[tmpa, hpi], writes=[(cosT, t)], tag=tg)
        bk.barrier()
        negC = sb("f_negC", [128, 1], F32)
        bk.op("po", "memset", negC[:], -20.0, writes=[negC], tag="FnegC")
        wqkvT = [sb(f"f_wqkvT{c}", [128, 3 * D], BF16) for c in range(8)]
        for c in range(8):
            bk.dma("sy", wqkvT[c][:], dd["wqkv_h"][:, c * 128:(c + 1) * 128],
                   transpose=True, reads=[dd["wqkv_h"]], writes=[wqkvT[c]],
                   tag="FwT", grp="f_w")
        for hf in range(2):
            t0 = hf * TH
            with ExitStack() as s1:
                sb1 = lambda n, sh, d: s1.enter_context(
                    nc.sbuf_tensor(f"{n}_{hf}", sh, d))
                xnT = [sb1(f"f_xnT{c}", [128, TH], BF16) for c in range(8)]
                v_sb = sb1("f_v", [128, H * 65], BF16)
                ro = sb1("f_ro", [128, D], F32)
                tmp = sb1("f_tmp", [128, 512], F32)
                tmp2 = sb1("f_tmp2", [128, D], F32)
                ms = sb1("f_ms", [128, H], F32)
                qro = sb1("f_qro", [128, D], BF16)
                pq = [s1.enter_context(nc.psum_tensor(f"f_pq{i}_{hf}",
                      [128, 512], F32)) for i in range(6)]
                for c in range(8):
                    bk.dma("sy", xnT[c][:],
                           dd["xn1_h"][t0:t0 + TH, c * 128:(c + 1) * 128],
                           transpose=True,
                           reads=[(dd["xn1_h"], t0 // 128 + j)
                                  for j in range(NK)],
                           writes=[xnT[c]], tag=f"F1x{hf}", grp="f_x")
                for k in range(NK):
                    kk = slice(k * 128, (k + 1) * 128)
                    tg = f"F1{hf}k{k}"
                    for part in range(6):
                        pp = pq[part]
                        for c in range(8):
                            bk.op("pe", "matmul", pp[:], xnT[c][:, kk],
                                  wqkvT[c][:, part * 512:(part + 1) * 512],
                                  start=(c == 0), stop=(c == 7),
                                  reads=[xnT[c], wqkvT[c]], writes=[pp],
                                  accum=(c > 0), tag=tg)
                    if k == 0:
                        bk.op("ve", "memset",
                              bass.AP(v_sb, 64, [[H * 65, 128], [65, H], [1, 1]]),
                              1.0, writes=[(v_sb, "ones")], tag=tg)
                    for b2 in range(2):
                        bk.op("ve", "tensor_copy",
                              bass.AP(v_sb, b2 * 8 * 65,
                                      [[H * 65, 128], [65, 8], [1, 64]]),
                              pq[4 + b2][:].rearrange("p (h d) -> p h d", d=64),
                              reads=[pq[4 + b2], (v_sb, "ones")],
                              writes=[(v_sb, b2)], tag=tg)
                    bk.dma("sy", dd["vro_h"][(t0 // 128 + k) * 128:
                                             (t0 // 128 + k + 1) * 128, :],
                           v_sb[:], reads=[(v_sb, 0), (v_sb, 1), (v_sb, "ones")],
                           writes=[(dd["vro_h"], t0 // 128 + k)], tag=tg,
                           grp="f_qk")
                    for (pa, pb, wrep, dst) in ((0, 1, "qnw", "qro_h"),
                                                (2, 3, "knw", "kro_h")):
                        cs = slice(k * 32, (k + 1) * 32)
                        c3 = bass.AP(cosT, k * 32,
                                     [[cosT.shape[1], 128], [0, 8], [1, 32]])
                        s3 = bass.AP(sinT, k * 32,
                                     [[sinT.shape[1], 128], [0, 8], [1, 32]])
                        for half_i, pp in ((0, pq[pa]), (1, pq[pb])):
                            p3a = pp[:].rearrange("p (h d) -> p h d", d=64)
                            q1 = pp[:].rearrange(
                                "p (h e d) -> p (h e) d", e=2, d=32
                            )
                            ro3 = ro[:, half_i * 512:(half_i + 1) * 512]
                            rr = lambda t_, off: bass.AP(
                                t_.tensor, t_.offset + off,
                                [[t_.ap[0][0], 128], [64, 8], [1, 32]])
                            q1v = bass.AP(pp, 0, [[512, 128], [64, 8], [1, 32]])
                            q2v = bass.AP(pp, 32, [[512, 128], [64, 8], [1, 32]])
                            o1 = bass.AP(ro, half_i * 512,
                                         [[ro.shape[1], 128], [64, 8], [1, 32]])
                            o2 = bass.AP(ro, half_i * 512 + 32,
                                         [[ro.shape[1], 128], [64, 8], [1, 32]])
                            t1 = bass.AP(tmp, 0, [[512, 128], [64, 8], [1, 32]])
                            tg2 = tg + f"r{pa}{half_i}"
                            bk.op("ve", "tensor_tensor", o1, q1v, c3, ALU.mult,
                                  reads=[pp, cosT], writes=[ro], tag=tg2)
                            bk.op("ve", "tensor_tensor", t1, q2v, s3, ALU.mult,
                                  reads=[pp, sinT], writes=[tmp], tag=tg2)
                            bk.op("ve", "tensor_sub", o1, o1, t1,
                                  reads=[ro, tmp], writes=[ro], tag=tg2)
                            bk.op("ve", "tensor_tensor", o2, q2v, c3, ALU.mult,
                                  reads=[pp, cosT], writes=[ro], tag=tg2)
                            bk.op("ve", "tensor_tensor", t1, q1v, s3, ALU.mult,
                                  reads=[pp, sinT], writes=[tmp], tag=tg2)
                            bk.op("ve", "tensor_add", o2, o2, t1,
                                  reads=[ro, tmp], writes=[ro], tag=tg2)
                        tg3 = tg + f"n{pa}"
                        bk.op("ve", "tensor_mul", tmp2[:], ro[:], ro[:],
                              reads=[ro], writes=[tmp2], tag=tg3)
                        bk.op("ve", "reduce_sum",
                              ms[:].rearrange("p (h o) -> p h o", o=1),
                              tmp2[:].rearrange("p (h d) -> p h d", d=64),
                              axis=AX.X, reads=[tmp2], writes=[ms], tag=tg3)
                        bk.op("ve", "tensor_scalar", ms[:], ms[:], 1.0 / 64,
                              RMS_EPS, ALU.mult, ALU.add, reads=[ms],
                              writes=[ms], tag=tg3)
                        bk.op("se", "activation", ms[:], ms[:], AF.Ln,
                              reads=[ms], writes=[ms], tag=tg3)
                        bk.op("se", "activation", ms[:], ms[:], AF.Exp,
                              scale=-0.5, reads=[ms], writes=[ms], tag=tg3)
                        msb = bass.AP(ms, 0, [[ms.shape[1], 128], [1, H], [0, 64]])
                        bk.op("ve", "tensor_tensor",
                              tmp2[:].rearrange("p (h d) -> p h d", d=64),
                              ro[:].rearrange("p (h d) -> p h d", d=64),
                              msb, ALU.mult, reads=[ro, ms], writes=[tmp2],
                              tag=tg3)
                        wr = pers[wrep]
                        wrb = bass.AP(wr, 0, [[wr.shape[1], 128], [0, H], [1, 64]])
                        bk.op("ve", "tensor_tensor",
                              qro[:].rearrange("p (h d) -> p h d", d=64),
                              tmp2[:].rearrange("p (h d) -> p h d", d=64),
                              wrb, ALU.mult, reads=[tmp2, wr], writes=[qro],
                              tag=tg3)
                        ti = t0 // 128 + k
                        bk.dma("sy", dd[dst][ti * 128:(ti + 1) * 128, :],
                               qro[:], reads=[qro], writes=[(dd[dst], ti)],
                               tag=tg3, grp="f_qk")
                bk.barrier()
            with ExitStack() as s2:
                sb2 = lambda n, sh, d: s2.enter_context(
                    nc.sbuf_tensor(f"{n}_{hf}", sh, d))
                v_sb2 = sb2("f2_v", [128, NK * H * 65], BF16)
                qT = sb2("f2_qT", [128, TH], BF16)
                kT = sb2("f2_kT", [128, TH], BF16)
                P_sb = [sb2(f"f2_P{i}", [128, 512], BF16) for i in range(2)]
                rl = sb2("f2_rl", [128, 1], F32)
                ao = sb2("f2_ao", [128, NK * D], BF16)
                psS = [s2.enter_context(nc.psum_tensor(f"f2_psS{i}_{hf}",
                       [128, 512], F32)) for i in range(2)]
                psO = [s2.enter_context(nc.psum_tensor(f"f2_psO{i}_{hf}",
                       [128, 65], F32)) for i in range(4)]
                VW = H * 65
                for k in range(NK):
                    bk.dma("sy", v_sb2[:, k * VW:(k + 1) * VW],
                           dd["vro_h"][(t0 // 128 + k) * 128:
                                       (t0 // 128 + k + 1) * 128, :],
                           reads=[(dd["vro_h"], t0 // 128 + k)],
                           writes=[(v_sb2, k)], tag=f"F2v{hf}", grp="f_qt")
                for hp in range(8):
                    tgh = f"F2{hf}p{hp}"
                    bk.dma("sy", qT[:],
                           dd["qro_h"][t0:t0 + TH, hp * 128:(hp + 1) * 128],
                           transpose=True,
                           reads=[(dd["qro_h"], t0 // 128 + j)
                                  for j in range(NK)],
                           writes=[qT], tag=tgh, grp="f_qt")
                    bk.dma("sy", kT[:],
                           dd["kro_h"][t0:t0 + TH, hp * 128:(hp + 1) * 128],
                           transpose=True,
                           reads=[(dd["kro_h"], t0 // 128 + j)
                                  for j in range(NK)],
                           writes=[kT], tag=tgh, grp="f_qt")
                    for hsub in range(2):
                        h = hp * 2 + hsub
                        o = hsub * 64
                        os_ = slice(o, o + 64)
                        for Q in range(4):
                            qs = slice(Q * 512, (Q + 1) * 512)
                            NJ = 4 * Q + 4

                            def emit_S(j, tg):
                                js = slice(j * 128, (j + 1) * 128)
                                bk.op("pe", "matmul", psS[j % 2][:], kT[os_, js],
                                      qT[os_, qs], start=True, stop=True,
                                      reads=[qT, kT], writes=[psS[j % 2]], tag=tg)
                                bk.op("se", "activation", P_sb[j % 2][:],
                                      psS[j % 2][:], AF.Exp, scale=0.125,
                                      bias=negC[:, 0:1],
                                      reads=[psS[j % 2], negC],
                                      writes=[P_sb[j % 2]], tag=tg)
                                if j >= 4 * Q:
                                    qt_d = j - 4 * Q
                                    bk.op("ve", "tensor_tensor",
                                          P_sb[j % 2][:, qt_d * 128:(qt_d + 1) * 128],
                                          P_sb[j % 2][:, qt_d * 128:(qt_d + 1) * 128],
                                          pers["mask_incl"][:], ALU.mult,
                                          reads=[P_sb[j % 2], pers["mask_incl"]],
                                          writes=[P_sb[j % 2]], tag=tg)

                            def emit_O(j, tg):
                                for qt_ in range(4):
                                    gq = 4 * Q + qt_
                                    if gq < j:
                                        continue
                                    pO = psO[qt_]
                                    first = (j == 0)
                                    bk.op("pe", "matmul", pO[:, 0:65],
                                          P_sb[j % 2][:, qt_ * 128:(qt_ + 1) * 128],
                                          v_sb2[:, j * VW + h * 65:
                                                j * VW + h * 65 + 65],
                                          start=first, stop=(j == gq),
                                          reads=[P_sb[j % 2], (v_sb2, j)],
                                          writes=[pO], accum=not first, tag=tg)

                            for j in range(NJ):
                                tg = f"{tgh}h{hsub}Q{Q}j{j}"
                                emit_S(j, tg)
                                if j > 0:
                                    emit_O(j - 1, tg)
                            emit_O(NJ - 1, f"{tgh}h{hsub}Q{Q}jz")
                            for qt_ in range(4):
                                tg = f"{tgh}h{hsub}Q{Q}e{qt_}"
                                pO = psO[qt_]
                                bk.op("ve", "reciprocal", rl[:],
                                      pO[:, 64:65], reads=[pO], writes=[rl],
                                      tag=tg)
                                gq = 4 * Q + qt_
                                bk.op("ve", "tensor_single_scalar",
                                      ao[:, gq * D + h * 64:
                                         gq * D + h * 64 + 64],
                                      pO[:, 0:64], rl[:, 0:1], ALU.mult,
                                      reads=[pO, rl],
                                      writes=[(ao, (gq, h))], tag=tg)
                for k in range(NK):
                    ti = t0 // 128 + k
                    bk.dma("sy", dd["ao_h"][ti * 128:(ti + 1) * 128, :],
                           ao[:, k * D:(k + 1) * D],
                           reads=[(ao, (k, hh)) for hh in range(H)],
                           writes=[(dd["ao_h"], ti)], tag=f"F2o{hf}",
                           grp="f_ao")
                bk.barrier()
        with ExitStack() as s3:
            sb3 = lambda n, sh, d: s3.enter_context(nc.sbuf_tensor(n, sh, d))
            woT = [sb3(f"f3_woT{c}", [128, D], BF16) for c in range(8)]
            for c in range(8):
                bk.dma("sy", woT[c][:], dd["wout_h"][:, c * 128:(c + 1) * 128],
                       transpose=True, reads=[dd["wout_h"]], writes=[woT[c]],
                       tag="F3w", grp="f_w")
            aoT = [sb3(f"f3_aoT{c}", [128, 512], BF16) for c in range(8)]
            xrt = sb3("f3_xr", [128, D], F32)
            g1t = sb3("f3_g1", [128, D], F32)
            ot = sb3("f3_o", [128, D], F32)
            pw = [s3.enter_context(nc.psum_tensor(f"f3_pw{i}", [128, 512],
                                                  F32)) for i in range(2)]
            for q in range(T // 512):
                q0 = q * 512
                for c in range(8):
                    bk.dma("sy", aoT[c][:],
                           dd["ao_h"][q0:q0 + 512, c * 128:(c + 1) * 128],
                           transpose=True,
                           reads=[(dd["ao_h"], q0 // 128 + j)
                                  for j in range(4)],
                           writes=[aoT[c]], tag=f"F3a{q}", grp="f_at")
                for kq in range(4):
                    ti = q * 4 + kq
                    tg = f"F3t{ti}"
                    bk.dma("sy", xrt[:], dd["x_res"][ti * 128:(ti + 1) * 128, :],
                           reads=[(dd["x_res"], ti)], writes=[xrt], tag=tg,
                           grp="f_in")
                    bk.dma("sy", g1t[:], dd["g1"][ti * 128:(ti + 1) * 128, :],
                           reads=[(dd["g1"], ti)], writes=[g1t], tag=tg,
                           grp="f_in")
                    for b2 in range(2):
                        pp = pw[b2]
                        ds_ = slice(b2 * 512, (b2 + 1) * 512)
                        for c in range(8):
                            bk.op("pe", "matmul", pp[:],
                                  aoT[c][:, kq * 128:(kq + 1) * 128],
                                  woT[c][:, ds_], start=(c == 0), stop=(c == 7),
                                  reads=[aoT[c], woT[c]], writes=[pp],
                                  accum=(c > 0), tag=tg)
                        bk.op("ve", "tensor_mul", g1t[:, ds_], pp[:],
                              g1t[:, ds_], reads=[pp, g1t], writes=[g1t],
                              tag=tg)
                        bk.op("ve", "tensor_add", ot[:, ds_], g1t[:, ds_],
                              xrt[:, ds_], reads=[g1t, xrt], writes=[(ot, b2)],
                              tag=tg)
                    bk.dma("sy", dd["y_out"][ti * 128:(ti + 1) * 128, :],
                           ot[:], reads=[(ot, 0), (ot, 1)],
                           writes=[(dd["y_out"], ti)], tag=tg, grp="f_y")
        _end_stage(cx, "stF")



def _prep_inputs(kw):
    x = np.ascontiguousarray(np.asarray(kw["L0_attn_inputs"], dtype=np.float32))
    full = {
        "ln1_s": kw["L0_attn_ln1_scale"], "ln1_b": kw["L0_attn_ln1_bias"],
        "wq": np.asarray(kw["L0_attn_wq"]).reshape(D, D),
        "wk": np.asarray(kw["L0_attn_wk"]).reshape(D, D),
        "wv": np.asarray(kw["L0_attn_wv"]).reshape(D, D),
        "wo": np.asarray(kw["L0_attn_wo"]).reshape(D, D),
        "ln2_s": kw["L0_attn_ln2_scale"], "ln2_b": kw["L0_attn_ln2_bias"],
        "w1": kw["L0_attn_w1"], "b1": kw["L0_attn_b1"],
        "w2": kw["L0_attn_w2"], "b2": kw["L0_attn_b2"],
        "gate_w": kw["L0_ffn_gate_w"], "gate_b": kw["L0_ffn_gate_b"],
        "w13": kw["L0_ffn_w13"], "w2m": kw["L0_ffn_w2"],
        "w13s": kw["L0_ffn_w13_s"], "w2s": kw["L0_ffn_w2_s"],
        "wqkv": kw["L1_attn_w_qkv"], "wout": kw["L1_attn_w_out"],
        "qnw": kw["L1_attn_q_norm_w"], "knw": kw["L1_attn_k_norm_w"],
        "wg": kw["L1_ffn_w_gate"], "wu": kw["L1_ffn_w_up"],
        "wd": kw["L1_ffn_w_down"],
    }
    ids = np.ascontiguousarray(np.asarray(kw["L0_ffn_input_ids"], dtype=np.int32))
    full = {k: np.ascontiguousarray(np.asarray(v, dtype=np.float32))
            for k, v in full.items()}
    full["input_ids"] = ids
    in_maps = []
    for c in range(N_CORES):
        m = dict(full)
        m["x_in"] = np.ascontiguousarray(
            x[c * BC:(c + 1) * BC].reshape(T, D))
        in_maps.append(m)
    return in_maps


def kernel(**kw):
    in_maps = _prep_inputs(kw)
    nc = build_program()
    res = run_bass_kernel_spmd(nc, in_maps, core_ids=list(range(N_CORES)))
    out = np.concatenate(
        [r["y_out"].reshape(BC, S, D) for r in res.results], axis=0)
    return out.astype(np.float32)


# version 07
# speedup vs baseline: 1.0307x; 1.0307x over previous
import math
from contextlib import ExitStack

import numpy as np

import concourse.bass as bass
import concourse.mybir as mybir
from concourse.bass_utils import run_bass_kernel_spmd

F32 = mybir.dt.float32
BF16 = mybir.dt.bfloat16
AF = mybir.ActivationFunctionType
ALU = mybir.AluOpType
AX = mybir.AxisListType

N_CORES = 8
B, S, D, H, DH, MLP = 16, 2048, 1024, 16, 64, 4096
BC = B // N_CORES
T = BC * S
NT = T // 128
NE = 16
IM = 512
I1 = 2816
EPS0 = 1e-5
EPS1 = 1e-6
KEPS = 1e-3
RMS_EPS = 1e-6
FP4_MIDS = [0.25, 0.75, 1.25, 1.75, 2.5, 3.5, 5.0]
FP4_STEP = [0.5, 0.5, 0.5, 0.5, 1.0, 1.0, 2.0]
GELU_C = 0.7978845608028654
GELU_A = 0.044715
TEST_NO_B2 = False
TEST_NO_GELU = False



_ENGINES = {
    "pe": "tensor",
    "ve": "vector",
    "se": "scalar",
    "po": "gpsimd",
    "sy": "sync",
}


class _Op:
    __slots__ = ("fn", "args", "kwargs", "kind", "marked", "waits", "sub",
                 "emitted", "tag", "grp", "mval", "grp_sem")

    def __init__(self, fn, args, kwargs, kind, sub, tag=""):
        self.fn = fn
        self.args = args
        self.kwargs = kwargs
        self.kind = kind
        self.marked = kind == "d"
        self.waits = []
        self.sub = sub
        self.emitted = False
        self.tag = tag


class Tok:
    __slots__ = ("stream", "kind", "sub", "op", "clock", "grp")

    def __init__(self, stream, kind, sub, op, clock, grp=None):
        self.stream = stream
        self.kind = kind
        self.sub = sub
        self.op = op
        self.clock = clock
        self.grp = grp


class _Stream:
    def __init__(self, name):
        self.name = name
        self.ops = []
        self.nsub = {"c": 0, "d": 0}
        self.clock = {}
        self.last_tok = None
        self.last_by_kind = {}
        self.marked_prefix = {"c": [0], "d": [0]}
        self.emitted_sub = {"c": 0, "d": 0}


def _key(x):
    if isinstance(x, tuple):
        base, sub = x
        return (_key(base)[0], sub)
    if isinstance(x, bass.AP):
        return (x.tensor.name, None)
    return (x.name, None)


class BK:

    def __init__(self, nc, ctx):
        self.nc = nc
        self.ctx = ctx
        self.streams = {n: _Stream(n) for n in _ENGINES}
        self.sems = {}
        for n in _ENGINES:
            self.sems[(n, "c")] = ctx.enter_context(nc.semaphore(f"p_{n}"))
        self.grp_sem = {}
        self.grp_count = {}
        self.grp_ck = {}
        self.grp_queue = {}
        self.last_w = {}
        self.readers = {}
        self.psum_acc = {}

    SEM_ROLL = 30000

    def _grp(self, grp, inc):
        if grp not in self.grp_sem:
            self.grp_sem[grp] = self.ctx.enter_context(
                self.nc.semaphore(f"g_{grp}"))
            self.grp_count[grp] = 0
            self.grp_raw = getattr(self, "grp_raw", {})
            self.grp_gen = getattr(self, "grp_gen", {})
            self.grp_raw[grp] = 0
            self.grp_gen[grp] = 0
            self.grp_inc = getattr(self, "grp_inc", {})
            self.grp_inc[grp] = inc
        return self.grp_sem[grp]

    def _make_ck(self, grp):
        qn = self.grp_queue[grp]
        qst = self.streams[qn]
        cur = self.grp_count[grp]
        wo = _Op(None, (), {}, "w", 0, f"ck_{grp}")
        wo.grp = None
        wo.waits.append(("S", self.grp_sem[grp],
                         self.grp_inc[grp] * self.grp_raw[grp]))
        qst.ops.append(wo)
        cktok = self.op(qn, "nop", kind="c", mark=True, tag=f"ck_{grp}{cur}")
        cktok.clock[("D", grp)] = cur
        qst.clock[("D", grp)] = cur
        self.grp_ck[grp] = (cur, cktok)
        return self.grp_ck[grp]

    def _maybe_roll(self, grp):
        inc = self.grp_inc[grp]
        if (self.grp_raw[grp] + 1) * inc > self.SEM_ROLL:
            self._make_ck(grp)
            self.grp_gen[grp] += 1
            self.grp_sem[grp] = self.ctx.enter_context(
                self.nc.semaphore(f"g_{grp}_{self.grp_gen[grp]}"))
            self.grp_raw[grp] = 0

    def _dep(self, op, st, tok):
        if tok is None or tok.op is op:
            return
        cl = st.clock
        if tok.grp is not None:
            grp = tok.grp
            if cl.get(("D", grp), 0) >= tok.sub:
                return
            ck = self.grp_ck.get(grp)
            if ck is None or ck[0] < tok.sub:
                ck = self._make_ck(grp)
            self._dep(op, st, ck[1])
            cl[("D", tok.grp)] = max(cl.get(("D", tok.grp), 0), ck[0])
            return
        k = (tok.stream, tok.kind)
        if cl.get(k, 0) >= tok.sub:
            return
        if tok.op.emitted and not tok.op.marked:
            raise RuntimeError(
                f"late dependency on already-emitted unmarked op "
                f"{tok.stream}/{tok.kind}/{tok.sub} {tok.op.fn}")
        tok.op.marked = True
        op.waits.append(tok)
        for kk, vv in tok.clock.items():
            if cl.get(kk, 0) < vv:
                cl[kk] = vv
        cl[k] = max(cl.get(k, 0), tok.sub)

    def op(self, stream, fn, *args, reads=(), writes=(), kind="c",
           accum=False, mark=False, tag="", grp=None, **kwargs):
        st = self.streams[stream]
        st.nsub[kind] += 1
        if kind == "d":
            grp = grp or stream
            self._grp(grp, 16)
            self.grp_queue[grp] = stream
            self._maybe_roll(grp)
        elif stream == "po" and fn != "nop":
            grp = grp or "poQ"
            self._grp(grp, 1)
            self.grp_queue[grp] = stream
            self._maybe_roll(grp)
        else:
            grp = None
        o = _Op(fn, args, kwargs, kind, st.nsub[kind], tag)
        o.grp = grp
        if mark:
            o.marked = True
        psum_names = set()
        for r in list(reads) + list(writes):
            if isinstance(r, bass.AP) and isinstance(
                    r.tensor, bass.PSumTensorHandle):
                psum_names.add(r.tensor.name)
            elif isinstance(r, tuple) and isinstance(
                    r[0], bass.PSumTensorHandle):
                psum_names.add(r[0].name)
            elif isinstance(r, bass.PSumTensorHandle):
                psum_names.add(r.name)
        for r in reads:
            name, sub = _key(r)
            d = self.last_w.get(name)
            if d:
                if sub is None:
                    for tok in d.values():
                        self._dep(o, st, tok)
                else:
                    for s2 in (sub, "*"):
                        if s2 in d:
                            self._dep(o, st, d[s2])
        for w in writes:
            name, sub = _key(w)
            d = self.last_w.get(name)
            if d and not accum:
                if sub is None:
                    for tok in d.values():
                        self._dep(o, st, tok)
                else:
                    for s2 in (sub, "*"):
                        if s2 in d:
                            self._dep(o, st, d[s2])
            for rsub, tok in self.readers.get(name, []):
                if accum:
                    continue
                if sub is None or rsub is None or rsub == sub:
                    self._dep(o, st, tok)
        if psum_names and stream != "pe":
            for name in psum_names:
                for (astream, tok2) in self.psum_acc.get(name, []):
                    if astream != stream:
                        self._dep(o, st, tok2)
        tok = Tok(stream, kind, o.sub, o, dict(st.clock), grp)
        if grp is not None:
            self.grp_count[grp] += 1
            self.grp_raw[grp] += 1
            tok.sub = self.grp_count[grp]
            o.grp_sem = self.grp_sem[grp]
        else:
            tok.clock[(stream, kind)] = o.sub
        for r in reads:
            name, sub = _key(r)
            self.readers.setdefault(name, []).append((sub, tok))
        for w in writes:
            name, sub = _key(w)
            sub = "*" if sub is None else sub
            d = self.last_w.setdefault(name, {})
            if sub == "*":
                d.clear()
            d[sub] = tok
            if not accum:
                self.readers[name] = [
                    (rs, rt) for (rs, rt) in self.readers.get(name, [])
                    if not (sub == "*" or rs is None or rs == sub)]
        if psum_names and stream != "pe":
            for name in psum_names:
                lst = [e for e in self.psum_acc.get(name, [])
                       if e[0] == stream]
                lst.append((stream, tok))
                self.psum_acc[name] = lst
        st.ops.append(o)
        st.last_tok = tok
        st.last_by_kind[kind] = tok
        return tok

    def dma(self, queue, out, in_, reads=(), writes=(), tag="", grp=None, **kw):
        return self.op(queue, "dma_start", out, in_, kind="d",
                       reads=list(reads) + [in_], writes=list(writes) + [out],
                       tag=tag, grp=grp, **kw)

    def barrier(self):
        for grp in list(self.grp_sem):
            ck = self.grp_ck.get(grp)
            if ck is None or ck[0] < self.grp_count[grp]:
                self._make_ck(grp)
        toks = [t for st in self.streams.values()
                for t in st.last_by_kind.values()]
        for n, st in self.streams.items():
            o = _Op(None, (), {}, "w", 0, "barrier")
            o.grp = None
            for t in toks:
                self._dep(o, st, t)
            if o.waits:
                st.ops.append(o)

    def flush(self, blk):
        for st in self.streams.values():
            base = st.emitted_sub.get("mc", 0)
            for o in st.ops:
                if (o.kind == "c" and o.marked and o.grp is None):
                    base += 1
                o.mval = base
            st.emitted_sub["mc"] = base
            assert base < self.SEM_ROLL, f"compute sem overflow on {st.name}" 
        for name, st in self.streams.items():
            if not st.ops:
                continue
            ops = st.ops
            sems = self.sems

            def emit(eng, ops=ops, st=st, sems=sems, self=self):
                def wsem(t):
                    if isinstance(t, tuple):
                        return t[1], t[2]
                    return sems[(t.stream, "c")], t.op.mval
                for o in ops:
                    attach = None
                    if o.kind == "w":
                        for t in o.waits:
                            eng.wait_ge(*wsem(t))
                        o.emitted = True
                        continue
                    if o.waits:
                        attach = o.waits[0]
                        for t in o.waits[1:]:
                            eng.wait_ge(*wsem(t))
                    inst = getattr(eng, o.fn)(*o.args, **o.kwargs)
                    if attach is not None:
                        inst._wait_ge(*wsem(attach))
                    if o.grp is not None:
                        inst.then_inc(o.grp_sem, self.grp_inc[o.grp])
                    elif o.marked:
                        inst.then_inc(sems[(st.name, "c")], 1)
                    o.emitted = True

            getattr(blk, _ENGINES[name])(emit)
            st.ops = []



class Ctx:

    def __init__(self, nc, bk, blk, g):
        self.nc = nc
        self.bk = bk
        self.blk = blk
        self.g = g
        self.dram = {}
        self.pers = {}


def _declare_dram(nc, dbg):
    dd = {}

    def ext(name, shape, dt=F32):
        dd[name] = nc.dram_tensor(name, shape, dt, kind="ExternalInput")

    def scratch(name, shape, dt):
        kind = "ExternalOutput" if name in dbg else "Internal"
        dd[name] = nc.dram_tensor(name, shape, dt, kind=kind)

    ext("x_in", [T, D])
    ext("input_ids", [2, 512], mybir.dt.int32)
    ext("ln1_s", [D]); ext("ln1_b", [D]); ext("ln2_s", [D]); ext("ln2_b", [D])
    ext("wq", [D, D]); ext("wk", [D, D]); ext("wv", [D, D]); ext("wo", [D, D])
    ext("w1", [D, MLP]); ext("b1", [MLP]); ext("w2", [MLP, D]); ext("b2", [D])
    ext("gate_w", [NE, D]); ext("gate_b", [NE])
    ext("w13", [NE, 2 * IM, D]); ext("w2m", [NE, D, IM])
    ext("w13s", [2 * IM, D]); ext("w2s", [D, IM])
    ext("wqkv", [3 * D, D]); ext("wout", [D, D])
    ext("qnw", [DH]); ext("knw", [DH])
    ext("wg", [I1, D]); ext("wu", [I1, D]); ext("wd", [D, I1])
    dd["y_out"] = nc.dram_tensor("y_out", [T, D], F32, kind="ExternalOutput")

    scratch("wq_h", [D, D], BF16); scratch("wk_h", [D, D], BF16)
    scratch("wv_h", [D, D], BF16); scratch("wo_h", [D, D], BF16)
    scratch("w1_h", [D, MLP], BF16); scratch("w2_h", [MLP, D], BF16)
    scratch("w13_h", [NE, 2 * IM, D], BF16); scratch("w2m_h", [NE, D, IM], BF16)
    scratch("w13s_h", [2 * IM, D], BF16); scratch("w2s_h", [D, IM], BF16)
    scratch("wqkv_h", [3 * D, D], BF16); scratch("wout_h", [D, D], BF16)
    scratch("gate_w_h", [NE, D], BF16)
    scratch("wgq_h", [I1, D], BF16); scratch("wuq_h", [I1, D], BF16)
    scratch("wdq_h", [D, I1], BF16)

    scratch("xn0", [T, D], F32)
    scratch("xn0_h", [T, D], BF16)
    scratch("xln1_h", [T, D], BF16)
    scratch("x1", [T, D], F32)
    scratch("xln2_h", [T, D], BF16)
    scratch("g0", [T, D], F32)
    scratch("hm_h", [T, IM], BF16)
    scratch("x_res", [T, D], F32)
    scratch("xn1_h", [T, D], BF16)
    scratch("xq_h", [T, D], BF16)
    scratch("qro_h", [T, D], BF16)
    scratch("kro_h", [T, D], BF16)
    scratch("ao_h", [T, D], BF16)
    scratch("vro_h", [T, H * 65], BF16)
    scratch("hq_h", [T, I1], BF16)
    scratch("g1", [T, D], F32)
    return dd


def build_program(dbg=(), stages=("prep", "A", "D", "B", "C", "E", "G", "F")):
    nc = bass.Bass("TRN2", target_bir_lowering=False)
    dd = _declare_dram(nc, set(dbg))
    g = ExitStack()
    with g:
        bk = BK(nc, g)
        blk = g.enter_context(nc.Block())
        cx = Ctx(nc, bk, blk, g)
        cx.dram = dd
        stage_prep(cx, "prep" in stages)
        if "A" in stages:
            stage_A(cx)
        if "D" in stages:
            stage_D(cx)
        if "B" in stages:
            stage_B(cx)
        if "C" in stages:
            stage_C(cx)
        if "E" in stages:
            stage_E(cx)
        if "G" in stages:
            stage_G(cx)
        if "F" in stages:
            stage_F(cx)
        bk.barrier()
        bk.flush(blk)
    return nc


def _end_stage(cx, label=None):
    cx.bk.barrier()
    if label:
        with cx.nc.named_scope(label):
            cx.bk.flush(cx.blk)
    else:
        cx.bk.flush(cx.blk)
    cx.bk.last_w.clear()
    cx.bk.readers.clear()



def stage_prep(cx, do_casts):
    bk, nc, dd, g = cx.bk, cx.nc, cx.dram, cx.g

    if do_casts:
        casts = [("wq", "wq_h"), ("wk", "wk_h"), ("wv", "wv_h"),
                 ("wo", "wo_h"), ("w1", "w1_h"), ("w2", "w2_h"),
                 ("w13s", "w13s_h"), ("w2s", "w2s_h"),
                 ("wqkv", "wqkv_h"), ("wout", "wout_h"), ("gate_w", "gate_w_h")]
        for e in range(NE):
            casts.append((("w13", e), ("w13_h", e)))
            casts.append((("w2m", e), ("w2m_h", e)))
        for src, dst in casts:
            if isinstance(src, tuple):
                s_ap = dd[src[0]][src[1]]
                d_ap = dd[dst[0]][dst[1]]
                rd, wr = (dd[src[0]], src[1]), (dd[dst[0]], dst[1])
            else:
                s_ap, d_ap = dd[src][:], dd[dst][:]
                rd, wr = dd[src], dd[dst]
            bk.op("po", "dma_start", d_ap, s_ap, kind="d",
                  reads=[rd], writes=[wr], tag=f"cast_{src}")

    pers = cx.pers
    ident1 = g.enter_context(nc.sbuf_tensor("ident1", [1, 1], F32))
    bk.op("po", "memset", ident1[:], 1.0, writes=[ident1])
    vecs = [("ln1_s", D), ("ln1_b", D), ("ln2_s", D), ("ln2_b", D),
            ("b1", MLP), ("b2", D)]
    for vec, n in vecs:
        pers[vec] = g.enter_context(
            nc.sbuf_tensor(f"pp_{vec}", [128, n // 128], F32))
    with nc.sbuf_tensor("vrow", [1, MLP], F32) as vrow, \
         nc.psum_tensor("vps", [128, MLP // 128], F32) as vps:
        for vec, n in vecs:
            a = n // 128
            t = pers[vec]
            bk.dma("sy", vrow[:, 0:n], dd[vec][None, :], writes=[vrow],
                   tag=f"vec_{vec}")
            for c in range(a):
                bk.op("pe", "transpose", vps[:, c:c + 1],
                      vrow[:, c * 128:(c + 1) * 128], ident1[:],
                      reads=[vrow, ident1], writes=[(vps, c)], tag=f"vec_{vec}")
            bk.op("ve", "tensor_copy", t[:, 0:a], vps[:, 0:a],
                  reads=[(vps, cc) for cc in range(a)], writes=[t],
                  mark=True, tag=f"vec_{vec}")
        bk.barrier()
        bk.flush(cx.blk)
    ones1 = g.enter_context(nc.sbuf_tensor("ones1", [1, 128], BF16))
    bk.op("po", "memset", ones1[:], 1.0, writes=[ones1])
    brows = [("gate_b", NE), ("qnw", DH), ("knw", DH)]
    for vec, n in brows:
        pers[vec] = g.enter_context(nc.sbuf_tensor(f"rep_{vec}", [128, n], F32))
    with nc.sbuf_tensor("brow_b", [1, DH], BF16) as row, \
         nc.sbuf_tensor("brow_f", [1, DH], F32) as rowf, \
         nc.psum_tensor("brow_ps", [128, DH], F32) as ps:
        for vec, n in brows:
            bk.dma("sy", rowf[:, 0:n], dd[vec][None, :], writes=[rowf],
                   tag=f"row_{vec}", grp="p_vec")
            bk.op("ve", "tensor_copy", row[:, 0:n], rowf[:, 0:n],
                  reads=[rowf], writes=[row])
            bk.op("pe", "matmul", ps[:, 0:n], ones1[:], row[:, 0:n],
                  start=True, stop=True, reads=[ones1, row], writes=[ps])
            bk.op("ve", "tensor_copy", pers[vec][:, 0:n], ps[:, 0:n],
                  reads=[ps], writes=[pers[vec]], mark=True)
        bk.barrier()
        bk.flush(cx.blk)
    mask = g.enter_context(nc.sbuf_tensor("mask_incl", [128, 128], F32))
    bk.op("po", "iota", mask[:], [[1, 128]], base=0, channel_multiplier=-1,
          allow_small_or_imprecise_dtypes=True, writes=[mask])
    bk.op("po", "tensor_scalar", mask[:], mask[:], 0.0, None, ALU.is_ge,
          reads=[mask], writes=[mask])
    pers["mask_incl"] = mask
    identB = g.enter_context(nc.sbuf_tensor("identB", [128, 128], BF16))
    iotaB = g.enter_context(nc.sbuf_tensor("iotaB", [128, 128], F32))
    bk.op("po", "iota", iotaB[:], [[1, 128]], base=0, channel_multiplier=-1,
          allow_small_or_imprecise_dtypes=True, writes=[iotaB])
    bk.op("po", "tensor_scalar", identB[:], iotaB[:], 0.0, None, ALU.is_equal,
          reads=[iotaB], writes=[identB])
    pers["identB"] = identB
    onec = g.enter_context(nc.sbuf_tensor("onec", [128, 1], BF16))
    bk.op("po", "memset", onec[:], 1.0, writes=[onec])
    pers["onec"] = onec
    onesP = g.enter_context(nc.sbuf_tensor("onesP", [128, 64], BF16))
    bk.op("po", "memset", onesP[:], 1.0, writes=[onesP])
    pers["onesP"] = onesP
    onesF = g.enter_context(nc.sbuf_tensor("onesF", [1, 128], F32))
    bk.op("po", "memset", onesF[:], 1.0, writes=[onesF])
    pers["onesF"] = onesF
    b2row2 = g.enter_context(nc.sbuf_tensor("b2row2", [1, D], F32))
    with nc.sbuf_tensor("b2row", [1, D], F32) as b2row:
        bk.dma("sy", b2row[:], dd["b2"][None, :], writes=[b2row], tag="b2row",
               grp="p_vec")
        bk.op("ve", "tensor_scalar_mul", b2row2[:], b2row[:], 2.0,
              reads=[b2row], writes=[b2row2], mark=True)
        bk.barrier()
        bk.flush(cx.blk)
    pers["b2row2"] = b2row2
    _end_stage(cx)



def _ln_tile(bk, sb, se_eps, x_ap, out_f32, out_bf, bufs, tag):
    negm, sq, vs, v1, rstd = bufs
    ve, se = "ve", "se"
    bk.op(ve, "reduce_sum", negm[:], x_ap, axis=AX.X, negate=True,
          reads=[x_ap], writes=[negm], tag=tag)
    bk.op(ve, "tensor_scalar_mul", negm[:], negm[:], 1.0 / D,
          reads=[negm], writes=[negm], tag=tag)
    bk.op(se, "activation", sq[:], x_ap, AF.Square, bias=negm[:, 0:1],
          accum_out=vs[:, 0:1], reads=[x_ap, negm], writes=[sq, vs], tag=tag)
    bk.op(ve, "tensor_scalar", v1[:], vs[:], 1.0 / D, se_eps, ALU.mult, ALU.add,
          reads=[vs], writes=[v1], tag=tag)
    bk.op(se, "activation", rstd[:], v1[:], AF.Ln, reads=[v1], writes=[rstd],
          tag=tag)
    bk.op(se, "activation", rstd[:], rstd[:], AF.Exp, scale=-0.5,
          reads=[rstd], writes=[rstd], tag=tag)
    outs = []
    if out_f32 is not None:
        bk.op(ve, "tensor_scalar", out_f32[:], x_ap, negm[:, 0:1], rstd[:, 0:1],
              ALU.add, ALU.mult, reads=[x_ap, negm, rstd], writes=[out_f32],
              tag=tag)
    if out_bf is not None:
        src = out_f32 if out_f32 is not None else None
        if src is not None:
            bk.op(ve, "tensor_copy", out_bf[:], src[:], reads=[src],
                  writes=[out_bf], tag=tag)
        else:
            bk.op(ve, "tensor_scalar", out_bf[:], x_ap, negm[:, 0:1],
                  rstd[:, 0:1], ALU.add, ALU.mult,
                  reads=[x_ap, negm, rstd], writes=[out_bf], tag=tag)


def stage_A(cx):
    bk, nc, dd = cx.bk, cx.nc, cx.dram
    with ExitStack() as st:
        sb = lambda n, sh, d: st.enter_context(nc.sbuf_tensor(n, sh, d))
        NB = 3
        xt = [sb(f"a_x{i}", [128, D], F32) for i in range(NB)]
        xn = [sb(f"a_xn{i}", [128, D], F32) for i in range(NB)]
        xnb = [sb(f"a_xnb{i}", [128, D], BF16) for i in range(NB)]
        xlb = [sb(f"a_xlb{i}", [128, D], BF16) for i in range(NB)]
        sq = [sb(f"a_sq{i}", [128, D], F32) for i in range(NB)]
        cols = [[sb(f"a_c{i}_{j}", [128, 1], F32) for j in range(4)]
                for i in range(NB)]
        for t in range(NT):
            i = t % NB
            x_ap = dd["x_in"][t * 128:(t + 1) * 128, :]
            bk.dma("sy", xt[i][:], x_ap, tag=f"A{t}", grp="a_in")
            negm, vs, v1, rstd = cols[i]
            _ln_tile(bk, sb, EPS0, xt[i][:], xn[i], xnb[i],
                     (negm, sq[i], vs, v1, rstd), f"A{t}")
            bk.dma("sy", dd["xn0"][t * 128:(t + 1) * 128, :], xn[i][:],
                   reads=[xn[i]], writes=[(dd["xn0"], t)], tag=f"A{t}",
                   grp="a_out")
            bk.dma("sy", dd["xn0_h"][t * 128:(t + 1) * 128, :], xnb[i][:],
                   reads=[xnb[i]], writes=[(dd["xn0_h"], t)], tag=f"A{t}",
                   grp="a_out")
            _ln_tile(bk, sb, EPS1, xn[i][:], None, xlb[i],
                     (negm, sq[i], vs, v1, rstd), f"A{t}l")
            bk.dma("sy", dd["xln1_h"][t * 128:(t + 1) * 128, :], xlb[i][:],
                   reads=[xlb[i]], writes=[(dd["xln1_h"], t)], tag=f"A{t}",
                   grp="a_out")
        _end_stage(cx, "stA")


def stage_D(cx):
    bk, nc, dd, pers = cx.bk, cx.nc, cx.dram, cx.pers
    TH = T // 2
    NK = TH // 128
    with ExitStack() as st:
        sb = lambda n, sh, d: st.enter_context(nc.sbuf_tensor(n, sh, d))
        ps_ = lambda n, sh: st.enter_context(nc.psum_tensor(n, sh, F32))
        gwn = sb("d_gwn", [16, D], BF16)
        gwT = sb("d_gwT", [128, 8 * NE], BF16)
        psg = st.enter_context(nc.psum_tensor("d_psg", [128, NE], BF16))
        bk.dma("sy", gwn[:], dd["gate_w_h"][:], tag="gwT", grp="d_x")
        for c in range(8):
            bk.op("pe", "transpose", psg[:], gwn[:, c * 128:(c + 1) * 128],
                  pers["identB"][0:16, 0:16], reads=[gwn, pers["identB"]],
                  writes=[psg], tag="gwT")
            bk.op("ve", "tensor_copy", gwT[:, c * NE:(c + 1) * NE], psg[:],
                  reads=[psg], writes=[(gwT, c)], tag="gwT")
        xT = [sb(f"d_xT{c}", [128, TH], BF16) for c in range(8)]
        w13T = [[sb(f"d_w13T{p}_{c}", [128, 2 * IM], BF16) for c in range(8)]
                for p in range(2)]
        w2T = [[sb(f"d_w2T{p}_{c}", [128, D], BF16) for c in range(4)]
               for p in range(2)]
        hT = [[sb(f"d_hT{pp_}_{c}", [128, 512], BF16) for c in range(4)]
              for pp_ in range(2)]
        y_sb = sb("d_y", [128, NK * D], F32)
        comb = sb("d_comb", [128, NK * NE], F32)
        S_sb = sb("d_S", [128, NE], F32)
        biased = sb("d_bias", [128, NE], F32)
        eq = sb("d_eq", [128, NE], F32)
        tmp = sb("d_tmp", [128, NE], F32)
        mx = sb("d_mx", [128, 1], F32)
        ssum = sb("d_ssum", [128, 1], F32)
        sg = [sb(f"d_sg{p}", [128, 512], F32) for p in range(2)]
        hbf = [sb(f"d_hbf{p}", [128, 512], BF16) for p in range(2)]
        g0t = sb("d_g0t", [128, D], F32)
        psc = ps_("d_psc", [128, NE])
        pm = [ps_(f"d_pm{i}", [128, 512]) for i in range(4)]
        py = [ps_(f"d_py{i}", [128, 512]) for i in range(2)]
        for hf in range(2):
            t0 = hf * TH
            for c in range(8):
                bk.dma("sy", xT[c][:],
                       dd["xn0_h"][t0:t0 + TH, c * 128:(c + 1) * 128],
                       transpose=True,
                       reads=[(dd["xn0_h"], t) for t in range(hf * NK, (hf + 1) * NK)],
                       writes=[xT[c]], tag=f"D{hf}xT", grp="d_x")
            for k in range(NK):
                kk = slice(k * 128, (k + 1) * 128)
                for c in range(8):
                    bk.op("pe", "matmul", psc[:], xT[c][:, kk],
                          gwT[:, c * NE:(c + 1) * NE],
                          start=(c == 0), stop=(c == 7),
                          reads=[xT[c], (gwT, c)], writes=[psc],
                          accum=(c > 0), tag=f"D{hf}r{k}")
                bk.op("se", "activation", S_sb[:], psc[:], AF.Sigmoid,
                      reads=[psc], writes=[S_sb], tag=f"D{hf}r{k}")
                bk.op("ve", "tensor_add", biased[:], S_sb[:], pers["gate_b"][:, 0:NE],
                      reads=[S_sb, pers["gate_b"]], writes=[biased], tag=f"D{hf}r{k}")
                cslice = slice(k * NE, (k + 1) * NE)
                for r in range(4):
                    bk.op("ve", "reduce_max", mx[:], biased[:], axis=AX.X,
                          reads=[biased], writes=[mx], tag=f"D{hf}r{k}")
                    bk.op("ve", "tensor_single_scalar", eq[:], biased[:],
                          mx[:, 0:1], ALU.is_equal,
                          reads=[biased, mx], writes=[eq], tag=f"D{hf}r{k}")
                    if r == 0:
                        bk.op("ve", "tensor_mul", comb[:, cslice], eq[:], S_sb[:],
                              reads=[eq, S_sb], writes=[(comb, k)], tag=f"D{hf}r{k}")
                    else:
                        bk.op("ve", "tensor_mul", tmp[:], eq[:], S_sb[:],
                              reads=[eq, S_sb], writes=[tmp], tag=f"D{hf}r{k}")
                        bk.op("ve", "tensor_add", comb[:, cslice], comb[:, cslice],
                              tmp[:], reads=[(comb, k), tmp], writes=[(comb, k)],
                              tag=f"D{hf}r{k}")
                    if r < 3:
                        bk.op("ve", "tensor_scalar_mul", tmp[:], eq[:], -1e9,
                              reads=[eq], writes=[tmp], tag=f"D{hf}r{k}")
                        bk.op("ve", "tensor_add", biased[:], biased[:], tmp[:],
                              reads=[biased, tmp], writes=[biased], tag=f"D{hf}r{k}")
                bk.op("ve", "reduce_sum", ssum[:], comb[:, cslice], axis=AX.X,
                      reads=[(comb, k)], writes=[ssum], tag=f"D{hf}r{k}")
                bk.op("ve", "reciprocal", ssum[:], ssum[:],
                      reads=[ssum], writes=[ssum], tag=f"D{hf}r{k}")
                bk.op("ve", "tensor_scalar_mul", ssum[:], ssum[:], 2.5,
                      reads=[ssum], writes=[ssum], tag=f"D{hf}r{k}")
                bk.op("ve", "tensor_single_scalar", comb[:, cslice], comb[:, cslice],
                      ssum[:, 0:1], ALU.mult,
                      reads=[(comb, k), ssum], writes=[(comb, k)], tag=f"D{hf}r{k}")
            pend = [None]
            for e in range(NE + 1):
                p = e % 2
                shared = e == NE
                w13src = dd["w13s_h"][:] if shared else dd["w13_h"][e]
                w2src = dd["w2s_h"][:] if shared else dd["w2m_h"][e]
                w13key = dd["w13s_h"] if shared else (dd["w13_h"], e)
                w2key = dd["w2s_h"] if shared else (dd["w2m_h"], e)
                for c in range(8):
                    bk.dma("sy", w13T[p][c][:], w13src[:, c * 128:(c + 1) * 128],
                           transpose=True, reads=[w13key], writes=[w13T[p][c]],
                           tag=f"D{hf}e{e}w", grp=f"d_w{p}")
                for c in range(4):
                    bk.dma("sy", w2T[p][c][:], w2src[:, c * 128:(c + 1) * 128],
                           transpose=True, reads=[w2key], writes=[w2T[p][c]],
                           tag=f"D{hf}e{e}w", grp=f"d_w{p}")
                def emit_mm1(q):
                    for kq in range(4):
                        k = q * 4 + kq
                        kk = slice(k * 128, (k + 1) * 128)
                        pg, pu = pm[2 * (k % 2)], pm[2 * (k % 2) + 1]
                        tg = f"D{hf}e{e}k{k}"
                        for half_o in range(2):
                            dst = pg if half_o == 0 else pu
                            for c in range(8):
                                bk.op("pe", "matmul", dst[:],
                                      xT[c][:, kk],
                                      w13T[p][c][:, half_o * IM:(half_o + 1) * IM],
                                      start=(c == 0), stop=(c == 7),
                                      reads=[xT[c], w13T[p][c]], writes=[dst],
                                      accum=(c > 0), tag=tg)
                        bk.op("se", "activation", sg[k % 2][:], pg[:], AF.Silu,
                              reads=[pg], writes=[sg[k % 2]], tag=tg)
                        if shared:
                            bk.op("ve", "scalar_tensor_tensor", hbf[k % 2][:],
                                  sg[k % 2][:], 1.0, pu[:], ALU.mult, ALU.mult,
                                  reads=[sg[k % 2], pu], writes=[hbf[k % 2]],
                                  tag=tg)
                        else:
                            bk.op("ve", "scalar_tensor_tensor", hbf[k % 2][:],
                                  sg[k % 2][:], comb[:, k * NE + e:k * NE + e + 1],
                                  pu[:], ALU.mult, ALU.mult,
                                  reads=[sg[k % 2], pu, (comb, k)],
                                  writes=[hbf[k % 2]], tag=tg)
                        bk.dma("sy", dd["hm_h"][t0 + k * 128:t0 + (k + 1) * 128, :],
                               hbf[k % 2][:], reads=[hbf[k % 2]],
                               writes=[(dd["hm_h"], (hf, k))],
                               tag=tg, grp="d_hs")

                def emit_hload(q):
                    q0 = t0 + q * 512
                    for c in range(4):
                        bk.dma("sy", hT[q % 2][c][:],
                               dd["hm_h"][q0:q0 + 512, c * 128:(c + 1) * 128],
                               transpose=True,
                               reads=[(dd["hm_h"], (hf, q * 4 + j)) for j in range(4)],
                               writes=[hT[q % 2][c]], tag=f"D{hf}e{e}q{q}",
                               grp="d_ht")

                def emit_mm2(q, e2, p2):
                    for kq in range(4):
                        k = q * 4 + kq
                        tg = f"D{hf}e{e2}y{k}"
                        for b in range(2):
                            pyb = py[b]
                            for c in range(4):
                                bk.op("pe", "matmul", pyb[:],
                                      hT[q % 2][c][:, kq * 128:(kq + 1) * 128],
                                      w2T[p2][c][:, b * 512:(b + 1) * 512],
                                      start=(c == 0), stop=(c == 3),
                                      reads=[hT[q % 2][c], w2T[p2][c]], writes=[pyb],
                                      accum=(c > 0), tag=tg)
                            ys = y_sb[:, k * D + b * 512:k * D + (b + 1) * 512]
                            if e2 == 0:
                                bk.op("ve", "tensor_copy", ys, pyb[:],
                                      reads=[pyb], writes=[(y_sb, (k, b))],
                                      tag=tg)
                            else:
                                bk.op("ve", "tensor_add", ys, ys, pyb[:],
                                      reads=[pyb, (y_sb, (k, b))],
                                      writes=[(y_sb, (k, b))], tag=tg)

                for q in range(NK // 4):
                    emit_mm1(q)
                    emit_hload(q)
                    if pend[0] is not None:
                        emit_mm2(*pend[0])
                    pend[0] = (q, e, p)
            if pend[0] is not None:
                emit_mm2(*pend[0])
                pend[0] = None
            for k in range(NK):
                bk.op("se", "activation", g0t[:], y_sb[:, k * D:(k + 1) * D],
                      AF.Silu, reads=[(y_sb, (k, 0)), (y_sb, (k, 1))],
                      writes=[g0t], tag=f"D{hf}g{k}")
                bk.dma("sy", dd["g0"][t0 + k * 128:t0 + (k + 1) * 128, :], g0t[:],
                       reads=[g0t], writes=[(dd["g0"], hf * NK + k)],
                       tag=f"D{hf}g{k}", grp="d_gs")
        _end_stage(cx, "stD")


def stage_B(cx):
    bk, nc, dd, pers = cx.bk, cx.nc, cx.dram, cx.pers
    TH = T // 2
    NK = TH // 128
    identB = pers["identB"]
    with ExitStack() as st:
        sb = lambda n, sh, d: st.enter_context(nc.sbuf_tensor(n, sh, d))
        ps_ = lambda n, sh, d=F32: st.enter_context(nc.psum_tensor(n, sh, d))
        phiq = [sb(f"b_phiq{c}", [128, TH], BF16) for c in range(8)]
        phik = [sb(f"b_phik{c}", [128, TH], BF16) for c in range(8)]
        v_sb = sb("b_v", [128, NK * D], BF16)
        atn = [sb(f"b_atn{c}", [128, TH], BF16) for c in range(8)]
        state_f = sb("b_stf", [128, 65], F32)
        state_b = sb("b_stb", [128, 65], BF16)
        Am = [sb(f"b_Am{i}", [128, 128], BF16) for i in range(2)]
        kj = [sb(f"b_kj{i}", [128, 64], BF16) for i in range(2)]
        recip_f = sb("b_recf", [128, 128], F32)
        recip_b = sb("b_recb", [128, 128], BF16)
        bb = sb("b_bb", [128, 128], F32)

        for hf in range(2):
            t0 = hf * TH
            with ExitStack() as stq:
                psq = [stq.enter_context(nc.psum_tensor(
                    f"b1_psq{hf}_{i}", [128, 512], F32)) for i in range(2)]
                xlT = [stq.enter_context(nc.sbuf_tensor(f"b_xlT{hf}_{c}", [128, 512], BF16))
                       for c in range(8)]
                wq_sb = [stq.enter_context(nc.sbuf_tensor(f"b_wq{hf}_{c}", [128, D], BF16))
                         for c in range(8)]
                wk_sb = [stq.enter_context(nc.sbuf_tensor(f"b_wk{hf}_{c}", [128, D], BF16))
                         for c in range(8)]
                wv_sb = [stq.enter_context(nc.sbuf_tensor(f"b_wv{hf}_{c}", [128, D], BF16))
                         for c in range(8)]
                for c in range(8):
                    bk.dma("sy", wq_sb[c][:], dd["wq_h"][c * 128:(c + 1) * 128, :],
                           reads=[dd["wq_h"]], tag="Bw", grp="b_w")
                    bk.dma("sy", wk_sb[c][:], dd["wk_h"][c * 128:(c + 1) * 128, :],
                           reads=[dd["wk_h"]], tag="Bw", grp="b_w")
                    bk.dma("sy", wv_sb[c][:], dd["wv_h"][c * 128:(c + 1) * 128, :],
                           reads=[dd["wv_h"]], tag="Bw", grp="b_w")
                for qtr in range(4):
                    q0 = t0 + qtr * 512
                    for c in range(8):
                        bk.dma("sy", xlT[c][:],
                               dd["xln1_h"][q0:q0 + 512, c * 128:(c + 1) * 128],
                               transpose=True,
                               reads=[(dd["xln1_h"], q0 // 128 + j) for j in range(4)],
                               writes=[xlT[c]], tag=f"B1x{hf}{qtr}", grp="b_x")
                        bk.op("ve", "tensor_scalar", xlT[c][:], xlT[c][:],
                              pers["ln1_s"][:, c:c + 1],
                              pers["ln1_b"][:, c:c + 1], ALU.mult, ALU.add,
                              reads=[xlT[c], pers["ln1_s"], pers["ln1_b"]],
                              writes=[xlT[c]], tag=f"B1a{hf}{qtr}")
                    for dst, wsb, nm in ((phiq, wq_sb, "q"), (phik, wk_sb, "k")):
                        for dc in range(8):
                            pq = psq[dc % 2]
                            for c in range(8):
                                bk.op("pe", "matmul", pq[:],
                                      wsb[c][:, dc * 128:(dc + 1) * 128],
                                      xlT[c][:], start=(c == 0), stop=(c == 7),
                                      reads=[wsb[c], xlT[c]], writes=[pq],
                                      accum=(c > 0), tag=f"B1{nm}{hf}{qtr}{dc}")
                            bk.op("ve", "tensor_scalar",
                                  dst[dc][:, qtr * 512:(qtr + 1) * 512], pq[:],
                                  0.0, KEPS, ALU.max, ALU.add,
                                  reads=[pq], writes=[(dst[dc], qtr)],
                                  tag=f"B1{nm}e{hf}{qtr}{dc}")
                    for j in range(4):
                        tt = qtr * 4 + j
                        for b2 in range(2):
                            pv = psq[b2]
                            for c in range(8):
                                bk.op("pe", "matmul", pv[:],
                                      xlT[c][:, j * 128:(j + 1) * 128],
                                      wv_sb[c][:, b2 * 512:(b2 + 1) * 512],
                                      start=(c == 0), stop=(c == 7),
                                      reads=[xlT[c], wv_sb[c]], writes=[pv],
                                      accum=(c > 0), tag=f"B1v{hf}{tt}")
                            bk.op("ve", "tensor_copy",
                                  v_sb[:, tt * D + b2 * 512:tt * D + (b2 + 1) * 512],
                                  pv[:], reads=[pv], writes=[(v_sb, (tt, b2))],
                                  tag=f"B1ve{hf}{tt}")
            bk.barrier()
            st2 = ExitStack()
            psA = [st2.enter_context(nc.psum_tensor(f"b2_psA{i}_{hf}",
                   [128, 128], F32)) for i in range(2)]
            psT1 = st2.enter_context(nc.psum_tensor(f"b2_psT_{hf}",
                   [128, 64], BF16))
            psT = [psT1, psT1]
            psN = [st2.enter_context(nc.psum_tensor(f"b2_psN{i}_{hf}",
                   [128, 128], F32)) for i in range(2)]
            psDen = st2.enter_context(nc.psum_tensor(f"b2_psDen_{hf}",
                                                     [128, 128], F32))
            psB = st2.enter_context(nc.psum_tensor(f"b2_psB_{hf}",
                                                   [128, 128], F32))
            psD = st2.enter_context(nc.psum_tensor(f"b2_psD_{hf}",
                                                   [128, 65], F32))
            for h in range(H):
                o = (h % 2) * 64
                P2 = h // 2
                os_ = slice(o, o + 64)
                bk.op("ve", "memset", state_f[os_, :], 0.0,
                      writes=[(state_f, o)], tag=f"B2z{hf}{h}")
                bk.op("ve", "memset", state_b[os_, :], 0.0,
                      writes=[(state_b, o)], tag=f"B2z{hf}{h}")
                for c in range(NK):
                    cs = slice(c * 128, (c + 1) * 128)
                    pq_ap = phiq[P2][os_, cs]
                    pk_ap = phik[P2][os_, cs]
                    v_c = v_sb[:, c * D + h * 64:c * D + h * 64 + 64]
                    tg = f"B2{hf}h{h}c{c}"
                    bk.op("pe", "matmul", psA[c % 2][:], pk_ap, pq_ap, start=True,
                          stop=True, reads=[(phik[P2], c), (phiq[P2], c)],
                          writes=[psA[c % 2]], tag=tg)
                    bk.op("ve", "tensor_tensor", Am[c % 2][:], psA[c % 2][:],
                          pers["mask_incl"][:], ALU.mult,
                          reads=[psA[c % 2], pers["mask_incl"]], writes=[Am[c % 2]], tag=tg)
                    bk.op("pe", "transpose", psT[c % 2][:], pk_ap,
                          identB[os_, os_], reads=[(phik[P2], c), identB],
                          writes=[psT[c % 2]], tag=tg)
                    bk.op("se", "copy", kj[c % 2][:], psT[c % 2][:],
                          reads=[psT[c % 2]], writes=[kj[c % 2]], tag=tg)
                    bk.op("pe", "matmul", psN[c % 2][os_, :], state_b[os_, 0:64],
                          pq_ap, start=True, stop=False, tile_position=(o, o),
                          reads=[(state_b, o), (phiq[P2], c)], writes=[psN[c % 2]],
                          tag=tg)
                    bk.op("pe", "matmul", psN[c % 2][os_, :], v_c, Am[c % 2][:],
                          start=False, stop=True, tile_position=(0, o),
                          reads=[(v_sb, (c, 0)), (v_sb, (c, 1)), Am[c % 2]],
                          writes=[psN[c % 2]], accum=True, tag=tg)
                    bk.op("pe", "matmul", psDen[o:o + 1, :],
                          state_b[os_, 64:65], pq_ap, start=True, stop=False,
                          tile_position=(o, o),
                          reads=[(state_b, o), (phiq[P2], c)], writes=[psDen],
                          tag=tg)
                    bk.op("pe", "matmul", psDen[o:o + 1, :], pers["onec"][:],
                          Am[c % 2][:], start=False, stop=True, tile_position=(0, o),
                          reads=[pers["onec"], Am[c % 2]], writes=[psDen], accum=True,
                          tag=tg)
                    bk.op("pe", "matmul", psD[os_, 0:64], kj[c % 2][:], v_c,
                          start=True, stop=True, tile_position=(0, o),
                          reads=[kj[c % 2], (v_sb, (c, 0)), (v_sb, (c, 1))],
                          writes=[psD], tag=tg)
                    bk.op("pe", "matmul", psD[os_, 64:65], kj[c % 2][:],
                          pers["onec"][:], start=True, stop=True,
                          tile_position=(0, o), reads=[kj[c % 2], pers["onec"]],
                          writes=[psD], accum=True, tag=tg)
                    bk.op("ve", "tensor_add", state_f[os_, :], state_f[os_, :],
                          psD[os_, :], reads=[(state_f, o), psD],
                          writes=[(state_f, o)], tag=tg)
                    bk.op("se", "copy", state_b[os_, :], state_f[os_, :],
                          reads=[(state_f, o)], writes=[(state_b, o)], tag=tg)
                    bk.op("ve", "reciprocal", recip_f[o:o + 1, :],
                          psDen[o:o + 1, :], reads=[psDen],
                          writes=[recip_f], tag=tg)
                    bk.op("se", "copy", recip_b[o:o + 1, :],
                          recip_f[o:o + 1, :], reads=[recip_f],
                          writes=[recip_b], tag=tg)
                    bk.op("pe", "matmul", psB[os_, :],
                          pers["onesP"][o:o + 1, :], recip_b[o:o + 1, :],
                          start=True, stop=True, tile_position=(o, o),
                          reads=[pers["onesP"], recip_b], writes=[psB], tag=tg)
                    bk.op("se", "copy", bb[os_, :], psB[os_, :],
                          reads=[psB], writes=[bb], tag=tg)
                    bk.op("ve", "tensor_tensor", atn[P2][os_, cs], psN[c % 2][os_, :],
                          bb[os_, :], ALU.mult, reads=[psN[c % 2], bb],
                          writes=[(atn[P2], (o, c))], tag=tg)
            bk.barrier()
            st2.close()
            with ExitStack() as st3:
                psq = [st3.enter_context(nc.psum_tensor(
                    f"b3_psq{hf}_{i}", [128, 512], F32)) for i in range(2)]
                wo_sb = [st3.enter_context(
                    nc.sbuf_tensor(f"b_wo{hf}_{c}", [128, D], BF16))
                    for c in range(8)]
                for c in range(8):
                    bk.dma("sy", wo_sb[c][:],
                           dd["wo_h"][c * 128:(c + 1) * 128, :],
                           reads=[dd["wo_h"]], tag="Bw", grp="b_w")
                x1sb = st3.enter_context(nc.sbuf_tensor(f"b3_x1_{hf}", [128, D], F32))
                xnsb = st3.enter_context(nc.sbuf_tensor(f"b3_xn_{hf}", [128, D], F32))
                xl2b = st3.enter_context(nc.sbuf_tensor(f"b3_xl2_{hf}", [128, D], BF16))
                sq3 = st3.enter_context(nc.sbuf_tensor(f"b3_sq_{hf}", [128, D], F32))
                cols3 = [st3.enter_context(nc.sbuf_tensor(f"b3_c{hf}_{j}", [128, 1], F32))
                         for j in range(4)]
                for tt in range(NK):
                    tg = f"B3{hf}t{tt}"
                    ta = t0 + tt * 128
                    bk.dma("sy", xnsb[:], dd["xn0"][ta:ta + 128, :],
                           reads=[(dd["xn0"], ta // 128)], writes=[xnsb],
                           tag=tg, grp="b_x")
                    for b2 in range(2):
                        pw = psq[b2]
                        for c in range(8):
                            bk.op("pe", "matmul", pw[:],
                                  atn[c][:, tt * 128:(tt + 1) * 128],
                                  wo_sb[c][:, b2 * 512:(b2 + 1) * 512],
                                  start=(c == 0), stop=(c == 7),
                                  reads=[(atn[c], (0, tt)), (atn[c], (64, tt)),
                                         wo_sb[c]],
                                  writes=[pw], accum=(c > 0), tag=tg)
                        bk.op("ve", "tensor_add",
                              x1sb[:, b2 * 512:(b2 + 1) * 512], pw[:],
                              xnsb[:, b2 * 512:(b2 + 1) * 512],
                              reads=[pw, xnsb], writes=[x1sb], tag=tg)
                    bk.dma("sy", dd["x1"][ta:ta + 128, :], x1sb[:],
                           reads=[x1sb], writes=[(dd["x1"], ta // 128)],
                           tag=tg, grp="b_out")
                    negm, vs, v1c, rstd = cols3
                    _ln_tile(bk, None, EPS1, x1sb[:], None, xl2b,
                             (negm, sq3, vs, v1c, rstd), tg)
                    bk.dma("sy", dd["xln2_h"][ta:ta + 128, :], xl2b[:],
                           reads=[xl2b], writes=[(dd["xln2_h"], ta // 128)],
                           tag=tg, grp="b_out")
            bk.barrier()
        _end_stage(cx, "stB")


def stage_C(cx):
    bk, nc, dd, pers = cx.bk, cx.nc, cx.dram, cx.pers
    NB = T // 512
    NC1, NC2 = MLP // 128, 8
    with ExitStack() as st:
        sb = lambda n, sh, d: st.enter_context(nc.sbuf_tensor(n, sh, d))
        ps_ = lambda n, sh: st.enter_context(nc.psum_tensor(n, sh, F32))
        w1_sb = [sb(f"c_w1{c}", [128, MLP], BF16) for c in range(8)]
        w2_sb = [sb(f"c_w2{c}", [128, D], BF16) for c in range(32)]
        hT = [sb(f"c_hT{c}", [128, 512], BF16) for c in range(32)]
        xl2T = [sb(f"c_xl2T{c}", [128, 512], BF16) for c in range(8)]
        tsb = [sb(f"c_t{i}", [128, 512], F32) for i in range(2)]
        usb = [sb(f"c_u{i}", [128, 512], F32) for i in range(2)]
        csb = [sb(f"c_c{i}", [128, 512], F32) for i in range(2)]
        x1t = sb("c_x1t", [128, D], F32)
        g0t = sb("c_g0t", [128, D], F32)
        xint = sb("c_xint", [128, D], F32)
        pg = [ps_(f"c_pg{i}", [128, 512]) for i in range(2)]
        py = [ps_(f"c_py{i}", [128, 512]) for i in range(2)]
        for c in range(8):
            bk.dma("sy", w1_sb[c][:], dd["w1_h"][c * 128:(c + 1) * 128, :],
                   reads=[dd["w1_h"]], tag="Cw1", grp="c_w")
        for c in range(32):
            bk.dma("sy", w2_sb[c][:], dd["w2_h"][c * 128:(c + 1) * 128, :],
                   reads=[dd["w2_h"]], tag="Cw2", grp="c_w")
        for blk in range(NB):
            t0 = blk * 512
            for c in range(8):
                bk.dma("sy", xl2T[c][:],
                       dd["xln2_h"][t0:t0 + 512, c * 128:(c + 1) * 128],
                       transpose=True,
                       reads=[(dd["xln2_h"], t0 // 128 + j) for j in range(4)],
                       writes=[xl2T[c]], tag=f"C{blk}x", grp="c_x")
                bk.op("ve", "tensor_scalar", xl2T[c][:], xl2T[c][:],
                      pers["ln2_s"][:, c:c + 1], pers["ln2_b"][:, c:c + 1],
                      ALU.mult, ALU.add,
                      reads=[xl2T[c], pers["ln2_s"], pers["ln2_b"]],
                      writes=[xl2T[c]], tag=f"C{blk}x")
            for dc in range(NC1):
                tg = f"C{blk}h{dc}"
                pp = pg[dc % 2]
                for c in range(8):
                    bk.op("pe", "matmul", pp[:],
                          w1_sb[c][:, dc * 128:(dc + 1) * 128], xl2T[c][:],
                          start=(c == 0), stop=(c == 7),
                          reads=[w1_sb[c], xl2T[c]], writes=[pp],
                          accum=(c > 0), tag=tg)
                i = dc % 2
                b1c = pers["b1"][:, dc:dc + 1]
                bk.op("ve", "tensor_scalar", tsb[i][:], pp[:], b1c, None,
                      ALU.add, reads=[pp, pers["b1"]], writes=[tsb[i]], tag=tg)
                bk.op("se", "activation", usb[i][:], tsb[i][:], AF.Square,
                      reads=[tsb[i]], writes=[usb[i]], tag=tg)
                bk.op("ve", "tensor_scalar", usb[i][:], usb[i][:], GELU_A, 1.0,
                      ALU.mult, ALU.add, reads=[usb[i]], writes=[usb[i]], tag=tg)
                bk.op("ve", "tensor_tensor", csb[i][:], tsb[i][:], usb[i][:],
                      ALU.mult, reads=[tsb[i], usb[i]], writes=[csb[i]], tag=tg)
                bk.op("se", "activation", usb[i][:], csb[i][:], AF.Tanh,
                      scale=GELU_C, reads=[csb[i]], writes=[usb[i]], tag=tg)
                bk.op("ve", "scalar_tensor_tensor", hT[dc][:], usb[i][:], 1.0,
                      tsb[i][:], ALU.add, ALU.mult,
                      reads=[usb[i], tsb[i]], writes=[hT[dc]], tag=tg)
            for j in range(4):
                tt = t0 + j * 128
                ti = tt // 128
                tg = f"C{blk}y{j}"
                bk.dma("sy", x1t[:], dd["x1"][tt:tt + 128, :],
                       reads=[(dd["x1"], ti)], writes=[x1t], tag=tg, grp="c_in")
                bk.dma("sy", g0t[:], dd["g0"][tt:tt + 128, :],
                       reads=[(dd["g0"], ti)], writes=[g0t], tag=tg, grp="c_in")
                bk.dma("sy", xint[:], dd["x_in"][tt:tt + 128, :],
                       writes=[xint], tag=tg, grp="c_in")
                for b2 in range(2):
                    pp = py[b2]
                    ds_ = slice(b2 * 512, (b2 + 1) * 512)
                    if TEST_NO_B2:
                        for c in range(32):
                            bk.op("pe", "matmul", pp[:],
                                  hT[c][:, j * 128:(j + 1) * 128],
                                  w2_sb[c][:, ds_], start=(c == 0),
                                  stop=(c == 31),
                                  reads=[hT[c], w2_sb[c]], writes=[pp],
                                  accum=(c > 0), tag=tg)
                    else:
                        bk.op("pe", "matmul", pp[:], pers["onesF"][:],
                              pers["b2row2"][0:1, ds_], start=True, stop=False,
                              reads=[pers["onesF"], pers["b2row2"]],
                              writes=[pp], tag=tg)
                        for c in range(32):
                            bk.op("pe", "matmul", pp[:],
                                  hT[c][:, j * 128:(j + 1) * 128],
                                  w2_sb[c][:, ds_], start=False, stop=(c == 31),
                                  reads=[hT[c], w2_sb[c]], writes=[pp],
                                  accum=True, tag=tg)
                    bk.op("ve", "scalar_tensor_tensor", x1t[:, ds_], pp[:], 0.5,
                          x1t[:, ds_], ALU.mult, ALU.add,
                          reads=[pp, x1t], writes=[x1t], tag=tg)
                    bk.op("ve", "tensor_mul", g0t[:, ds_], x1t[:, ds_],
                          g0t[:, ds_], reads=[x1t, g0t], writes=[g0t], tag=tg)
                    bk.op("ve", "tensor_add", xint[:, ds_], g0t[:, ds_],
                          xint[:, ds_], reads=[g0t, xint], writes=[(xint, b2)],
                          tag=tg)
                bk.dma("sy", dd["x_res"][tt:tt + 128, :], xint[:],
                       reads=[(xint, 0), (xint, 1)],
                       writes=[(dd["x_res"], ti)], tag=tg, grp="c_out")
        _end_stage(cx, "stC")




def emit_fp4(bk, pers, x_ap, out_ap, bufs, N, tag):
    absx, p1, p2, c7, bm, sc = bufs
    NB = N // 16
    a3 = lambda t: t[:, 0:N].rearrange("p (b i) -> p b i", i=16)
    scb = bass.AP(sc, 0, [[sc.shape[1], 128], [1, NB], [0, 16]])
    bk.op("se", "activation", absx[:, 0:N], x_ap, AF.Abs,
          reads=[x_ap], writes=[absx], tag=tag)
    bk.op("ve", "reduce_max", bm[:, 0:NB].rearrange("p (b o) -> p b o", o=1),
          a3(absx), axis=AX.X, reads=[absx], writes=[bm], tag=tag)
    bk.op("ve", "tensor_scalar", sc[:, 0:NB], bm[:, 0:NB], 1.0 / 6.0, 1e-12,
          ALU.mult, ALU.max, reads=[bm], writes=[sc], tag=tag)
    bk.op("ve", "reciprocal", bm[:, 0:NB], sc[:, 0:NB],
          reads=[sc], writes=[bm], tag=tag)
    rsb = bass.AP(bm, 0, [[bm.shape[1], 128], [1, NB], [0, 16]])
    bk.op("ve", "tensor_tensor", a3(absx), a3(absx), rsb, ALU.mult,
          reads=[absx, bm], writes=[absx], tag=tag)
    bk.op("ve", "tensor_scalar", p1[:, 0:N], absx[:, 0:N], 0.25, None,
          ALU.is_gt, reads=[absx], writes=[p1], tag=tag)
    for t in (0.75, 1.25, 1.75):
        bk.op("ve", "scalar_tensor_tensor", p1[:, 0:N], absx[:, 0:N], t,
              p1[:, 0:N], ALU.is_gt, ALU.add, reads=[absx, p1], writes=[p1],
              tag=tag)
    bk.op("ve", "tensor_scalar", p2[:, 0:N], absx[:, 0:N], 2.5, None,
          ALU.is_gt, reads=[absx], writes=[p2], tag=tag)
    bk.op("ve", "scalar_tensor_tensor", p2[:, 0:N], absx[:, 0:N], 3.5,
          p2[:, 0:N], ALU.is_gt, ALU.add, reads=[absx, p2], writes=[p2],
          tag=tag)
    bk.op("ve", "tensor_scalar", c7[:, 0:N], absx[:, 0:N], 5.0, 2.0,
          ALU.is_gt, ALU.mult, reads=[absx], writes=[c7], tag=tag)
    bk.op("ve", "tensor_add", p2[:, 0:N], p2[:, 0:N], c7[:, 0:N],
          reads=[p2, c7], writes=[p2], tag=tag)
    bk.op("ve", "scalar_tensor_tensor", p1[:, 0:N], p1[:, 0:N], 0.5,
          p2[:, 0:N], ALU.mult, ALU.add, reads=[p1, p2], writes=[p1], tag=tag)
    bk.op("se", "sign", c7[:, 0:N], x_ap, reads=[x_ap], writes=[c7], tag=tag)
    bk.op("ve", "tensor_mul", p1[:, 0:N], p1[:, 0:N], c7[:, 0:N],
          reads=[p1, c7], writes=[p1], tag=tag)
    bk.op("ve", "tensor_tensor",
          out_ap.rearrange("p (b i) -> p b i", i=16), a3(p1),
          bass.AP(sc, 0, [[sc.shape[1], 128], [1, NB], [0, 16]]), ALU.mult,
          reads=[p1, sc], writes=[out_ap], tag=tag)


def stage_E(cx):
    bk, nc, dd, pers = cx.bk, cx.nc, cx.dram, cx.pers
    with ExitStack() as st:
        sb = lambda n, sh, d: st.enter_context(nc.sbuf_tensor(n, sh, d))
        NBUF = 2
        xt = [sb(f"e_x{i}", [128, D], F32) for i in range(NBUF)]
        xn = [sb(f"e_xn{i}", [128, D], F32) for i in range(NBUF)]
        xnb = [sb(f"e_xnb{i}", [128, D], BF16) for i in range(NBUF)]
        xqb = [sb(f"e_xqb{i}", [128, D], BF16) for i in range(NBUF)]
        sq = sb("e_sq", [128, D], F32)
        cols = [sb(f"e_c{j}", [128, 1], F32) for j in range(4)]
        qb = (sb("e_qa", [128, D], F32), sb("e_qp1", [128, D], F32),
              sb("e_qp2", [128, D], F32), sb("e_qc7", [128, D], F32),
              sb("e_qbm", [128, D // 16], F32), sb("e_qsc", [128, D // 16], F32))
        for t in range(NT):
            i = t % NBUF
            tg = f"E{t}"
            bk.dma("sy", xt[i][:], dd["x_res"][t * 128:(t + 1) * 128, :],
                   reads=[(dd["x_res"], t)], writes=[xt[i]], tag=tg,
                   grp="e_in")
            negm, vs, v1, rstd = cols
            _ln_tile(bk, None, EPS0, xt[i][:], xn[i], xnb[i],
                     (negm, sq, vs, v1, rstd), tg)
            bk.dma("sy", dd["xn1_h"][t * 128:(t + 1) * 128, :], xnb[i][:],
                   reads=[xnb[i]], writes=[(dd["xn1_h"], t)], tag=tg,
                   grp="e_out")
            emit_fp4(bk, pers, xn[i][:], xqb[i][:], qb, D, tg)
            bk.dma("sy", dd["xq_h"][t * 128:(t + 1) * 128, :], xqb[i][:],
                   reads=[xqb[i]], writes=[(dd["xq_h"], t)], tag=tg,
                   grp="e_out")
        _end_stage(cx, "stE")


def stage_G(cx):
    bk, nc, dd, pers = cx.bk, cx.nc, cx.dram, cx.pers
    TH = T // 2
    NK = TH // 128
    NB6 = (I1 + 511) // 512
    with ExitStack() as st:
        sb = lambda n, sh, d: st.enter_context(nc.sbuf_tensor(n, sh, d))
        ps_ = lambda n, sh: st.enter_context(nc.psum_tensor(n, sh, F32))
        with ExitStack() as stq:
            wt = [stq.enter_context(nc.sbuf_tensor(f"g_wt{i}", [128, I1], F32))
                  for i in range(2)]
            wq = [stq.enter_context(nc.sbuf_tensor(f"g_wq{i}", [128, I1], BF16))
                  for i in range(2)]
            qb = (stq.enter_context(nc.sbuf_tensor("g_qa", [128, I1], F32)),
                  stq.enter_context(nc.sbuf_tensor("g_qp1", [128, I1], F32)),
                  stq.enter_context(nc.sbuf_tensor("g_qp2", [128, I1], F32)),
                  stq.enter_context(nc.sbuf_tensor("g_qc7", [128, I1], F32)),
                  stq.enter_context(nc.sbuf_tensor("g_qbm", [128, I1 // 16], F32)),
                  stq.enter_context(nc.sbuf_tensor("g_qsc", [128, I1 // 16], F32)))
            jobs = []
            for w_, wq_, rows, cols_ in (("wg", "wgq_h", I1, D),
                                         ("wu", "wuq_h", I1, D),
                                         ("wd", "wdq_h", D, I1)):
                for r0 in range(0, rows, 128):
                    jobs.append((w_, wq_, r0, cols_))
            for ji, (w_, wq_, r0, cols_) in enumerate(jobs):
                i = ji % 2
                tg = f"Gq{w_}{r0}"
                bk.dma("sy", wt[i][:, 0:cols_], dd[w_][r0:r0 + 128, :],
                       reads=[(dd[w_], r0)], writes=[wt[i]], tag=tg,
                       grp="g_wq")
                emit_fp4(bk, pers, wt[i][:, 0:cols_], wq[i][:, 0:cols_],
                         qb, cols_, tg)
                bk.dma("sy", dd[wq_][r0:r0 + 128, :], wq[i][:, 0:cols_],
                       reads=[wq[i]], writes=[(dd[wq_], r0)], tag=tg,
                       grp="g_wqs")
            bk.barrier()
        wgT = [sb(f"g_wgT{c}", [128, I1], BF16) for c in range(8)]
        wuT = [sb(f"g_wuT{c}", [128, I1], BF16) for c in range(8)]
        wdT = [sb(f"g_wdT{c}", [128, D], BF16) for c in range(22)]
        for c in range(8):
            bk.dma("sy", wgT[c][:], dd["wgq_h"][:, c * 128:(c + 1) * 128],
                   transpose=True,
                   reads=[(dd["wgq_h"], r0) for r0 in range(0, I1, 128)],
                   writes=[wgT[c]], tag="GwT", grp="g_w")
            bk.dma("sy", wuT[c][:], dd["wuq_h"][:, c * 128:(c + 1) * 128],
                   transpose=True,
                   reads=[(dd["wuq_h"], r0) for r0 in range(0, I1, 128)],
                   writes=[wuT[c]], tag="GwT", grp="g_w")
        for c in range(22):
            bk.dma("sy", wdT[c][:], dd["wdq_h"][:, c * 128:(c + 1) * 128],
                   transpose=True,
                   reads=[(dd["wdq_h"], r0) for r0 in range(0, D, 128)],
                   writes=[wdT[c]], tag="GwT", grp="g_w")
        xqT = [sb(f"g_xqT{c}", [128, 512], BF16) for c in range(8)]
        hT = [[sb(f"g_hT{pp_}_{c}", [128, 512], BF16) for c in range(22)]
              for pp_ in range(2)]
        sg_1 = sb("g_sg", [128, 512], F32)
        sg = [sg_1, sg_1]
        hf_1 = sb("g_hf", [128, 512], F32)
        hf = [hf_1, hf_1]
        hq = [sb(f"g_hq{i}", [128, 512], BF16) for i in range(2)]
        g1t = sb("g_g1t", [128, D], F32)
        qb2_1 = (sb("g_q2a", [128, 512], F32), sb("g_q2p1", [128, 512], F32),
                 sg_1, sb("g_q2c7", [128, 512], F32),
                 sb("g_q2bm", [128, 32], F32), sb("g_q2sc", [128, 32], F32))
        qb2 = [qb2_1, qb2_1]
        pg = [ps_(f"g_pg{i}", [128, 512]) for i in range(2)]
        pu = [ps_(f"g_pu{i}", [128, 512]) for i in range(2)]
        py = [ps_(f"g_py{i}", [128, 512]) for i in range(2)]
        def g_mm1(hf_, q):
            t0 = hf_ * TH
            for kq in range(4):
                k = q * 4 + kq
                kk = slice(kq * 128, (kq + 1) * 128)
                tg = f"G{hf_}k{k}"
                for blkb in range(NB6):
                    bw = min(512, I1 - blkb * 512)
                    bs = slice(blkb * 512, blkb * 512 + bw)
                    for c in range(8):
                        bk.op("pe", "matmul", pg[blkb % 2][:, 0:bw], xqT[c][:, kk],
                              wgT[c][:, bs], start=(c == 0), stop=(c == 7),
                              reads=[xqT[c], wgT[c]], writes=[pg[blkb % 2]],
                              accum=(c > 0), tag=tg)
                    for c in range(8):
                        bk.op("pe", "matmul", pu[blkb % 2][:, 0:bw], xqT[c][:, kk],
                              wuT[c][:, bs], start=(c == 0), stop=(c == 7),
                              reads=[xqT[c], wuT[c]], writes=[pu[blkb % 2]],
                              accum=(c > 0), tag=tg)
                    bk.op("se", "activation", sg[blkb % 2][:, 0:bw],
                          pg[blkb % 2][:, 0:bw],
                          AF.Silu, reads=[pg[blkb % 2]], writes=[sg[blkb % 2]],
                          tag=tg)
                    bk.op("ve", "tensor_mul", hf[blkb % 2][:, 0:bw],
                          sg[blkb % 2][:, 0:bw],
                          pu[blkb % 2][:, 0:bw], reads=[sg[blkb % 2], pu[blkb % 2]],
                          writes=[hf[blkb % 2]], tag=tg)
                    emit_fp4(bk, pers, hf[blkb % 2][:, 0:bw], hq[blkb % 2][:, 0:bw],
                             qb2[blkb % 2], bw, tg)
                    bk.dma("sy", dd["hq_h"][t0 + k * 128:t0 + (k + 1) * 128,
                                            blkb * 512:blkb * 512 + bw],
                           hq[blkb % 2][:, 0:bw], reads=[hq[blkb % 2]],
                           writes=[(dd["hq_h"], (hf_, k, blkb))], tag=tg,
                           grp="g_hs")

        def g_hload(hf_, q):
            t0 = hf_ * TH
            q0 = t0 + q * 512
            for c in range(22):
                bk.dma("sy", hT[q % 2][c][:],
                       dd["hq_h"][q0:q0 + 512, c * 128:(c + 1) * 128],
                       transpose=True,
                       reads=[(dd["hq_h"], (hf_, q * 4 + j, bb))
                              for j in range(4) for bb in range(NB6)],
                       writes=[hT[q % 2][c]], tag=f"G{hf_}q{q}", grp="g_ht")

        def g_mm2(hf_, q):
            t0 = hf_ * TH
            for kq in range(4):
                k = q * 4 + kq
                ti = t0 // 128 + k
                tg = f"G{hf_}y{k}"
                for b2 in range(2):
                    pp = py[b2]
                    for c in range(22):
                        bk.op("pe", "matmul", pp[:],
                              hT[q % 2][c][:, kq * 128:(kq + 1) * 128],
                              wdT[c][:, b2 * 512:(b2 + 1) * 512],
                              start=(c == 0), stop=(c == 21),
                              reads=[hT[q % 2][c], wdT[c]], writes=[pp],
                              accum=(c > 0), tag=tg)
                    bk.op("se", "activation", g1t[:, b2 * 512:(b2 + 1) * 512],
                          pp[:], AF.Silu, reads=[pp], writes=[(g1t, b2)],
                          tag=tg)
                bk.dma("sy", dd["g1"][ti * 128:(ti + 1) * 128, :], g1t[:],
                       reads=[(g1t, 0), (g1t, 1)],
                       writes=[(dd["g1"], ti)], tag=tg, grp="g_gs")

        pendg = None
        for hf_ in range(2):
            t0 = hf_ * TH
            for q in range(NK // 4):
                qq0 = t0 + q * 512
                for c in range(8):
                    bk.dma("sy", xqT[c][:],
                           dd["xq_h"][qq0:qq0 + 512, c * 128:(c + 1) * 128],
                           transpose=True,
                           reads=[(dd["xq_h"], qq0 // 128 + j) for j in range(4)],
                           writes=[xqT[c]], tag=f"G{hf_}x{q}", grp="g_x")
                g_mm1(hf_, q)
                g_hload(hf_, q)
                if pendg is not None:
                    g_mm2(*pendg)
                pendg = (hf_, q)
        if pendg is not None:
            g_mm2(*pendg)
        _end_stage(cx, "stG")


def stage_F(cx):
    bk, nc, dd, pers = cx.bk, cx.nc, cx.dram, cx.pers
    TH = T // 2
    NK = TH // 128
    R2PI = 1.0 / (2.0 * math.pi)
    RND_C = 12582912.0
    with ExitStack() as st:
        sb = lambda n, sh, d: st.enter_context(nc.sbuf_tensor(n, sh, d))
        ps_ = lambda n, sh: st.enter_context(nc.psum_tensor(n, sh, F32))
        cosT = sb("f_cos", [128, NK * 32], F32)
        sinT = sb("f_sin", [128, NK * 32], F32)
        with ExitStack() as s0:
            ivrow = s0.enter_context(nc.sbuf_tensor("f_ivr", [1, 32], F32))
            ivrep = s0.enter_context(nc.sbuf_tensor("f_ivrep", [128, 32], F32))
            ivb = s0.enter_context(nc.sbuf_tensor("f_ivb", [1, 32], BF16))
            poscol = s0.enter_context(nc.sbuf_tensor("f_pos", [128, 1], F32))
            ang = s0.enter_context(nc.sbuf_tensor("f_ang", [128, 32], F32))
            tmpa = s0.enter_context(nc.sbuf_tensor("f_tmpa", [128, 32], F32))
            hpi = s0.enter_context(nc.sbuf_tensor("f_hpi", [128, 1], F32))
            psr = s0.enter_context(nc.psum_tensor("f_psr", [128, 32], F32))
            bk.op("po", "iota", ivrow[:], [[1, 32]], base=0,
                  channel_multiplier=0, allow_small_or_imprecise_dtypes=True,
                  writes=[ivrow], tag="Frt")
            bk.op("po", "memset", hpi[:], float(math.pi / 2.0), writes=[hpi],
                  tag="Frt")
            bk.op("se", "activation", ivrow[:], ivrow[:], AF.Exp,
                  scale=float(-math.log(10000.0) / 32.0),
                  reads=[ivrow], writes=[ivrow], tag="Frt")
            bk.op("ve", "tensor_copy", ivb[:], ivrow[:], reads=[ivrow],
                  writes=[ivb], tag="Frt")
            bk.op("pe", "matmul", psr[:], pers["onesF"][:].rearrange(
                  "a b -> a b"), ivrow[:], start=True, stop=True,
                  reads=[pers["onesF"], ivrow], writes=[psr], tag="Frt")
            bk.op("ve", "tensor_copy", ivrep[:], psr[:], reads=[psr],
                  writes=[ivrep], tag="Frt")
            for t in range(NK):
                tg = f"Frt{t}"
                cs = slice(t * 32, (t + 1) * 32)
                bk.op("po", "iota", poscol[:], [[0, 1]], base=t * 128,
                      channel_multiplier=1,
                      allow_small_or_imprecise_dtypes=True,
                      writes=[poscol], tag=tg)
                bk.op("ve", "tensor_single_scalar", ang[:], ivrep[:],
                      poscol[:, 0:1], ALU.mult, reads=[ivrep, poscol],
                      writes=[ang], tag=tg)
                bk.op("ve", "tensor_scalar", tmpa[:], ang[:], R2PI, RND_C,
                      ALU.mult, ALU.add, reads=[ang], writes=[tmpa], tag=tg)
                bk.op("ve", "tensor_scalar", tmpa[:], tmpa[:], -RND_C, None,
                      ALU.add, reads=[tmpa], writes=[tmpa], tag=tg)
                bk.op("ve", "scalar_tensor_tensor", ang[:], tmpa[:],
                      float(-2.0 * math.pi), ang[:], ALU.mult, ALU.add,
                      reads=[tmpa, ang], writes=[ang], tag=tg)
                bk.op("ve", "tensor_scalar", ang[:], ang[:],
                      float(math.pi), float(-math.pi), ALU.min, ALU.max,
                      reads=[ang], writes=[ang], tag=tg)
                bk.op("se", "activation", sinT[:, cs], ang[:], AF.Sin,
                      reads=[ang], writes=[(sinT, t)], tag=tg)
                bk.op("se", "activation", tmpa[:], ang[:], AF.Abs,
                      reads=[ang], writes=[tmpa], tag=tg)
                bk.op("se", "activation", cosT[:, cs], tmpa[:], AF.Sin,
                      scale=-1.0, bias=hpi[:, 0:1],
                      reads=[tmpa, hpi], writes=[(cosT, t)], tag=tg)
        bk.barrier()
        negC = sb("f_negC", [128, 1], F32)
        bk.op("po", "memset", negC[:], -20.0, writes=[negC], tag="FnegC")
        wqkvT = [sb(f"f_wqkvT{c}", [128, 3 * D], BF16) for c in range(8)]
        for c in range(8):
            bk.dma("sy", wqkvT[c][:], dd["wqkv_h"][:, c * 128:(c + 1) * 128],
                   transpose=True, reads=[dd["wqkv_h"]], writes=[wqkvT[c]],
                   tag="FwT", grp="f_w")
        for hf in range(2):
            t0 = hf * TH
            with ExitStack() as s1:
                sb1 = lambda n, sh, d: s1.enter_context(
                    nc.sbuf_tensor(f"{n}_{hf}", sh, d))
                xnT = [sb1(f"f_xnT{c}", [128, TH], BF16) for c in range(8)]
                v_sb = sb1("f_v", [128, H * 65], BF16)
                ro = sb1("f_ro", [128, D], F32)
                tmp = sb1("f_tmp", [128, 512], F32)
                tmp2 = sb1("f_tmp2", [128, D], F32)
                ms = sb1("f_ms", [128, H], F32)
                qro = sb1("f_qro", [128, D], BF16)
                pq = [s1.enter_context(nc.psum_tensor(f"f_pq{i}_{hf}",
                      [128, 512], F32)) for i in range(6)]
                for c in range(8):
                    bk.dma("sy", xnT[c][:],
                           dd["xn1_h"][t0:t0 + TH, c * 128:(c + 1) * 128],
                           transpose=True,
                           reads=[(dd["xn1_h"], t0 // 128 + j)
                                  for j in range(NK)],
                           writes=[xnT[c]], tag=f"F1x{hf}", grp="f_x")
                for k in range(NK):
                    kk = slice(k * 128, (k + 1) * 128)
                    tg = f"F1{hf}k{k}"
                    for part in range(6):
                        pp = pq[part]
                        for c in range(8):
                            bk.op("pe", "matmul", pp[:], xnT[c][:, kk],
                                  wqkvT[c][:, part * 512:(part + 1) * 512],
                                  start=(c == 0), stop=(c == 7),
                                  reads=[xnT[c], wqkvT[c]], writes=[pp],
                                  accum=(c > 0), tag=tg)
                    if k == 0:
                        bk.op("ve", "memset",
                              bass.AP(v_sb, 64, [[H * 65, 128], [65, H], [1, 1]]),
                              1.0, writes=[(v_sb, "ones")], tag=tg)
                    for b2 in range(2):
                        bk.op("ve", "tensor_copy",
                              bass.AP(v_sb, b2 * 8 * 65,
                                      [[H * 65, 128], [65, 8], [1, 64]]),
                              pq[4 + b2][:].rearrange("p (h d) -> p h d", d=64),
                              reads=[pq[4 + b2], (v_sb, "ones")],
                              writes=[(v_sb, b2)], tag=tg)
                    bk.dma("sy", dd["vro_h"][(t0 // 128 + k) * 128:
                                             (t0 // 128 + k + 1) * 128, :],
                           v_sb[:], reads=[(v_sb, 0), (v_sb, 1), (v_sb, "ones")],
                           writes=[(dd["vro_h"], t0 // 128 + k)], tag=tg,
                           grp="f_qk")
                    for (pa, pb, wrep, dst) in ((0, 1, "qnw", "qro_h"),
                                                (2, 3, "knw", "kro_h")):
                        cs = slice(k * 32, (k + 1) * 32)
                        c3 = bass.AP(cosT, k * 32,
                                     [[cosT.shape[1], 128], [0, 8], [1, 32]])
                        s3 = bass.AP(sinT, k * 32,
                                     [[sinT.shape[1], 128], [0, 8], [1, 32]])
                        for half_i, pp in ((0, pq[pa]), (1, pq[pb])):
                            p3a = pp[:].rearrange("p (h d) -> p h d", d=64)
                            q1 = pp[:].rearrange(
                                "p (h e d) -> p (h e) d", e=2, d=32
                            )
                            ro3 = ro[:, half_i * 512:(half_i + 1) * 512]
                            rr = lambda t_, off: bass.AP(
                                t_.tensor, t_.offset + off,
                                [[t_.ap[0][0], 128], [64, 8], [1, 32]])
                            q1v = bass.AP(pp, 0, [[512, 128], [64, 8], [1, 32]])
                            q2v = bass.AP(pp, 32, [[512, 128], [64, 8], [1, 32]])
                            o1 = bass.AP(ro, half_i * 512,
                                         [[ro.shape[1], 128], [64, 8], [1, 32]])
                            o2 = bass.AP(ro, half_i * 512 + 32,
                                         [[ro.shape[1], 128], [64, 8], [1, 32]])
                            t1 = bass.AP(tmp, 0, [[512, 128], [64, 8], [1, 32]])
                            tg2 = tg + f"r{pa}{half_i}"
                            bk.op("ve", "tensor_tensor", o1, q1v, c3, ALU.mult,
                                  reads=[pp, cosT], writes=[ro], tag=tg2)
                            bk.op("ve", "tensor_tensor", t1, q2v, s3, ALU.mult,
                                  reads=[pp, sinT], writes=[tmp], tag=tg2)
                            bk.op("ve", "tensor_sub", o1, o1, t1,
                                  reads=[ro, tmp], writes=[ro], tag=tg2)
                            bk.op("ve", "tensor_tensor", o2, q2v, c3, ALU.mult,
                                  reads=[pp, cosT], writes=[ro], tag=tg2)
                            bk.op("ve", "tensor_tensor", t1, q1v, s3, ALU.mult,
                                  reads=[pp, sinT], writes=[tmp], tag=tg2)
                            bk.op("ve", "tensor_add", o2, o2, t1,
                                  reads=[ro, tmp], writes=[ro], tag=tg2)
                        tg3 = tg + f"n{pa}"
                        bk.op("ve", "tensor_mul", tmp2[:], ro[:], ro[:],
                              reads=[ro], writes=[tmp2], tag=tg3)
                        bk.op("ve", "reduce_sum",
                              ms[:].rearrange("p (h o) -> p h o", o=1),
                              tmp2[:].rearrange("p (h d) -> p h d", d=64),
                              axis=AX.X, reads=[tmp2], writes=[ms], tag=tg3)
                        bk.op("ve", "tensor_scalar", ms[:], ms[:], 1.0 / 64,
                              RMS_EPS, ALU.mult, ALU.add, reads=[ms],
                              writes=[ms], tag=tg3)
                        bk.op("se", "activation", ms[:], ms[:], AF.Ln,
                              reads=[ms], writes=[ms], tag=tg3)
                        bk.op("se", "activation", ms[:], ms[:], AF.Exp,
                              scale=-0.5, reads=[ms], writes=[ms], tag=tg3)
                        msb = bass.AP(ms, 0, [[ms.shape[1], 128], [1, H], [0, 64]])
                        bk.op("ve", "tensor_tensor",
                              tmp2[:].rearrange("p (h d) -> p h d", d=64),
                              ro[:].rearrange("p (h d) -> p h d", d=64),
                              msb, ALU.mult, reads=[ro, ms], writes=[tmp2],
                              tag=tg3)
                        wr = pers[wrep]
                        wrb = bass.AP(wr, 0, [[wr.shape[1], 128], [0, H], [1, 64]])
                        bk.op("ve", "tensor_tensor",
                              qro[:].rearrange("p (h d) -> p h d", d=64),
                              tmp2[:].rearrange("p (h d) -> p h d", d=64),
                              wrb, ALU.mult, reads=[tmp2, wr], writes=[qro],
                              tag=tg3)
                        ti = t0 // 128 + k
                        bk.dma("sy", dd[dst][ti * 128:(ti + 1) * 128, :],
                               qro[:], reads=[qro], writes=[(dd[dst], ti)],
                               tag=tg3, grp="f_qk")
                bk.barrier()
            with ExitStack() as s2:
                sb2 = lambda n, sh, d: s2.enter_context(
                    nc.sbuf_tensor(f"{n}_{hf}", sh, d))
                v_sb2 = sb2("f2_v", [128, NK * H * 65], BF16)
                qT = sb2("f2_qT", [128, TH], BF16)
                kT = sb2("f2_kT", [128, TH], BF16)
                P_sb = [sb2(f"f2_P{i}", [128, 512], BF16) for i in range(2)]
                rl = sb2("f2_rl", [128, 1], F32)
                ao = sb2("f2_ao", [128, NK * D], BF16)
                psS = [s2.enter_context(nc.psum_tensor(f"f2_psS{i}_{hf}",
                       [128, 512], F32)) for i in range(2)]
                psO = [s2.enter_context(nc.psum_tensor(f"f2_psO{i}_{hf}",
                       [128, 65], F32)) for i in range(4)]
                VW = H * 65
                for k in range(NK):
                    bk.dma("sy", v_sb2[:, k * VW:(k + 1) * VW],
                           dd["vro_h"][(t0 // 128 + k) * 128:
                                       (t0 // 128 + k + 1) * 128, :],
                           reads=[(dd["vro_h"], t0 // 128 + k)],
                           writes=[(v_sb2, k)], tag=f"F2v{hf}", grp="f_qt")
                for hp in range(8):
                    tgh = f"F2{hf}p{hp}"
                    bk.dma("sy", qT[:],
                           dd["qro_h"][t0:t0 + TH, hp * 128:(hp + 1) * 128],
                           transpose=True,
                           reads=[(dd["qro_h"], t0 // 128 + j)
                                  for j in range(NK)],
                           writes=[qT], tag=tgh, grp="f_qt")
                    bk.dma("sy", kT[:],
                           dd["kro_h"][t0:t0 + TH, hp * 128:(hp + 1) * 128],
                           transpose=True,
                           reads=[(dd["kro_h"], t0 // 128 + j)
                                  for j in range(NK)],
                           writes=[kT], tag=tgh, grp="f_qt")
                    for hsub in range(2):
                        h = hp * 2 + hsub
                        o = hsub * 64
                        os_ = slice(o, o + 64)
                        for Q in range(4):
                            qs = slice(Q * 512, (Q + 1) * 512)
                            NJ = 4 * Q + 4

                            def emit_S(j, tg):
                                js = slice(j * 128, (j + 1) * 128)
                                bk.op("pe", "matmul", psS[j % 2][:], kT[os_, js],
                                      qT[os_, qs], start=True, stop=True,
                                      reads=[qT, kT], writes=[psS[j % 2]], tag=tg)
                                bk.op("se", "activation", P_sb[j % 2][:],
                                      psS[j % 2][:], AF.Exp, scale=0.125,
                                      bias=negC[:, 0:1],
                                      reads=[psS[j % 2], negC],
                                      writes=[P_sb[j % 2]], tag=tg)
                                if j >= 4 * Q:
                                    qt_d = j - 4 * Q
                                    bk.op("ve", "tensor_tensor",
                                          P_sb[j % 2][:, qt_d * 128:(qt_d + 1) * 128],
                                          P_sb[j % 2][:, qt_d * 128:(qt_d + 1) * 128],
                                          pers["mask_incl"][:], ALU.mult,
                                          reads=[P_sb[j % 2], pers["mask_incl"]],
                                          writes=[P_sb[j % 2]], tag=tg)

                            def emit_O(j, tg):
                                for qt_ in range(4):
                                    gq = 4 * Q + qt_
                                    if gq < j:
                                        continue
                                    pO = psO[qt_]
                                    first = (j == 0)
                                    bk.op("pe", "matmul", pO[:, 0:65],
                                          P_sb[j % 2][:, qt_ * 128:(qt_ + 1) * 128],
                                          v_sb2[:, j * VW + h * 65:
                                                j * VW + h * 65 + 65],
                                          start=first, stop=(j == gq),
                                          reads=[P_sb[j % 2], (v_sb2, j)],
                                          writes=[pO], accum=not first, tag=tg)

                            for j in range(NJ):
                                tg = f"{tgh}h{hsub}Q{Q}j{j}"
                                emit_S(j, tg)
                                if j > 0:
                                    emit_O(j - 1, tg)
                            emit_O(NJ - 1, f"{tgh}h{hsub}Q{Q}jz")
                            for qt_ in range(4):
                                tg = f"{tgh}h{hsub}Q{Q}e{qt_}"
                                pO = psO[qt_]
                                bk.op("ve", "reciprocal", rl[:],
                                      pO[:, 64:65], reads=[pO], writes=[rl],
                                      tag=tg)
                                gq = 4 * Q + qt_
                                bk.op("ve", "tensor_single_scalar",
                                      ao[:, gq * D + h * 64:
                                         gq * D + h * 64 + 64],
                                      pO[:, 0:64], rl[:, 0:1], ALU.mult,
                                      reads=[pO, rl],
                                      writes=[(ao, (gq, h))], tag=tg)
                for k in range(NK):
                    ti = t0 // 128 + k
                    bk.dma("sy", dd["ao_h"][ti * 128:(ti + 1) * 128, :],
                           ao[:, k * D:(k + 1) * D],
                           reads=[(ao, (k, hh)) for hh in range(H)],
                           writes=[(dd["ao_h"], ti)], tag=f"F2o{hf}",
                           grp="f_ao")
                bk.barrier()
        with ExitStack() as s3:
            sb3 = lambda n, sh, d: s3.enter_context(nc.sbuf_tensor(n, sh, d))
            woT = [sb3(f"f3_woT{c}", [128, D], BF16) for c in range(8)]
            for c in range(8):
                bk.dma("sy", woT[c][:], dd["wout_h"][:, c * 128:(c + 1) * 128],
                       transpose=True, reads=[dd["wout_h"]], writes=[woT[c]],
                       tag="F3w", grp="f_w")
            aoT = [sb3(f"f3_aoT{c}", [128, 512], BF16) for c in range(8)]
            xrt = sb3("f3_xr", [128, D], F32)
            g1t = sb3("f3_g1", [128, D], F32)
            ot = sb3("f3_o", [128, D], F32)
            pw = [s3.enter_context(nc.psum_tensor(f"f3_pw{i}", [128, 512],
                                                  F32)) for i in range(2)]
            for q in range(T // 512):
                q0 = q * 512
                for c in range(8):
                    bk.dma("sy", aoT[c][:],
                           dd["ao_h"][q0:q0 + 512, c * 128:(c + 1) * 128],
                           transpose=True,
                           reads=[(dd["ao_h"], q0 // 128 + j)
                                  for j in range(4)],
                           writes=[aoT[c]], tag=f"F3a{q}", grp="f_at")
                for kq in range(4):
                    ti = q * 4 + kq
                    tg = f"F3t{ti}"
                    bk.dma("sy", xrt[:], dd["x_res"][ti * 128:(ti + 1) * 128, :],
                           reads=[(dd["x_res"], ti)], writes=[xrt], tag=tg,
                           grp="f_in")
                    bk.dma("sy", g1t[:], dd["g1"][ti * 128:(ti + 1) * 128, :],
                           reads=[(dd["g1"], ti)], writes=[g1t], tag=tg,
                           grp="f_in")
                    for b2 in range(2):
                        pp = pw[b2]
                        ds_ = slice(b2 * 512, (b2 + 1) * 512)
                        for c in range(8):
                            bk.op("pe", "matmul", pp[:],
                                  aoT[c][:, kq * 128:(kq + 1) * 128],
                                  woT[c][:, ds_], start=(c == 0), stop=(c == 7),
                                  reads=[aoT[c], woT[c]], writes=[pp],
                                  accum=(c > 0), tag=tg)
                        bk.op("ve", "tensor_mul", g1t[:, ds_], pp[:],
                              g1t[:, ds_], reads=[pp, g1t], writes=[g1t],
                              tag=tg)
                        bk.op("ve", "tensor_add", ot[:, ds_], g1t[:, ds_],
                              xrt[:, ds_], reads=[g1t, xrt], writes=[(ot, b2)],
                              tag=tg)
                    bk.dma("sy", dd["y_out"][ti * 128:(ti + 1) * 128, :],
                           ot[:], reads=[(ot, 0), (ot, 1)],
                           writes=[(dd["y_out"], ti)], tag=tg, grp="f_y")
        _end_stage(cx, "stF")



def _prep_inputs(kw):
    x = np.ascontiguousarray(np.asarray(kw["L0_attn_inputs"], dtype=np.float32))
    full = {
        "ln1_s": kw["L0_attn_ln1_scale"], "ln1_b": kw["L0_attn_ln1_bias"],
        "wq": np.asarray(kw["L0_attn_wq"]).reshape(D, D),
        "wk": np.asarray(kw["L0_attn_wk"]).reshape(D, D),
        "wv": np.asarray(kw["L0_attn_wv"]).reshape(D, D),
        "wo": np.asarray(kw["L0_attn_wo"]).reshape(D, D),
        "ln2_s": kw["L0_attn_ln2_scale"], "ln2_b": kw["L0_attn_ln2_bias"],
        "w1": kw["L0_attn_w1"], "b1": kw["L0_attn_b1"],
        "w2": kw["L0_attn_w2"], "b2": kw["L0_attn_b2"],
        "gate_w": kw["L0_ffn_gate_w"], "gate_b": kw["L0_ffn_gate_b"],
        "w13": kw["L0_ffn_w13"], "w2m": kw["L0_ffn_w2"],
        "w13s": kw["L0_ffn_w13_s"], "w2s": kw["L0_ffn_w2_s"],
        "wqkv": kw["L1_attn_w_qkv"], "wout": kw["L1_attn_w_out"],
        "qnw": kw["L1_attn_q_norm_w"], "knw": kw["L1_attn_k_norm_w"],
        "wg": kw["L1_ffn_w_gate"], "wu": kw["L1_ffn_w_up"],
        "wd": kw["L1_ffn_w_down"],
    }
    ids = np.ascontiguousarray(np.asarray(kw["L0_ffn_input_ids"], dtype=np.int32))
    full = {k: np.ascontiguousarray(np.asarray(v, dtype=np.float32))
            for k, v in full.items()}
    full["input_ids"] = ids
    in_maps = []
    for c in range(N_CORES):
        m = dict(full)
        m["x_in"] = np.ascontiguousarray(
            x[c * BC:(c + 1) * BC].reshape(T, D))
        in_maps.append(m)
    return in_maps


def kernel(**kw):
    in_maps = _prep_inputs(kw)
    nc = build_program()
    res = run_bass_kernel_spmd(nc, in_maps, core_ids=list(range(N_CORES)))
    out = np.concatenate(
        [r["y_out"].reshape(BC, S, D) for r in res.results], axis=0)
    return out.astype(np.float32)


# version 08 (final)
# speedup vs baseline: 1.0470x; 1.0158x over previous
import math
from contextlib import ExitStack

import numpy as np

import concourse.bass as bass
import concourse.mybir as mybir
from concourse.bass_utils import run_bass_kernel_spmd

F32 = mybir.dt.float32
BF16 = mybir.dt.bfloat16
AF = mybir.ActivationFunctionType
ALU = mybir.AluOpType
AX = mybir.AxisListType

N_CORES = 8
B, S, D, H, DH, MLP = 16, 2048, 1024, 16, 64, 4096
BC = B // N_CORES
T = BC * S
NT = T // 128
NE = 16
IM = 512
I1 = 2816
EPS0 = 1e-5
EPS1 = 1e-6
KEPS = 1e-3
RMS_EPS = 1e-6
FP4_MIDS = [0.25, 0.75, 1.25, 1.75, 2.5, 3.5, 5.0]
FP4_STEP = [0.5, 0.5, 0.5, 0.5, 1.0, 1.0, 2.0]
GELU_C = 0.7978845608028654
GELU_A = 0.044715
TEST_NO_B2 = False
TEST_NO_GELU = False



_ENGINES = {
    "pe": "tensor",
    "ve": "vector",
    "se": "scalar",
    "po": "gpsimd",
    "sy": "sync",
}


class _Op:
    __slots__ = ("fn", "args", "kwargs", "kind", "marked", "waits", "sub",
                 "emitted", "tag", "grp", "mval", "grp_sem")

    def __init__(self, fn, args, kwargs, kind, sub, tag=""):
        self.fn = fn
        self.args = args
        self.kwargs = kwargs
        self.kind = kind
        self.marked = kind == "d"
        self.waits = []
        self.sub = sub
        self.emitted = False
        self.tag = tag


class Tok:
    __slots__ = ("stream", "kind", "sub", "op", "clock", "grp")

    def __init__(self, stream, kind, sub, op, clock, grp=None):
        self.stream = stream
        self.kind = kind
        self.sub = sub
        self.op = op
        self.clock = clock
        self.grp = grp


class _Stream:
    def __init__(self, name):
        self.name = name
        self.ops = []
        self.nsub = {"c": 0, "d": 0}
        self.clock = {}
        self.last_tok = None
        self.last_by_kind = {}
        self.marked_prefix = {"c": [0], "d": [0]}
        self.emitted_sub = {"c": 0, "d": 0}


def _key(x):
    if isinstance(x, tuple):
        base, sub = x
        return (_key(base)[0], sub)
    if isinstance(x, bass.AP):
        return (x.tensor.name, None)
    return (x.name, None)


class BK:

    def __init__(self, nc, ctx):
        self.nc = nc
        self.ctx = ctx
        self.streams = {n: _Stream(n) for n in _ENGINES}
        self.sems = {}
        for n in _ENGINES:
            self.sems[(n, "c")] = ctx.enter_context(nc.semaphore(f"p_{n}"))
        self.grp_sem = {}
        self.grp_count = {}
        self.grp_ck = {}
        self.grp_queue = {}
        self.last_w = {}
        self.readers = {}
        self.psum_acc = {}

    SEM_ROLL = 30000

    def _grp(self, grp, inc):
        if grp not in self.grp_sem:
            self.grp_sem[grp] = self.ctx.enter_context(
                self.nc.semaphore(f"g_{grp}"))
            self.grp_count[grp] = 0
            self.grp_raw = getattr(self, "grp_raw", {})
            self.grp_gen = getattr(self, "grp_gen", {})
            self.grp_raw[grp] = 0
            self.grp_gen[grp] = 0
            self.grp_inc = getattr(self, "grp_inc", {})
            self.grp_inc[grp] = inc
        return self.grp_sem[grp]

    def _make_ck(self, grp):
        qn = self.grp_queue[grp]
        qst = self.streams[qn]
        cur = self.grp_count[grp]
        wo = _Op(None, (), {}, "w", 0, f"ck_{grp}")
        wo.grp = None
        wo.waits.append(("S", self.grp_sem[grp],
                         self.grp_inc[grp] * self.grp_raw[grp]))
        qst.ops.append(wo)
        cktok = self.op(qn, "nop", kind="c", mark=True, tag=f"ck_{grp}{cur}")
        cktok.clock[("D", grp)] = cur
        qst.clock[("D", grp)] = cur
        self.grp_ck[grp] = (cur, cktok)
        return self.grp_ck[grp]

    def _maybe_roll(self, grp):
        inc = self.grp_inc[grp]
        if (self.grp_raw[grp] + 1) * inc > self.SEM_ROLL:
            self._make_ck(grp)
            self.grp_gen[grp] += 1
            self.grp_sem[grp] = self.ctx.enter_context(
                self.nc.semaphore(f"g_{grp}_{self.grp_gen[grp]}"))
            self.grp_raw[grp] = 0

    def _dep(self, op, st, tok):
        if tok is None or tok.op is op:
            return
        cl = st.clock
        if tok.grp is not None:
            grp = tok.grp
            if cl.get(("D", grp), 0) >= tok.sub:
                return
            ck = self.grp_ck.get(grp)
            if ck is None or ck[0] < tok.sub:
                ck = self._make_ck(grp)
            self._dep(op, st, ck[1])
            cl[("D", tok.grp)] = max(cl.get(("D", tok.grp), 0), ck[0])
            return
        k = (tok.stream, tok.kind)
        if cl.get(k, 0) >= tok.sub:
            return
        if tok.op.emitted and not tok.op.marked:
            raise RuntimeError(
                f"late dependency on already-emitted unmarked op "
                f"{tok.stream}/{tok.kind}/{tok.sub} {tok.op.fn}")
        tok.op.marked = True
        op.waits.append(tok)
        for kk, vv in tok.clock.items():
            if cl.get(kk, 0) < vv:
                cl[kk] = vv
        cl[k] = max(cl.get(k, 0), tok.sub)

    def op(self, stream, fn, *args, reads=(), writes=(), kind="c",
           accum=False, mark=False, tag="", grp=None, **kwargs):
        st = self.streams[stream]
        st.nsub[kind] += 1
        if kind == "d":
            grp = grp or stream
            self._grp(grp, 16)
            self.grp_queue[grp] = stream
            self._maybe_roll(grp)
        elif stream == "po" and fn != "nop":
            grp = grp or "poQ"
            self._grp(grp, 1)
            self.grp_queue[grp] = stream
            self._maybe_roll(grp)
        else:
            grp = None
        o = _Op(fn, args, kwargs, kind, st.nsub[kind], tag)
        o.grp = grp
        if mark:
            o.marked = True
        psum_names = set()
        for r in list(reads) + list(writes):
            if isinstance(r, bass.AP) and isinstance(
                    r.tensor, bass.PSumTensorHandle):
                psum_names.add(r.tensor.name)
            elif isinstance(r, tuple) and isinstance(
                    r[0], bass.PSumTensorHandle):
                psum_names.add(r[0].name)
            elif isinstance(r, bass.PSumTensorHandle):
                psum_names.add(r.name)
        for r in reads:
            name, sub = _key(r)
            d = self.last_w.get(name)
            if d:
                if sub is None:
                    for tok in d.values():
                        self._dep(o, st, tok)
                else:
                    for s2 in (sub, "*"):
                        if s2 in d:
                            self._dep(o, st, d[s2])
        for w in writes:
            name, sub = _key(w)
            d = self.last_w.get(name)
            if d and not accum:
                if sub is None:
                    for tok in d.values():
                        self._dep(o, st, tok)
                else:
                    for s2 in (sub, "*"):
                        if s2 in d:
                            self._dep(o, st, d[s2])
            for rsub, tok in self.readers.get(name, []):
                if accum:
                    continue
                if sub is None or rsub is None or rsub == sub:
                    self._dep(o, st, tok)
        if psum_names and stream != "pe":
            for name in psum_names:
                for (astream, tok2) in self.psum_acc.get(name, []):
                    if astream != stream:
                        self._dep(o, st, tok2)
        tok = Tok(stream, kind, o.sub, o, dict(st.clock), grp)
        if grp is not None:
            self.grp_count[grp] += 1
            self.grp_raw[grp] += 1
            tok.sub = self.grp_count[grp]
            o.grp_sem = self.grp_sem[grp]
        else:
            tok.clock[(stream, kind)] = o.sub
        for r in reads:
            name, sub = _key(r)
            self.readers.setdefault(name, []).append((sub, tok))
        for w in writes:
            name, sub = _key(w)
            sub = "*" if sub is None else sub
            d = self.last_w.setdefault(name, {})
            if sub == "*":
                d.clear()
            d[sub] = tok
            if not accum:
                self.readers[name] = [
                    (rs, rt) for (rs, rt) in self.readers.get(name, [])
                    if not (sub == "*" or rs is None or rs == sub)]
        if psum_names and stream != "pe":
            for name in psum_names:
                lst = [e for e in self.psum_acc.get(name, [])
                       if e[0] == stream]
                lst.append((stream, tok))
                self.psum_acc[name] = lst
        st.ops.append(o)
        st.last_tok = tok
        st.last_by_kind[kind] = tok
        return tok

    def dma(self, queue, out, in_, reads=(), writes=(), tag="", grp=None, **kw):
        return self.op(queue, "dma_start", out, in_, kind="d",
                       reads=list(reads) + [in_], writes=list(writes) + [out],
                       tag=tag, grp=grp, **kw)

    def barrier(self):
        for grp in list(self.grp_sem):
            ck = self.grp_ck.get(grp)
            if ck is None or ck[0] < self.grp_count[grp]:
                self._make_ck(grp)
        toks = [t for st in self.streams.values()
                for t in st.last_by_kind.values()]
        for n, st in self.streams.items():
            o = _Op(None, (), {}, "w", 0, "barrier")
            o.grp = None
            for t in toks:
                self._dep(o, st, t)
            if o.waits:
                st.ops.append(o)

    def flush(self, blk):
        for st in self.streams.values():
            base = st.emitted_sub.get("mc", 0)
            for o in st.ops:
                if (o.kind == "c" and o.marked and o.grp is None):
                    base += 1
                o.mval = base
            st.emitted_sub["mc"] = base
            assert base < self.SEM_ROLL, f"compute sem overflow on {st.name}" 
        for name, st in self.streams.items():
            if not st.ops:
                continue
            ops = st.ops
            sems = self.sems

            def emit(eng, ops=ops, st=st, sems=sems, self=self):
                def wsem(t):
                    if isinstance(t, tuple):
                        return t[1], t[2]
                    return sems[(t.stream, "c")], t.op.mval
                for o in ops:
                    attach = None
                    if o.kind == "w":
                        for t in o.waits:
                            eng.wait_ge(*wsem(t))
                        o.emitted = True
                        continue
                    if o.waits:
                        attach = o.waits[0]
                        for t in o.waits[1:]:
                            eng.wait_ge(*wsem(t))
                    inst = getattr(eng, o.fn)(*o.args, **o.kwargs)
                    if attach is not None:
                        inst._wait_ge(*wsem(attach))
                    if o.grp is not None:
                        inst.then_inc(o.grp_sem, self.grp_inc[o.grp])
                    elif o.marked:
                        inst.then_inc(sems[(st.name, "c")], 1)
                    o.emitted = True

            getattr(blk, _ENGINES[name])(emit)
            st.ops = []



class Ctx:

    def __init__(self, nc, bk, blk, g):
        self.nc = nc
        self.bk = bk
        self.blk = blk
        self.g = g
        self.dram = {}
        self.pers = {}


def _declare_dram(nc, dbg):
    dd = {}

    def ext(name, shape, dt=F32):
        dd[name] = nc.dram_tensor(name, shape, dt, kind="ExternalInput")

    def scratch(name, shape, dt):
        kind = "ExternalOutput" if name in dbg else "Internal"
        dd[name] = nc.dram_tensor(name, shape, dt, kind=kind)

    ext("x_in", [T, D])
    ext("input_ids", [2, 512], mybir.dt.int32)
    ext("ln1_s", [D]); ext("ln1_b", [D]); ext("ln2_s", [D]); ext("ln2_b", [D])
    ext("wq", [D, D]); ext("wk", [D, D]); ext("wv", [D, D]); ext("wo", [D, D])
    ext("w1", [D, MLP]); ext("b1", [MLP]); ext("w2", [MLP, D]); ext("b2", [D])
    ext("gate_w", [NE, D]); ext("gate_b", [NE])
    ext("w13", [NE, 2 * IM, D]); ext("w2m", [NE, D, IM])
    ext("w13s", [2 * IM, D]); ext("w2s", [D, IM])
    ext("wqkv", [3 * D, D]); ext("wout", [D, D])
    ext("qnw", [DH]); ext("knw", [DH])
    ext("wg", [I1, D]); ext("wu", [I1, D]); ext("wd", [D, I1])
    dd["y_out"] = nc.dram_tensor("y_out", [T, D], F32, kind="ExternalOutput")

    scratch("wq_h", [D, D], BF16); scratch("wk_h", [D, D], BF16)
    scratch("wv_h", [D, D], BF16); scratch("wo_h", [D, D], BF16)
    scratch("w1_h", [D, MLP], BF16); scratch("w2_h", [MLP, D], BF16)
    scratch("w13_h", [NE, 2 * IM, D], BF16); scratch("w2m_h", [NE, D, IM], BF16)
    scratch("w13s_h", [2 * IM, D], BF16); scratch("w2s_h", [D, IM], BF16)
    scratch("wqkv_h", [3 * D, D], BF16); scratch("wout_h", [D, D], BF16)
    scratch("gate_w_h", [NE, D], BF16)
    scratch("wgq_h", [I1, D], BF16); scratch("wuq_h", [I1, D], BF16)
    scratch("wdq_h", [D, I1], BF16)

    scratch("xn0", [T, D], F32)
    scratch("xn0_h", [T, D], BF16)
    scratch("xln1_h", [T, D], BF16)
    scratch("x1", [T, D], F32)
    scratch("xln2_h", [T, D], BF16)
    scratch("g0", [T, D], F32)
    scratch("hm_h", [T, IM], BF16)
    scratch("x_res", [T, D], F32)
    scratch("xn1_h", [T, D], BF16)
    scratch("xq_h", [T, D], BF16)
    scratch("qro_h", [T, D], BF16)
    scratch("kro_h", [T, D], BF16)
    scratch("ao_h", [T, D], BF16)
    scratch("vro_h", [T, H * 65], BF16)
    scratch("hq_h", [T, I1], BF16)
    scratch("g1", [T, D], F32)
    return dd


def build_program(dbg=(), stages=("prep", "A", "D", "B", "C", "E", "G", "F")):
    nc = bass.Bass("TRN2", target_bir_lowering=False)
    dd = _declare_dram(nc, set(dbg))
    g = ExitStack()
    with g:
        bk = BK(nc, g)
        blk = g.enter_context(nc.Block())
        cx = Ctx(nc, bk, blk, g)
        cx.dram = dd
        stage_prep(cx, "prep" in stages)
        if "A" in stages:
            stage_A(cx)
        if "D" in stages:
            stage_D(cx)
        if "B" in stages:
            stage_B(cx)
        if "C" in stages:
            stage_C(cx)
        if "E" in stages:
            stage_E(cx)
        if "G" in stages:
            stage_G(cx)
        if "F" in stages:
            stage_F(cx)
        bk.barrier()
        bk.flush(blk)
    return nc


def _end_stage(cx, label=None):
    cx.bk.barrier()
    if label:
        with cx.nc.named_scope(label):
            cx.bk.flush(cx.blk)
    else:
        cx.bk.flush(cx.blk)
    cx.bk.last_w.clear()
    cx.bk.readers.clear()



def stage_prep(cx, do_casts):
    bk, nc, dd, g = cx.bk, cx.nc, cx.dram, cx.g

    if do_casts:
        casts = [("wq", "wq_h"), ("wk", "wk_h"), ("wv", "wv_h"),
                 ("wo", "wo_h"), ("w1", "w1_h"), ("w2", "w2_h"),
                 ("w13s", "w13s_h"), ("w2s", "w2s_h"),
                 ("wqkv", "wqkv_h"), ("wout", "wout_h"), ("gate_w", "gate_w_h")]
        for e in range(NE):
            casts.append((("w13", e), ("w13_h", e)))
            casts.append((("w2m", e), ("w2m_h", e)))
        for src, dst in casts:
            if isinstance(src, tuple):
                s_ap = dd[src[0]][src[1]]
                d_ap = dd[dst[0]][dst[1]]
                rd, wr = (dd[src[0]], src[1]), (dd[dst[0]], dst[1])
            else:
                s_ap, d_ap = dd[src][:], dd[dst][:]
                rd, wr = dd[src], dd[dst]
            bk.op("po", "dma_start", d_ap, s_ap, kind="d",
                  reads=[rd], writes=[wr], tag=f"cast_{src}")

    pers = cx.pers
    ident1 = g.enter_context(nc.sbuf_tensor("ident1", [1, 1], F32))
    bk.op("po", "memset", ident1[:], 1.0, writes=[ident1])
    vecs = [("ln1_s", D), ("ln1_b", D), ("ln2_s", D), ("ln2_b", D),
            ("b1", MLP), ("b2", D)]
    for vec, n in vecs:
        pers[vec] = g.enter_context(
            nc.sbuf_tensor(f"pp_{vec}", [128, n // 128], F32))
    with nc.sbuf_tensor("vrow", [1, MLP], F32) as vrow, \
         nc.psum_tensor("vps", [128, MLP // 128], F32) as vps:
        for vec, n in vecs:
            a = n // 128
            t = pers[vec]
            bk.dma("sy", vrow[:, 0:n], dd[vec][None, :], writes=[vrow],
                   tag=f"vec_{vec}")
            for c in range(a):
                bk.op("pe", "transpose", vps[:, c:c + 1],
                      vrow[:, c * 128:(c + 1) * 128], ident1[:],
                      reads=[vrow, ident1], writes=[(vps, c)], tag=f"vec_{vec}")
            bk.op("ve", "tensor_copy", t[:, 0:a], vps[:, 0:a],
                  reads=[(vps, cc) for cc in range(a)], writes=[t],
                  mark=True, tag=f"vec_{vec}")
        bk.barrier()
        bk.flush(cx.blk)
    ones1 = g.enter_context(nc.sbuf_tensor("ones1", [1, 128], BF16))
    bk.op("po", "memset", ones1[:], 1.0, writes=[ones1])
    brows = [("gate_b", NE), ("qnw", DH), ("knw", DH)]
    for vec, n in brows:
        pers[vec] = g.enter_context(nc.sbuf_tensor(f"rep_{vec}", [128, n], F32))
    with nc.sbuf_tensor("brow_b", [1, DH], BF16) as row, \
         nc.sbuf_tensor("brow_f", [1, DH], F32) as rowf, \
         nc.psum_tensor("brow_ps", [128, DH], F32) as ps:
        for vec, n in brows:
            bk.dma("sy", rowf[:, 0:n], dd[vec][None, :], writes=[rowf],
                   tag=f"row_{vec}", grp="p_vec")
            bk.op("ve", "tensor_copy", row[:, 0:n], rowf[:, 0:n],
                  reads=[rowf], writes=[row])
            bk.op("pe", "matmul", ps[:, 0:n], ones1[:], row[:, 0:n],
                  start=True, stop=True, reads=[ones1, row], writes=[ps])
            bk.op("ve", "tensor_copy", pers[vec][:, 0:n], ps[:, 0:n],
                  reads=[ps], writes=[pers[vec]], mark=True)
        bk.barrier()
        bk.flush(cx.blk)
    mask = g.enter_context(nc.sbuf_tensor("mask_incl", [128, 128], F32))
    bk.op("po", "iota", mask[:], [[1, 128]], base=0, channel_multiplier=-1,
          allow_small_or_imprecise_dtypes=True, writes=[mask])
    bk.op("po", "tensor_scalar", mask[:], mask[:], 0.0, None, ALU.is_ge,
          reads=[mask], writes=[mask])
    pers["mask_incl"] = mask
    identB = g.enter_context(nc.sbuf_tensor("identB", [128, 128], BF16))
    iotaB = g.enter_context(nc.sbuf_tensor("iotaB", [128, 128], F32))
    bk.op("po", "iota", iotaB[:], [[1, 128]], base=0, channel_multiplier=-1,
          allow_small_or_imprecise_dtypes=True, writes=[iotaB])
    bk.op("po", "tensor_scalar", identB[:], iotaB[:], 0.0, None, ALU.is_equal,
          reads=[iotaB], writes=[identB])
    pers["identB"] = identB
    onec = g.enter_context(nc.sbuf_tensor("onec", [128, 1], BF16))
    bk.op("po", "memset", onec[:], 1.0, writes=[onec])
    pers["onec"] = onec
    onesP = g.enter_context(nc.sbuf_tensor("onesP", [128, 64], BF16))
    bk.op("po", "memset", onesP[:], 1.0, writes=[onesP])
    pers["onesP"] = onesP
    onesF = g.enter_context(nc.sbuf_tensor("onesF", [1, 128], F32))
    bk.op("po", "memset", onesF[:], 1.0, writes=[onesF])
    pers["onesF"] = onesF
    b2row2 = g.enter_context(nc.sbuf_tensor("b2row2", [1, D], F32))
    with nc.sbuf_tensor("b2row", [1, D], F32) as b2row:
        bk.dma("sy", b2row[:], dd["b2"][None, :], writes=[b2row], tag="b2row",
               grp="p_vec")
        bk.op("ve", "tensor_scalar_mul", b2row2[:], b2row[:], 2.0,
              reads=[b2row], writes=[b2row2], mark=True)
        bk.barrier()
        bk.flush(cx.blk)
    pers["b2row2"] = b2row2
    _end_stage(cx)



def _ln_tile(bk, sb, se_eps, x_ap, out_f32, out_bf, bufs, tag):
    negm, sq, vs, v1, rstd = bufs
    ve, se = "ve", "se"
    bk.op(ve, "reduce_sum", negm[:], x_ap, axis=AX.X, negate=True,
          reads=[x_ap], writes=[negm], tag=tag)
    bk.op(ve, "tensor_scalar_mul", negm[:], negm[:], 1.0 / D,
          reads=[negm], writes=[negm], tag=tag)
    bk.op(se, "activation", sq[:], x_ap, AF.Square, bias=negm[:, 0:1],
          accum_out=vs[:, 0:1], reads=[x_ap, negm], writes=[sq, vs], tag=tag)
    bk.op(ve, "tensor_scalar", v1[:], vs[:], 1.0 / D, se_eps, ALU.mult, ALU.add,
          reads=[vs], writes=[v1], tag=tag)
    bk.op(se, "activation", rstd[:], v1[:], AF.Ln, reads=[v1], writes=[rstd],
          tag=tag)
    bk.op(se, "activation", rstd[:], rstd[:], AF.Exp, scale=-0.5,
          reads=[rstd], writes=[rstd], tag=tag)
    outs = []
    if out_f32 is not None:
        bk.op(ve, "tensor_scalar", out_f32[:], x_ap, negm[:, 0:1], rstd[:, 0:1],
              ALU.add, ALU.mult, reads=[x_ap, negm, rstd], writes=[out_f32],
              tag=tag)
    if out_bf is not None:
        src = out_f32 if out_f32 is not None else None
        if src is not None:
            bk.op(ve, "tensor_copy", out_bf[:], src[:], reads=[src],
                  writes=[out_bf], tag=tag)
        else:
            bk.op(ve, "tensor_scalar", out_bf[:], x_ap, negm[:, 0:1],
                  rstd[:, 0:1], ALU.add, ALU.mult,
                  reads=[x_ap, negm, rstd], writes=[out_bf], tag=tag)


def stage_A(cx):
    bk, nc, dd = cx.bk, cx.nc, cx.dram
    with ExitStack() as st:
        sb = lambda n, sh, d: st.enter_context(nc.sbuf_tensor(n, sh, d))
        NB = 3
        xt = [sb(f"a_x{i}", [128, D], F32) for i in range(NB)]
        xn = [sb(f"a_xn{i}", [128, D], F32) for i in range(NB)]
        xnb = [sb(f"a_xnb{i}", [128, D], BF16) for i in range(NB)]
        xlb = [sb(f"a_xlb{i}", [128, D], BF16) for i in range(NB)]
        sq = [sb(f"a_sq{i}", [128, D], F32) for i in range(NB)]
        cols = [[sb(f"a_c{i}_{j}", [128, 1], F32) for j in range(4)]
                for i in range(NB)]
        for t in range(NT):
            i = t % NB
            x_ap = dd["x_in"][t * 128:(t + 1) * 128, :]
            bk.dma("sy", xt[i][:], x_ap, tag=f"A{t}", grp="a_in")
            negm, vs, v1, rstd = cols[i]
            _ln_tile(bk, sb, EPS0, xt[i][:], xn[i], xnb[i],
                     (negm, sq[i], vs, v1, rstd), f"A{t}")
            bk.dma("sy", dd["xn0"][t * 128:(t + 1) * 128, :], xn[i][:],
                   reads=[xn[i]], writes=[(dd["xn0"], t)], tag=f"A{t}",
                   grp="a_out")
            bk.dma("sy", dd["xn0_h"][t * 128:(t + 1) * 128, :], xnb[i][:],
                   reads=[xnb[i]], writes=[(dd["xn0_h"], t)], tag=f"A{t}",
                   grp="a_out")
            _ln_tile(bk, sb, EPS1, xn[i][:], None, xlb[i],
                     (negm, sq[i], vs, v1, rstd), f"A{t}l")
            bk.dma("sy", dd["xln1_h"][t * 128:(t + 1) * 128, :], xlb[i][:],
                   reads=[xlb[i]], writes=[(dd["xln1_h"], t)], tag=f"A{t}",
                   grp="a_out")
        _end_stage(cx, "stA")


def stage_D(cx):
    bk, nc, dd, pers = cx.bk, cx.nc, cx.dram, cx.pers
    TH = T // 2
    NK = TH // 128
    with ExitStack() as st:
        sb = lambda n, sh, d: st.enter_context(nc.sbuf_tensor(n, sh, d))
        ps_ = lambda n, sh: st.enter_context(nc.psum_tensor(n, sh, F32))
        gwn = sb("d_gwn", [16, D], BF16)
        gwT = sb("d_gwT", [128, 8 * NE], BF16)
        psg = st.enter_context(nc.psum_tensor("d_psg", [128, NE], BF16))
        bk.dma("sy", gwn[:], dd["gate_w_h"][:], tag="gwT", grp="d_x")
        for c in range(8):
            bk.op("pe", "transpose", psg[:], gwn[:, c * 128:(c + 1) * 128],
                  pers["identB"][0:16, 0:16], reads=[gwn, pers["identB"]],
                  writes=[psg], tag="gwT")
            bk.op("ve", "tensor_copy", gwT[:, c * NE:(c + 1) * NE], psg[:],
                  reads=[psg], writes=[(gwT, c)], tag="gwT")
        xT = [sb(f"d_xT{c}", [128, TH], BF16) for c in range(8)]
        w13T = [[sb(f"d_w13T{p}_{c}", [128, 2 * IM], BF16) for c in range(8)]
                for p in range(2)]
        w2T = [[sb(f"d_w2T{p}_{c}", [128, D], BF16) for c in range(4)]
               for p in range(2)]
        hT = [[sb(f"d_hT{pp_}_{c}", [128, 512], BF16) for c in range(4)]
              for pp_ in range(2)]
        y_sb = sb("d_y", [128, NK * D], F32)
        comb = sb("d_comb", [128, NK * NE], F32)
        S_sb = sb("d_S", [128, NE], F32)
        biased = sb("d_bias", [128, NE], F32)
        eq = sb("d_eq", [128, NE], F32)
        tmp = sb("d_tmp", [128, NE], F32)
        mx = sb("d_mx", [128, 1], F32)
        ssum = sb("d_ssum", [128, 1], F32)
        sg = [sb(f"d_sg{p}", [128, 512], F32) for p in range(2)]
        hbf = [sb(f"d_hbf{p}", [128, 512], BF16) for p in range(2)]
        g0t = sb("d_g0t", [128, D], F32)
        psc = ps_("d_psc", [128, NE])
        pm = [ps_(f"d_pm{i}", [128, 512]) for i in range(4)]
        py = [ps_(f"d_py{i}", [128, 512]) for i in range(2)]
        for hf in range(2):
            t0 = hf * TH
            for c in range(8):
                bk.dma("se", xT[c][:],
                       dd["xn0_h"][t0:t0 + TH, c * 128:(c + 1) * 128],
                       transpose=True,
                       reads=[(dd["xn0_h"], t) for t in range(hf * NK, (hf + 1) * NK)],
                       writes=[xT[c]], tag=f"D{hf}xT", grp="d_x")
            for k in range(NK):
                kk = slice(k * 128, (k + 1) * 128)
                for c in range(8):
                    bk.op("pe", "matmul", psc[:], xT[c][:, kk],
                          gwT[:, c * NE:(c + 1) * NE],
                          start=(c == 0), stop=(c == 7),
                          reads=[xT[c], (gwT, c)], writes=[psc],
                          accum=(c > 0), tag=f"D{hf}r{k}")
                bk.op("se", "activation", S_sb[:], psc[:], AF.Sigmoid,
                      reads=[psc], writes=[S_sb], tag=f"D{hf}r{k}")
                bk.op("ve", "tensor_add", biased[:], S_sb[:], pers["gate_b"][:, 0:NE],
                      reads=[S_sb, pers["gate_b"]], writes=[biased], tag=f"D{hf}r{k}")
                cslice = slice(k * NE, (k + 1) * NE)
                for r in range(4):
                    bk.op("ve", "reduce_max", mx[:], biased[:], axis=AX.X,
                          reads=[biased], writes=[mx], tag=f"D{hf}r{k}")
                    bk.op("ve", "tensor_single_scalar", eq[:], biased[:],
                          mx[:, 0:1], ALU.is_equal,
                          reads=[biased, mx], writes=[eq], tag=f"D{hf}r{k}")
                    if r == 0:
                        bk.op("ve", "tensor_mul", comb[:, cslice], eq[:], S_sb[:],
                              reads=[eq, S_sb], writes=[(comb, k)], tag=f"D{hf}r{k}")
                    else:
                        bk.op("ve", "tensor_mul", tmp[:], eq[:], S_sb[:],
                              reads=[eq, S_sb], writes=[tmp], tag=f"D{hf}r{k}")
                        bk.op("ve", "tensor_add", comb[:, cslice], comb[:, cslice],
                              tmp[:], reads=[(comb, k), tmp], writes=[(comb, k)],
                              tag=f"D{hf}r{k}")
                    if r < 3:
                        bk.op("ve", "tensor_scalar_mul", tmp[:], eq[:], -1e9,
                              reads=[eq], writes=[tmp], tag=f"D{hf}r{k}")
                        bk.op("ve", "tensor_add", biased[:], biased[:], tmp[:],
                              reads=[biased, tmp], writes=[biased], tag=f"D{hf}r{k}")
                bk.op("ve", "reduce_sum", ssum[:], comb[:, cslice], axis=AX.X,
                      reads=[(comb, k)], writes=[ssum], tag=f"D{hf}r{k}")
                bk.op("ve", "reciprocal", ssum[:], ssum[:],
                      reads=[ssum], writes=[ssum], tag=f"D{hf}r{k}")
                bk.op("ve", "tensor_scalar_mul", ssum[:], ssum[:], 2.5,
                      reads=[ssum], writes=[ssum], tag=f"D{hf}r{k}")
                bk.op("ve", "tensor_single_scalar", comb[:, cslice], comb[:, cslice],
                      ssum[:, 0:1], ALU.mult,
                      reads=[(comb, k), ssum], writes=[(comb, k)], tag=f"D{hf}r{k}")
            pend = [None]
            for e in range(NE + 1):
                p = e % 2
                shared = e == NE
                w13src = dd["w13s_h"][:] if shared else dd["w13_h"][e]
                w2src = dd["w2s_h"][:] if shared else dd["w2m_h"][e]
                w13key = dd["w13s_h"] if shared else (dd["w13_h"], e)
                w2key = dd["w2s_h"] if shared else (dd["w2m_h"], e)
                for c in range(8):
                    bk.dma("se", w13T[p][c][:], w13src[:, c * 128:(c + 1) * 128],
                           transpose=True, reads=[w13key], writes=[w13T[p][c]],
                           tag=f"D{hf}e{e}w", grp=f"d_w{p}")
                for c in range(4):
                    bk.dma("se", w2T[p][c][:], w2src[:, c * 128:(c + 1) * 128],
                           transpose=True, reads=[w2key], writes=[w2T[p][c]],
                           tag=f"D{hf}e{e}w", grp=f"d_w{p}")
                def emit_mm1(q):
                    for kq in range(4):
                        k = q * 4 + kq
                        kk = slice(k * 128, (k + 1) * 128)
                        pg, pu = pm[2 * (k % 2)], pm[2 * (k % 2) + 1]
                        tg = f"D{hf}e{e}k{k}"
                        for half_o in range(2):
                            dst = pg if half_o == 0 else pu
                            for c in range(8):
                                bk.op("pe", "matmul", dst[:],
                                      xT[c][:, kk],
                                      w13T[p][c][:, half_o * IM:(half_o + 1) * IM],
                                      start=(c == 0), stop=(c == 7),
                                      reads=[xT[c], w13T[p][c]], writes=[dst],
                                      accum=(c > 0), tag=tg)
                        bk.op("se", "activation", sg[k % 2][:], pg[:], AF.Silu,
                              reads=[pg], writes=[sg[k % 2]], tag=tg)
                        if shared:
                            bk.op("ve", "scalar_tensor_tensor", hbf[k % 2][:],
                                  sg[k % 2][:], 1.0, pu[:], ALU.mult, ALU.mult,
                                  reads=[sg[k % 2], pu], writes=[hbf[k % 2]],
                                  tag=tg)
                        else:
                            bk.op("ve", "scalar_tensor_tensor", hbf[k % 2][:],
                                  sg[k % 2][:], comb[:, k * NE + e:k * NE + e + 1],
                                  pu[:], ALU.mult, ALU.mult,
                                  reads=[sg[k % 2], pu, (comb, k)],
                                  writes=[hbf[k % 2]], tag=tg)
                        bk.dma("sy", dd["hm_h"][t0 + k * 128:t0 + (k + 1) * 128, :],
                               hbf[k % 2][:], reads=[hbf[k % 2]],
                               writes=[(dd["hm_h"], (hf, k))],
                               tag=tg, grp="d_hs")

                def emit_hload(q):
                    q0 = t0 + q * 512
                    for c in range(4):
                        bk.dma("se", hT[q % 2][c][:],
                               dd["hm_h"][q0:q0 + 512, c * 128:(c + 1) * 128],
                               transpose=True,
                               reads=[(dd["hm_h"], (hf, q * 4 + j)) for j in range(4)],
                               writes=[hT[q % 2][c]], tag=f"D{hf}e{e}q{q}",
                               grp="d_ht")

                def emit_mm2(q, e2, p2):
                    for kq in range(4):
                        k = q * 4 + kq
                        tg = f"D{hf}e{e2}y{k}"
                        for b in range(2):
                            pyb = py[b]
                            for c in range(4):
                                bk.op("pe", "matmul", pyb[:],
                                      hT[q % 2][c][:, kq * 128:(kq + 1) * 128],
                                      w2T[p2][c][:, b * 512:(b + 1) * 512],
                                      start=(c == 0), stop=(c == 3),
                                      reads=[hT[q % 2][c], w2T[p2][c]], writes=[pyb],
                                      accum=(c > 0), tag=tg)
                            ys = y_sb[:, k * D + b * 512:k * D + (b + 1) * 512]
                            if e2 == 0:
                                bk.op("ve", "tensor_copy", ys, pyb[:],
                                      reads=[pyb], writes=[(y_sb, (k, b))],
                                      tag=tg)
                            else:
                                bk.op("ve", "tensor_add", ys, ys, pyb[:],
                                      reads=[pyb, (y_sb, (k, b))],
                                      writes=[(y_sb, (k, b))], tag=tg)

                for q in range(NK // 4):
                    emit_mm1(q)
                    emit_hload(q)
                    if pend[0] is not None:
                        emit_mm2(*pend[0])
                    pend[0] = (q, e, p)
            if pend[0] is not None:
                emit_mm2(*pend[0])
                pend[0] = None
            for k in range(NK):
                bk.op("se", "activation", g0t[:], y_sb[:, k * D:(k + 1) * D],
                      AF.Silu, reads=[(y_sb, (k, 0)), (y_sb, (k, 1))],
                      writes=[g0t], tag=f"D{hf}g{k}")
                bk.dma("sy", dd["g0"][t0 + k * 128:t0 + (k + 1) * 128, :], g0t[:],
                       reads=[g0t], writes=[(dd["g0"], hf * NK + k)],
                       tag=f"D{hf}g{k}", grp="d_gs")
        _end_stage(cx, "stD")


def stage_B(cx):
    bk, nc, dd, pers = cx.bk, cx.nc, cx.dram, cx.pers
    TH = T // 2
    NK = TH // 128
    identB = pers["identB"]
    with ExitStack() as st:
        sb = lambda n, sh, d: st.enter_context(nc.sbuf_tensor(n, sh, d))
        ps_ = lambda n, sh, d=F32: st.enter_context(nc.psum_tensor(n, sh, d))
        phiq = [sb(f"b_phiq{c}", [128, TH], BF16) for c in range(8)]
        phik = [sb(f"b_phik{c}", [128, TH], BF16) for c in range(8)]
        v_sb = sb("b_v", [128, NK * D], BF16)
        atn = [sb(f"b_atn{c}", [128, TH], BF16) for c in range(8)]
        state_f = sb("b_stf", [128, 65], F32)
        state_b = sb("b_stb", [128, 65], BF16)
        Am = [sb(f"b_Am{i}", [128, 128], BF16) for i in range(2)]
        kj = [sb(f"b_kj{i}", [128, 64], BF16) for i in range(2)]
        recip_f = sb("b_recf", [128, 128], F32)
        recip_b = sb("b_recb", [128, 128], BF16)
        bb = sb("b_bb", [128, 128], F32)

        for hf in range(2):
            t0 = hf * TH
            with ExitStack() as stq:
                psq = [stq.enter_context(nc.psum_tensor(
                    f"b1_psq{hf}_{i}", [128, 512], F32)) for i in range(2)]
                xlT = [stq.enter_context(nc.sbuf_tensor(f"b_xlT{hf}_{c}", [128, 512], BF16))
                       for c in range(8)]
                wq_sb = [stq.enter_context(nc.sbuf_tensor(f"b_wq{hf}_{c}", [128, D], BF16))
                         for c in range(8)]
                wk_sb = [stq.enter_context(nc.sbuf_tensor(f"b_wk{hf}_{c}", [128, D], BF16))
                         for c in range(8)]
                wv_sb = [stq.enter_context(nc.sbuf_tensor(f"b_wv{hf}_{c}", [128, D], BF16))
                         for c in range(8)]
                for c in range(8):
                    bk.dma("sy", wq_sb[c][:], dd["wq_h"][c * 128:(c + 1) * 128, :],
                           reads=[dd["wq_h"]], tag="Bw", grp="b_w")
                    bk.dma("sy", wk_sb[c][:], dd["wk_h"][c * 128:(c + 1) * 128, :],
                           reads=[dd["wk_h"]], tag="Bw", grp="b_w")
                    bk.dma("sy", wv_sb[c][:], dd["wv_h"][c * 128:(c + 1) * 128, :],
                           reads=[dd["wv_h"]], tag="Bw", grp="b_w")
                for qtr in range(4):
                    q0 = t0 + qtr * 512
                    for c in range(8):
                        bk.dma("se", xlT[c][:],
                               dd["xln1_h"][q0:q0 + 512, c * 128:(c + 1) * 128],
                               transpose=True,
                               reads=[(dd["xln1_h"], q0 // 128 + j) for j in range(4)],
                               writes=[xlT[c]], tag=f"B1x{hf}{qtr}", grp="b_x")
                        bk.op("ve", "tensor_scalar", xlT[c][:], xlT[c][:],
                              pers["ln1_s"][:, c:c + 1],
                              pers["ln1_b"][:, c:c + 1], ALU.mult, ALU.add,
                              reads=[xlT[c], pers["ln1_s"], pers["ln1_b"]],
                              writes=[xlT[c]], tag=f"B1a{hf}{qtr}")
                    for dst, wsb, nm in ((phiq, wq_sb, "q"), (phik, wk_sb, "k")):
                        for dc in range(8):
                            pq = psq[dc % 2]
                            for c in range(8):
                                bk.op("pe", "matmul", pq[:],
                                      wsb[c][:, dc * 128:(dc + 1) * 128],
                                      xlT[c][:], start=(c == 0), stop=(c == 7),
                                      reads=[wsb[c], xlT[c]], writes=[pq],
                                      accum=(c > 0), tag=f"B1{nm}{hf}{qtr}{dc}")
                            bk.op("ve", "tensor_scalar",
                                  dst[dc][:, qtr * 512:(qtr + 1) * 512], pq[:],
                                  0.0, KEPS, ALU.max, ALU.add,
                                  reads=[pq], writes=[(dst[dc], qtr)],
                                  tag=f"B1{nm}e{hf}{qtr}{dc}")
                    for j in range(4):
                        tt = qtr * 4 + j
                        for b2 in range(2):
                            pv = psq[b2]
                            for c in range(8):
                                bk.op("pe", "matmul", pv[:],
                                      xlT[c][:, j * 128:(j + 1) * 128],
                                      wv_sb[c][:, b2 * 512:(b2 + 1) * 512],
                                      start=(c == 0), stop=(c == 7),
                                      reads=[xlT[c], wv_sb[c]], writes=[pv],
                                      accum=(c > 0), tag=f"B1v{hf}{tt}")
                            bk.op("ve", "tensor_copy",
                                  v_sb[:, tt * D + b2 * 512:tt * D + (b2 + 1) * 512],
                                  pv[:], reads=[pv], writes=[(v_sb, (tt, b2))],
                                  tag=f"B1ve{hf}{tt}")
            bk.barrier()
            st2 = ExitStack()
            psA = [st2.enter_context(nc.psum_tensor(f"b2_psA{i}_{hf}",
                   [128, 128], F32)) for i in range(2)]
            psT1 = st2.enter_context(nc.psum_tensor(f"b2_psT_{hf}",
                   [128, 64], BF16))
            psT = [psT1, psT1]
            psN = [st2.enter_context(nc.psum_tensor(f"b2_psN{i}_{hf}",
                   [128, 128], F32)) for i in range(2)]
            psDen = st2.enter_context(nc.psum_tensor(f"b2_psDen_{hf}",
                                                     [128, 128], F32))
            psB = st2.enter_context(nc.psum_tensor(f"b2_psB_{hf}",
                                                   [128, 128], F32))
            psD = st2.enter_context(nc.psum_tensor(f"b2_psD_{hf}",
                                                   [128, 65], F32))
            for h in range(H):
                o = (h % 2) * 64
                P2 = h // 2
                os_ = slice(o, o + 64)
                bk.op("ve", "memset", state_f[os_, :], 0.0,
                      writes=[(state_f, o)], tag=f"B2z{hf}{h}")
                bk.op("ve", "memset", state_b[os_, :], 0.0,
                      writes=[(state_b, o)], tag=f"B2z{hf}{h}")
                for c in range(NK):
                    cs = slice(c * 128, (c + 1) * 128)
                    pq_ap = phiq[P2][os_, cs]
                    pk_ap = phik[P2][os_, cs]
                    v_c = v_sb[:, c * D + h * 64:c * D + h * 64 + 64]
                    tg = f"B2{hf}h{h}c{c}"
                    bk.op("pe", "matmul", psA[c % 2][:], pk_ap, pq_ap, start=True,
                          stop=True, reads=[(phik[P2], c), (phiq[P2], c)],
                          writes=[psA[c % 2]], tag=tg)
                    bk.op("ve", "tensor_tensor", Am[c % 2][:], psA[c % 2][:],
                          pers["mask_incl"][:], ALU.mult,
                          reads=[psA[c % 2], pers["mask_incl"]], writes=[Am[c % 2]], tag=tg)
                    bk.op("pe", "transpose", psT[c % 2][:], pk_ap,
                          identB[os_, os_], reads=[(phik[P2], c), identB],
                          writes=[psT[c % 2]], tag=tg)
                    bk.op("se", "copy", kj[c % 2][:], psT[c % 2][:],
                          reads=[psT[c % 2]], writes=[kj[c % 2]], tag=tg)
                    bk.op("pe", "matmul", psN[c % 2][os_, :], state_b[os_, 0:64],
                          pq_ap, start=True, stop=False, tile_position=(o, o),
                          reads=[(state_b, o), (phiq[P2], c)], writes=[psN[c % 2]],
                          tag=tg)
                    bk.op("pe", "matmul", psN[c % 2][os_, :], v_c, Am[c % 2][:],
                          start=False, stop=True, tile_position=(0, o),
                          reads=[(v_sb, (c, 0)), (v_sb, (c, 1)), Am[c % 2]],
                          writes=[psN[c % 2]], accum=True, tag=tg)
                    bk.op("pe", "matmul", psDen[o:o + 1, :],
                          state_b[os_, 64:65], pq_ap, start=True, stop=False,
                          tile_position=(o, o),
                          reads=[(state_b, o), (phiq[P2], c)], writes=[psDen],
                          tag=tg)
                    bk.op("pe", "matmul", psDen[o:o + 1, :], pers["onec"][:],
                          Am[c % 2][:], start=False, stop=True, tile_position=(0, o),
                          reads=[pers["onec"], Am[c % 2]], writes=[psDen], accum=True,
                          tag=tg)
                    bk.op("pe", "matmul", psD[os_, 0:64], kj[c % 2][:], v_c,
                          start=True, stop=True, tile_position=(0, o),
                          reads=[kj[c % 2], (v_sb, (c, 0)), (v_sb, (c, 1))],
                          writes=[psD], tag=tg)
                    bk.op("pe", "matmul", psD[os_, 64:65], kj[c % 2][:],
                          pers["onec"][:], start=True, stop=True,
                          tile_position=(0, o), reads=[kj[c % 2], pers["onec"]],
                          writes=[psD], accum=True, tag=tg)
                    bk.op("ve", "tensor_add", state_f[os_, :], state_f[os_, :],
                          psD[os_, :], reads=[(state_f, o), psD],
                          writes=[(state_f, o)], tag=tg)
                    bk.op("se", "copy", state_b[os_, :], state_f[os_, :],
                          reads=[(state_f, o)], writes=[(state_b, o)], tag=tg)
                    bk.op("ve", "reciprocal", recip_f[o:o + 1, :],
                          psDen[o:o + 1, :], reads=[psDen],
                          writes=[recip_f], tag=tg)
                    bk.op("se", "copy", recip_b[o:o + 1, :],
                          recip_f[o:o + 1, :], reads=[recip_f],
                          writes=[recip_b], tag=tg)
                    bk.op("pe", "matmul", psB[os_, :],
                          pers["onesP"][o:o + 1, :], recip_b[o:o + 1, :],
                          start=True, stop=True, tile_position=(o, o),
                          reads=[pers["onesP"], recip_b], writes=[psB], tag=tg)
                    bk.op("se", "copy", bb[os_, :], psB[os_, :],
                          reads=[psB], writes=[bb], tag=tg)
                    bk.op("ve", "tensor_tensor", atn[P2][os_, cs], psN[c % 2][os_, :],
                          bb[os_, :], ALU.mult, reads=[psN[c % 2], bb],
                          writes=[(atn[P2], (o, c))], tag=tg)
            bk.barrier()
            st2.close()
            with ExitStack() as st3:
                psq = [st3.enter_context(nc.psum_tensor(
                    f"b3_psq{hf}_{i}", [128, 512], F32)) for i in range(2)]
                wo_sb = [st3.enter_context(
                    nc.sbuf_tensor(f"b_wo{hf}_{c}", [128, D], BF16))
                    for c in range(8)]
                for c in range(8):
                    bk.dma("sy", wo_sb[c][:],
                           dd["wo_h"][c * 128:(c + 1) * 128, :],
                           reads=[dd["wo_h"]], tag="Bw", grp="b_w")
                x1sb = st3.enter_context(nc.sbuf_tensor(f"b3_x1_{hf}", [128, D], F32))
                xnsb = st3.enter_context(nc.sbuf_tensor(f"b3_xn_{hf}", [128, D], F32))
                xl2b = st3.enter_context(nc.sbuf_tensor(f"b3_xl2_{hf}", [128, D], BF16))
                sq3 = st3.enter_context(nc.sbuf_tensor(f"b3_sq_{hf}", [128, D], F32))
                cols3 = [st3.enter_context(nc.sbuf_tensor(f"b3_c{hf}_{j}", [128, 1], F32))
                         for j in range(4)]
                for tt in range(NK):
                    tg = f"B3{hf}t{tt}"
                    ta = t0 + tt * 128
                    bk.dma("sy", xnsb[:], dd["xn0"][ta:ta + 128, :],
                           reads=[(dd["xn0"], ta // 128)], writes=[xnsb],
                           tag=tg, grp="b_x")
                    for b2 in range(2):
                        pw = psq[b2]
                        for c in range(8):
                            bk.op("pe", "matmul", pw[:],
                                  atn[c][:, tt * 128:(tt + 1) * 128],
                                  wo_sb[c][:, b2 * 512:(b2 + 1) * 512],
                                  start=(c == 0), stop=(c == 7),
                                  reads=[(atn[c], (0, tt)), (atn[c], (64, tt)),
                                         wo_sb[c]],
                                  writes=[pw], accum=(c > 0), tag=tg)
                        bk.op("ve", "tensor_add",
                              x1sb[:, b2 * 512:(b2 + 1) * 512], pw[:],
                              xnsb[:, b2 * 512:(b2 + 1) * 512],
                              reads=[pw, xnsb], writes=[x1sb], tag=tg)
                    bk.dma("sy", dd["x1"][ta:ta + 128, :], x1sb[:],
                           reads=[x1sb], writes=[(dd["x1"], ta // 128)],
                           tag=tg, grp="b_out")
                    negm, vs, v1c, rstd = cols3
                    _ln_tile(bk, None, EPS1, x1sb[:], None, xl2b,
                             (negm, sq3, vs, v1c, rstd), tg)
                    bk.dma("sy", dd["xln2_h"][ta:ta + 128, :], xl2b[:],
                           reads=[xl2b], writes=[(dd["xln2_h"], ta // 128)],
                           tag=tg, grp="b_out")
            bk.barrier()
        _end_stage(cx, "stB")


def stage_C(cx):
    bk, nc, dd, pers = cx.bk, cx.nc, cx.dram, cx.pers
    NB = T // 512
    NC1, NC2 = MLP // 128, 8
    with ExitStack() as st:
        sb = lambda n, sh, d: st.enter_context(nc.sbuf_tensor(n, sh, d))
        ps_ = lambda n, sh: st.enter_context(nc.psum_tensor(n, sh, F32))
        w1_sb = [sb(f"c_w1{c}", [128, MLP], BF16) for c in range(8)]
        w2_sb = [sb(f"c_w2{c}", [128, D], BF16) for c in range(32)]
        hT = [sb(f"c_hT{c}", [128, 512], BF16) for c in range(32)]
        xl2T = [sb(f"c_xl2T{c}", [128, 512], BF16) for c in range(8)]
        tsb = [sb(f"c_t{i}", [128, 512], F32) for i in range(2)]
        usb = [sb(f"c_u{i}", [128, 512], F32) for i in range(2)]
        csb = [sb(f"c_c{i}", [128, 512], F32) for i in range(2)]
        x1t = sb("c_x1t", [128, D], F32)
        g0t = sb("c_g0t", [128, D], F32)
        xint = sb("c_xint", [128, D], F32)
        pg = [ps_(f"c_pg{i}", [128, 512]) for i in range(2)]
        py = [ps_(f"c_py{i}", [128, 512]) for i in range(2)]
        for c in range(8):
            bk.dma("sy", w1_sb[c][:], dd["w1_h"][c * 128:(c + 1) * 128, :],
                   reads=[dd["w1_h"]], tag="Cw1", grp="c_w")
        for c in range(32):
            bk.dma("sy", w2_sb[c][:], dd["w2_h"][c * 128:(c + 1) * 128, :],
                   reads=[dd["w2_h"]], tag="Cw2", grp="c_w")
        for blk in range(NB):
            t0 = blk * 512
            for c in range(8):
                bk.dma("se", xl2T[c][:],
                       dd["xln2_h"][t0:t0 + 512, c * 128:(c + 1) * 128],
                       transpose=True,
                       reads=[(dd["xln2_h"], t0 // 128 + j) for j in range(4)],
                       writes=[xl2T[c]], tag=f"C{blk}x", grp="c_x")
                bk.op("ve", "tensor_scalar", xl2T[c][:], xl2T[c][:],
                      pers["ln2_s"][:, c:c + 1], pers["ln2_b"][:, c:c + 1],
                      ALU.mult, ALU.add,
                      reads=[xl2T[c], pers["ln2_s"], pers["ln2_b"]],
                      writes=[xl2T[c]], tag=f"C{blk}x")
            for dc in range(NC1):
                tg = f"C{blk}h{dc}"
                pp = pg[dc % 2]
                for c in range(8):
                    bk.op("pe", "matmul", pp[:],
                          w1_sb[c][:, dc * 128:(dc + 1) * 128], xl2T[c][:],
                          start=(c == 0), stop=(c == 7),
                          reads=[w1_sb[c], xl2T[c]], writes=[pp],
                          accum=(c > 0), tag=tg)
                i = dc % 2
                b1c = pers["b1"][:, dc:dc + 1]
                bk.op("ve", "tensor_scalar", tsb[i][:], pp[:], b1c, None,
                      ALU.add, reads=[pp, pers["b1"]], writes=[tsb[i]], tag=tg)
                bk.op("se", "activation", usb[i][:], tsb[i][:], AF.Square,
                      reads=[tsb[i]], writes=[usb[i]], tag=tg)
                bk.op("ve", "tensor_scalar", usb[i][:], usb[i][:], GELU_A, 1.0,
                      ALU.mult, ALU.add, reads=[usb[i]], writes=[usb[i]], tag=tg)
                bk.op("ve", "tensor_tensor", csb[i][:], tsb[i][:], usb[i][:],
                      ALU.mult, reads=[tsb[i], usb[i]], writes=[csb[i]], tag=tg)
                bk.op("se", "activation", usb[i][:], csb[i][:], AF.Tanh,
                      scale=GELU_C, reads=[csb[i]], writes=[usb[i]], tag=tg)
                bk.op("ve", "scalar_tensor_tensor", hT[dc][:], usb[i][:], 1.0,
                      tsb[i][:], ALU.add, ALU.mult,
                      reads=[usb[i], tsb[i]], writes=[hT[dc]], tag=tg)
            for j in range(4):
                tt = t0 + j * 128
                ti = tt // 128
                tg = f"C{blk}y{j}"
                bk.dma("sy", x1t[:], dd["x1"][tt:tt + 128, :],
                       reads=[(dd["x1"], ti)], writes=[x1t], tag=tg, grp="c_in")
                bk.dma("sy", g0t[:], dd["g0"][tt:tt + 128, :],
                       reads=[(dd["g0"], ti)], writes=[g0t], tag=tg, grp="c_in")
                bk.dma("sy", xint[:], dd["x_in"][tt:tt + 128, :],
                       writes=[xint], tag=tg, grp="c_in")
                for b2 in range(2):
                    pp = py[b2]
                    ds_ = slice(b2 * 512, (b2 + 1) * 512)
                    if TEST_NO_B2:
                        for c in range(32):
                            bk.op("pe", "matmul", pp[:],
                                  hT[c][:, j * 128:(j + 1) * 128],
                                  w2_sb[c][:, ds_], start=(c == 0),
                                  stop=(c == 31),
                                  reads=[hT[c], w2_sb[c]], writes=[pp],
                                  accum=(c > 0), tag=tg)
                    else:
                        bk.op("pe", "matmul", pp[:], pers["onesF"][:],
                              pers["b2row2"][0:1, ds_], start=True, stop=False,
                              reads=[pers["onesF"], pers["b2row2"]],
                              writes=[pp], tag=tg)
                        for c in range(32):
                            bk.op("pe", "matmul", pp[:],
                                  hT[c][:, j * 128:(j + 1) * 128],
                                  w2_sb[c][:, ds_], start=False, stop=(c == 31),
                                  reads=[hT[c], w2_sb[c]], writes=[pp],
                                  accum=True, tag=tg)
                    bk.op("ve", "scalar_tensor_tensor", x1t[:, ds_], pp[:], 0.5,
                          x1t[:, ds_], ALU.mult, ALU.add,
                          reads=[pp, x1t], writes=[x1t], tag=tg)
                    bk.op("ve", "tensor_mul", g0t[:, ds_], x1t[:, ds_],
                          g0t[:, ds_], reads=[x1t, g0t], writes=[g0t], tag=tg)
                    bk.op("ve", "tensor_add", xint[:, ds_], g0t[:, ds_],
                          xint[:, ds_], reads=[g0t, xint], writes=[(xint, b2)],
                          tag=tg)
                bk.dma("sy", dd["x_res"][tt:tt + 128, :], xint[:],
                       reads=[(xint, 0), (xint, 1)],
                       writes=[(dd["x_res"], ti)], tag=tg, grp="c_out")
        _end_stage(cx, "stC")




def emit_fp4(bk, pers, x_ap, out_ap, bufs, N, tag):
    absx, p1, p2, c7, bm, sc = bufs
    NB = N // 16
    a3 = lambda t: t[:, 0:N].rearrange("p (b i) -> p b i", i=16)
    scb = bass.AP(sc, 0, [[sc.shape[1], 128], [1, NB], [0, 16]])
    bk.op("se", "activation", absx[:, 0:N], x_ap, AF.Abs,
          reads=[x_ap], writes=[absx], tag=tag)
    bk.op("ve", "reduce_max", bm[:, 0:NB].rearrange("p (b o) -> p b o", o=1),
          a3(absx), axis=AX.X, reads=[absx], writes=[bm], tag=tag)
    bk.op("ve", "tensor_scalar", sc[:, 0:NB], bm[:, 0:NB], 1.0 / 6.0, 1e-12,
          ALU.mult, ALU.max, reads=[bm], writes=[sc], tag=tag)
    bk.op("ve", "reciprocal", bm[:, 0:NB], sc[:, 0:NB],
          reads=[sc], writes=[bm], tag=tag)
    rsb = bass.AP(bm, 0, [[bm.shape[1], 128], [1, NB], [0, 16]])
    bk.op("ve", "tensor_tensor", a3(absx), a3(absx), rsb, ALU.mult,
          reads=[absx, bm], writes=[absx], tag=tag)
    bk.op("ve", "tensor_scalar", p1[:, 0:N], absx[:, 0:N], 0.25, None,
          ALU.is_gt, reads=[absx], writes=[p1], tag=tag)
    for t in (0.75, 1.25, 1.75):
        bk.op("ve", "scalar_tensor_tensor", p1[:, 0:N], absx[:, 0:N], t,
              p1[:, 0:N], ALU.is_gt, ALU.add, reads=[absx, p1], writes=[p1],
              tag=tag)
    bk.op("ve", "tensor_scalar", p2[:, 0:N], absx[:, 0:N], 2.5, None,
          ALU.is_gt, reads=[absx], writes=[p2], tag=tag)
    bk.op("ve", "scalar_tensor_tensor", p2[:, 0:N], absx[:, 0:N], 3.5,
          p2[:, 0:N], ALU.is_gt, ALU.add, reads=[absx, p2], writes=[p2],
          tag=tag)
    bk.op("ve", "tensor_scalar", c7[:, 0:N], absx[:, 0:N], 5.0, 2.0,
          ALU.is_gt, ALU.mult, reads=[absx], writes=[c7], tag=tag)
    bk.op("ve", "tensor_add", p2[:, 0:N], p2[:, 0:N], c7[:, 0:N],
          reads=[p2, c7], writes=[p2], tag=tag)
    bk.op("ve", "scalar_tensor_tensor", p1[:, 0:N], p1[:, 0:N], 0.5,
          p2[:, 0:N], ALU.mult, ALU.add, reads=[p1, p2], writes=[p1], tag=tag)
    bk.op("se", "sign", c7[:, 0:N], x_ap, reads=[x_ap], writes=[c7], tag=tag)
    bk.op("ve", "tensor_mul", p1[:, 0:N], p1[:, 0:N], c7[:, 0:N],
          reads=[p1, c7], writes=[p1], tag=tag)
    bk.op("ve", "tensor_tensor",
          out_ap.rearrange("p (b i) -> p b i", i=16), a3(p1),
          bass.AP(sc, 0, [[sc.shape[1], 128], [1, NB], [0, 16]]), ALU.mult,
          reads=[p1, sc], writes=[out_ap], tag=tag)


def stage_E(cx):
    bk, nc, dd, pers = cx.bk, cx.nc, cx.dram, cx.pers
    with ExitStack() as st:
        sb = lambda n, sh, d: st.enter_context(nc.sbuf_tensor(n, sh, d))
        NBUF = 2
        xt = [sb(f"e_x{i}", [128, D], F32) for i in range(NBUF)]
        xn = [sb(f"e_xn{i}", [128, D], F32) for i in range(NBUF)]
        xnb = [sb(f"e_xnb{i}", [128, D], BF16) for i in range(NBUF)]
        xqb = [sb(f"e_xqb{i}", [128, D], BF16) for i in range(NBUF)]
        sq = sb("e_sq", [128, D], F32)
        cols = [sb(f"e_c{j}", [128, 1], F32) for j in range(4)]
        qb = (sb("e_qa", [128, D], F32), sb("e_qp1", [128, D], F32),
              sb("e_qp2", [128, D], F32), sb("e_qc7", [128, D], F32),
              sb("e_qbm", [128, D // 16], F32), sb("e_qsc", [128, D // 16], F32))
        for t in range(NT):
            i = t % NBUF
            tg = f"E{t}"
            bk.dma("sy", xt[i][:], dd["x_res"][t * 128:(t + 1) * 128, :],
                   reads=[(dd["x_res"], t)], writes=[xt[i]], tag=tg,
                   grp="e_in")
            negm, vs, v1, rstd = cols
            _ln_tile(bk, None, EPS0, xt[i][:], xn[i], xnb[i],
                     (negm, sq, vs, v1, rstd), tg)
            bk.dma("sy", dd["xn1_h"][t * 128:(t + 1) * 128, :], xnb[i][:],
                   reads=[xnb[i]], writes=[(dd["xn1_h"], t)], tag=tg,
                   grp="e_out")
            emit_fp4(bk, pers, xn[i][:], xqb[i][:], qb, D, tg)
            bk.dma("sy", dd["xq_h"][t * 128:(t + 1) * 128, :], xqb[i][:],
                   reads=[xqb[i]], writes=[(dd["xq_h"], t)], tag=tg,
                   grp="e_out")
        _end_stage(cx, "stE")


def stage_G(cx):
    bk, nc, dd, pers = cx.bk, cx.nc, cx.dram, cx.pers
    TH = T // 2
    NK = TH // 128
    NB6 = (I1 + 511) // 512
    with ExitStack() as st:
        sb = lambda n, sh, d: st.enter_context(nc.sbuf_tensor(n, sh, d))
        ps_ = lambda n, sh: st.enter_context(nc.psum_tensor(n, sh, F32))
        with ExitStack() as stq:
            wt = [stq.enter_context(nc.sbuf_tensor(f"g_wt{i}", [128, I1], F32))
                  for i in range(2)]
            wq = [stq.enter_context(nc.sbuf_tensor(f"g_wq{i}", [128, I1], BF16))
                  for i in range(2)]
            qb = (stq.enter_context(nc.sbuf_tensor("g_qa", [128, I1], F32)),
                  stq.enter_context(nc.sbuf_tensor("g_qp1", [128, I1], F32)),
                  stq.enter_context(nc.sbuf_tensor("g_qp2", [128, I1], F32)),
                  stq.enter_context(nc.sbuf_tensor("g_qc7", [128, I1], F32)),
                  stq.enter_context(nc.sbuf_tensor("g_qbm", [128, I1 // 16], F32)),
                  stq.enter_context(nc.sbuf_tensor("g_qsc", [128, I1 // 16], F32)))
            jobs = []
            for w_, wq_, rows, cols_ in (("wg", "wgq_h", I1, D),
                                         ("wu", "wuq_h", I1, D),
                                         ("wd", "wdq_h", D, I1)):
                for r0 in range(0, rows, 128):
                    jobs.append((w_, wq_, r0, cols_))
            for ji, (w_, wq_, r0, cols_) in enumerate(jobs):
                i = ji % 2
                tg = f"Gq{w_}{r0}"
                bk.dma("sy", wt[i][:, 0:cols_], dd[w_][r0:r0 + 128, :],
                       reads=[(dd[w_], r0)], writes=[wt[i]], tag=tg,
                       grp="g_wq")
                emit_fp4(bk, pers, wt[i][:, 0:cols_], wq[i][:, 0:cols_],
                         qb, cols_, tg)
                bk.dma("sy", dd[wq_][r0:r0 + 128, :], wq[i][:, 0:cols_],
                       reads=[wq[i]], writes=[(dd[wq_], r0)], tag=tg,
                       grp="g_wqs")
            bk.barrier()
        wgT = [sb(f"g_wgT{c}", [128, I1], BF16) for c in range(8)]
        wuT = [sb(f"g_wuT{c}", [128, I1], BF16) for c in range(8)]
        wdT = [sb(f"g_wdT{c}", [128, D], BF16) for c in range(22)]
        for c in range(8):
            bk.dma("se", wgT[c][:], dd["wgq_h"][:, c * 128:(c + 1) * 128],
                   transpose=True,
                   reads=[(dd["wgq_h"], r0) for r0 in range(0, I1, 128)],
                   writes=[wgT[c]], tag="GwT", grp="g_w")
            bk.dma("se", wuT[c][:], dd["wuq_h"][:, c * 128:(c + 1) * 128],
                   transpose=True,
                   reads=[(dd["wuq_h"], r0) for r0 in range(0, I1, 128)],
                   writes=[wuT[c]], tag="GwT", grp="g_w")
        for c in range(22):
            bk.dma("se", wdT[c][:], dd["wdq_h"][:, c * 128:(c + 1) * 128],
                   transpose=True,
                   reads=[(dd["wdq_h"], r0) for r0 in range(0, D, 128)],
                   writes=[wdT[c]], tag="GwT", grp="g_w")
        xqT = [sb(f"g_xqT{c}", [128, 512], BF16) for c in range(8)]
        hT = [[sb(f"g_hT{pp_}_{c}", [128, 512], BF16) for c in range(22)]
              for pp_ in range(2)]
        sg_1 = sb("g_sg", [128, 512], F32)
        sg = [sg_1, sg_1]
        hf_1 = sb("g_hf", [128, 512], F32)
        hf = [hf_1, hf_1]
        hq = [sb(f"g_hq{i}", [128, 512], BF16) for i in range(2)]
        g1t = sb("g_g1t", [128, D], F32)
        qb2_1 = (sb("g_q2a", [128, 512], F32), sb("g_q2p1", [128, 512], F32),
                 sg_1, sb("g_q2c7", [128, 512], F32),
                 sb("g_q2bm", [128, 32], F32), sb("g_q2sc", [128, 32], F32))
        qb2 = [qb2_1, qb2_1]
        pg = [ps_(f"g_pg{i}", [128, 512]) for i in range(2)]
        pu = [ps_(f"g_pu{i}", [128, 512]) for i in range(2)]
        py = [ps_(f"g_py{i}", [128, 512]) for i in range(2)]
        def g_mm1(hf_, q):
            t0 = hf_ * TH
            for kq in range(4):
                k = q * 4 + kq
                kk = slice(kq * 128, (kq + 1) * 128)
                tg = f"G{hf_}k{k}"
                for blkb in range(NB6):
                    bw = min(512, I1 - blkb * 512)
                    bs = slice(blkb * 512, blkb * 512 + bw)
                    for c in range(8):
                        bk.op("pe", "matmul", pg[blkb % 2][:, 0:bw], xqT[c][:, kk],
                              wgT[c][:, bs], start=(c == 0), stop=(c == 7),
                              reads=[xqT[c], wgT[c]], writes=[pg[blkb % 2]],
                              accum=(c > 0), tag=tg)
                    for c in range(8):
                        bk.op("pe", "matmul", pu[blkb % 2][:, 0:bw], xqT[c][:, kk],
                              wuT[c][:, bs], start=(c == 0), stop=(c == 7),
                              reads=[xqT[c], wuT[c]], writes=[pu[blkb % 2]],
                              accum=(c > 0), tag=tg)
                    bk.op("se", "activation", sg[blkb % 2][:, 0:bw],
                          pg[blkb % 2][:, 0:bw],
                          AF.Silu, reads=[pg[blkb % 2]], writes=[sg[blkb % 2]],
                          tag=tg)
                    bk.op("ve", "tensor_mul", hf[blkb % 2][:, 0:bw],
                          sg[blkb % 2][:, 0:bw],
                          pu[blkb % 2][:, 0:bw], reads=[sg[blkb % 2], pu[blkb % 2]],
                          writes=[hf[blkb % 2]], tag=tg)
                    emit_fp4(bk, pers, hf[blkb % 2][:, 0:bw], hq[blkb % 2][:, 0:bw],
                             qb2[blkb % 2], bw, tg)
                    bk.dma("sy", dd["hq_h"][t0 + k * 128:t0 + (k + 1) * 128,
                                            blkb * 512:blkb * 512 + bw],
                           hq[blkb % 2][:, 0:bw], reads=[hq[blkb % 2]],
                           writes=[(dd["hq_h"], (hf_, k, blkb))], tag=tg,
                           grp="g_hs")

        def g_hload(hf_, q):
            t0 = hf_ * TH
            q0 = t0 + q * 512
            for c in range(22):
                bk.dma("se", hT[q % 2][c][:],
                       dd["hq_h"][q0:q0 + 512, c * 128:(c + 1) * 128],
                       transpose=True,
                       reads=[(dd["hq_h"], (hf_, q * 4 + j, bb))
                              for j in range(4) for bb in range(NB6)],
                       writes=[hT[q % 2][c]], tag=f"G{hf_}q{q}", grp="g_ht")

        def g_mm2(hf_, q):
            t0 = hf_ * TH
            for kq in range(4):
                k = q * 4 + kq
                ti = t0 // 128 + k
                tg = f"G{hf_}y{k}"
                for b2 in range(2):
                    pp = py[b2]
                    for c in range(22):
                        bk.op("pe", "matmul", pp[:],
                              hT[q % 2][c][:, kq * 128:(kq + 1) * 128],
                              wdT[c][:, b2 * 512:(b2 + 1) * 512],
                              start=(c == 0), stop=(c == 21),
                              reads=[hT[q % 2][c], wdT[c]], writes=[pp],
                              accum=(c > 0), tag=tg)
                    bk.op("se", "activation", g1t[:, b2 * 512:(b2 + 1) * 512],
                          pp[:], AF.Silu, reads=[pp], writes=[(g1t, b2)],
                          tag=tg)
                bk.dma("sy", dd["g1"][ti * 128:(ti + 1) * 128, :], g1t[:],
                       reads=[(g1t, 0), (g1t, 1)],
                       writes=[(dd["g1"], ti)], tag=tg, grp="g_gs")

        pendg = None
        for hf_ in range(2):
            t0 = hf_ * TH
            for q in range(NK // 4):
                qq0 = t0 + q * 512
                for c in range(8):
                    bk.dma("se", xqT[c][:],
                           dd["xq_h"][qq0:qq0 + 512, c * 128:(c + 1) * 128],
                           transpose=True,
                           reads=[(dd["xq_h"], qq0 // 128 + j) for j in range(4)],
                           writes=[xqT[c]], tag=f"G{hf_}x{q}", grp="g_x")
                g_mm1(hf_, q)
                g_hload(hf_, q)
                if pendg is not None:
                    g_mm2(*pendg)
                pendg = (hf_, q)
        if pendg is not None:
            g_mm2(*pendg)
        _end_stage(cx, "stG")


def stage_F(cx):
    bk, nc, dd, pers = cx.bk, cx.nc, cx.dram, cx.pers
    TH = T // 2
    NK = TH // 128
    R2PI = 1.0 / (2.0 * math.pi)
    RND_C = 12582912.0
    with ExitStack() as st:
        sb = lambda n, sh, d: st.enter_context(nc.sbuf_tensor(n, sh, d))
        ps_ = lambda n, sh: st.enter_context(nc.psum_tensor(n, sh, F32))
        cosT = sb("f_cos", [128, NK * 32], F32)
        sinT = sb("f_sin", [128, NK * 32], F32)
        with ExitStack() as s0:
            ivrow = s0.enter_context(nc.sbuf_tensor("f_ivr", [1, 32], F32))
            ivrep = s0.enter_context(nc.sbuf_tensor("f_ivrep", [128, 32], F32))
            ivb = s0.enter_context(nc.sbuf_tensor("f_ivb", [1, 32], BF16))
            poscol = s0.enter_context(nc.sbuf_tensor("f_pos", [128, 1], F32))
            ang = s0.enter_context(nc.sbuf_tensor("f_ang", [128, 32], F32))
            tmpa = s0.enter_context(nc.sbuf_tensor("f_tmpa", [128, 32], F32))
            hpi = s0.enter_context(nc.sbuf_tensor("f_hpi", [128, 1], F32))
            psr = s0.enter_context(nc.psum_tensor("f_psr", [128, 32], F32))
            bk.op("po", "iota", ivrow[:], [[1, 32]], base=0,
                  channel_multiplier=0, allow_small_or_imprecise_dtypes=True,
                  writes=[ivrow], tag="Frt")
            bk.op("po", "memset", hpi[:], float(math.pi / 2.0), writes=[hpi],
                  tag="Frt")
            bk.op("se", "activation", ivrow[:], ivrow[:], AF.Exp,
                  scale=float(-math.log(10000.0) / 32.0),
                  reads=[ivrow], writes=[ivrow], tag="Frt")
            bk.op("ve", "tensor_copy", ivb[:], ivrow[:], reads=[ivrow],
                  writes=[ivb], tag="Frt")
            bk.op("pe", "matmul", psr[:], pers["onesF"][:].rearrange(
                  "a b -> a b"), ivrow[:], start=True, stop=True,
                  reads=[pers["onesF"], ivrow], writes=[psr], tag="Frt")
            bk.op("ve", "tensor_copy", ivrep[:], psr[:], reads=[psr],
                  writes=[ivrep], tag="Frt")
            for t in range(NK):
                tg = f"Frt{t}"
                cs = slice(t * 32, (t + 1) * 32)
                bk.op("po", "iota", poscol[:], [[0, 1]], base=t * 128,
                      channel_multiplier=1,
                      allow_small_or_imprecise_dtypes=True,
                      writes=[poscol], tag=tg)
                bk.op("ve", "tensor_single_scalar", ang[:], ivrep[:],
                      poscol[:, 0:1], ALU.mult, reads=[ivrep, poscol],
                      writes=[ang], tag=tg)
                bk.op("ve", "tensor_scalar", tmpa[:], ang[:], R2PI, RND_C,
                      ALU.mult, ALU.add, reads=[ang], writes=[tmpa], tag=tg)
                bk.op("ve", "tensor_scalar", tmpa[:], tmpa[:], -RND_C, None,
                      ALU.add, reads=[tmpa], writes=[tmpa], tag=tg)
                bk.op("ve", "scalar_tensor_tensor", ang[:], tmpa[:],
                      float(-2.0 * math.pi), ang[:], ALU.mult, ALU.add,
                      reads=[tmpa, ang], writes=[ang], tag=tg)
                bk.op("ve", "tensor_scalar", ang[:], ang[:],
                      float(math.pi), float(-math.pi), ALU.min, ALU.max,
                      reads=[ang], writes=[ang], tag=tg)
                bk.op("se", "activation", sinT[:, cs], ang[:], AF.Sin,
                      reads=[ang], writes=[(sinT, t)], tag=tg)
                bk.op("se", "activation", tmpa[:], ang[:], AF.Abs,
                      reads=[ang], writes=[tmpa], tag=tg)
                bk.op("se", "activation", cosT[:, cs], tmpa[:], AF.Sin,
                      scale=-1.0, bias=hpi[:, 0:1],
                      reads=[tmpa, hpi], writes=[(cosT, t)], tag=tg)
        bk.barrier()
        negC = sb("f_negC", [128, 1], F32)
        bk.op("po", "memset", negC[:], -20.0, writes=[negC], tag="FnegC")
        wqkvT = [sb(f"f_wqkvT{c}", [128, 3 * D], BF16) for c in range(8)]
        for c in range(8):
            bk.dma("se", wqkvT[c][:], dd["wqkv_h"][:, c * 128:(c + 1) * 128],
                   transpose=True, reads=[dd["wqkv_h"]], writes=[wqkvT[c]],
                   tag="FwT", grp="f_w")
        for hf in range(2):
            t0 = hf * TH
            with ExitStack() as s1:
                sb1 = lambda n, sh, d: s1.enter_context(
                    nc.sbuf_tensor(f"{n}_{hf}", sh, d))
                xnT = [sb1(f"f_xnT{c}", [128, TH], BF16) for c in range(8)]
                v_sb = sb1("f_v", [128, H * 65], BF16)
                ro = sb1("f_ro", [128, D], F32)
                tmp = sb1("f_tmp", [128, 512], F32)
                tmp2 = sb1("f_tmp2", [128, D], F32)
                ms = sb1("f_ms", [128, H], F32)
                qro = sb1("f_qro", [128, D], BF16)
                pq = [s1.enter_context(nc.psum_tensor(f"f_pq{i}_{hf}",
                      [128, 512], F32)) for i in range(6)]
                for c in range(8):
                    bk.dma("se", xnT[c][:],
                           dd["xn1_h"][t0:t0 + TH, c * 128:(c + 1) * 128],
                           transpose=True,
                           reads=[(dd["xn1_h"], t0 // 128 + j)
                                  for j in range(NK)],
                           writes=[xnT[c]], tag=f"F1x{hf}", grp="f_x")
                for k in range(NK):
                    kk = slice(k * 128, (k + 1) * 128)
                    tg = f"F1{hf}k{k}"
                    for part in range(6):
                        pp = pq[part]
                        for c in range(8):
                            bk.op("pe", "matmul", pp[:], xnT[c][:, kk],
                                  wqkvT[c][:, part * 512:(part + 1) * 512],
                                  start=(c == 0), stop=(c == 7),
                                  reads=[xnT[c], wqkvT[c]], writes=[pp],
                                  accum=(c > 0), tag=tg)
                    if k == 0:
                        bk.op("ve", "memset",
                              bass.AP(v_sb, 64, [[H * 65, 128], [65, H], [1, 1]]),
                              1.0, writes=[(v_sb, "ones")], tag=tg)
                    for b2 in range(2):
                        bk.op("ve", "tensor_copy",
                              bass.AP(v_sb, b2 * 8 * 65,
                                      [[H * 65, 128], [65, 8], [1, 64]]),
                              pq[4 + b2][:].rearrange("p (h d) -> p h d", d=64),
                              reads=[pq[4 + b2], (v_sb, "ones")],
                              writes=[(v_sb, b2)], tag=tg)
                    bk.dma("sy", dd["vro_h"][(t0 // 128 + k) * 128:
                                             (t0 // 128 + k + 1) * 128, :],
                           v_sb[:], reads=[(v_sb, 0), (v_sb, 1), (v_sb, "ones")],
                           writes=[(dd["vro_h"], t0 // 128 + k)], tag=tg,
                           grp="f_qk")
                    for (pa, pb, wrep, dst) in ((0, 1, "qnw", "qro_h"),
                                                (2, 3, "knw", "kro_h")):
                        cs = slice(k * 32, (k + 1) * 32)
                        c3 = bass.AP(cosT, k * 32,
                                     [[cosT.shape[1], 128], [0, 8], [1, 32]])
                        s3 = bass.AP(sinT, k * 32,
                                     [[sinT.shape[1], 128], [0, 8], [1, 32]])
                        for half_i, pp in ((0, pq[pa]), (1, pq[pb])):
                            p3a = pp[:].rearrange("p (h d) -> p h d", d=64)
                            q1 = pp[:].rearrange(
                                "p (h e d) -> p (h e) d", e=2, d=32
                            )
                            ro3 = ro[:, half_i * 512:(half_i + 1) * 512]
                            rr = lambda t_, off: bass.AP(
                                t_.tensor, t_.offset + off,
                                [[t_.ap[0][0], 128], [64, 8], [1, 32]])
                            q1v = bass.AP(pp, 0, [[512, 128], [64, 8], [1, 32]])
                            q2v = bass.AP(pp, 32, [[512, 128], [64, 8], [1, 32]])
                            o1 = bass.AP(ro, half_i * 512,
                                         [[ro.shape[1], 128], [64, 8], [1, 32]])
                            o2 = bass.AP(ro, half_i * 512 + 32,
                                         [[ro.shape[1], 128], [64, 8], [1, 32]])
                            t1 = bass.AP(tmp, 0, [[512, 128], [64, 8], [1, 32]])
                            tg2 = tg + f"r{pa}{half_i}"
                            bk.op("ve", "tensor_tensor", o1, q1v, c3, ALU.mult,
                                  reads=[pp, cosT], writes=[ro], tag=tg2)
                            bk.op("ve", "tensor_tensor", t1, q2v, s3, ALU.mult,
                                  reads=[pp, sinT], writes=[tmp], tag=tg2)
                            bk.op("ve", "tensor_sub", o1, o1, t1,
                                  reads=[ro, tmp], writes=[ro], tag=tg2)
                            bk.op("ve", "tensor_tensor", o2, q2v, c3, ALU.mult,
                                  reads=[pp, cosT], writes=[ro], tag=tg2)
                            bk.op("ve", "tensor_tensor", t1, q1v, s3, ALU.mult,
                                  reads=[pp, sinT], writes=[tmp], tag=tg2)
                            bk.op("ve", "tensor_add", o2, o2, t1,
                                  reads=[ro, tmp], writes=[ro], tag=tg2)
                        tg3 = tg + f"n{pa}"
                        bk.op("ve", "tensor_mul", tmp2[:], ro[:], ro[:],
                              reads=[ro], writes=[tmp2], tag=tg3)
                        bk.op("ve", "reduce_sum",
                              ms[:].rearrange("p (h o) -> p h o", o=1),
                              tmp2[:].rearrange("p (h d) -> p h d", d=64),
                              axis=AX.X, reads=[tmp2], writes=[ms], tag=tg3)
                        bk.op("ve", "tensor_scalar", ms[:], ms[:], 1.0 / 64,
                              RMS_EPS, ALU.mult, ALU.add, reads=[ms],
                              writes=[ms], tag=tg3)
                        bk.op("se", "activation", ms[:], ms[:], AF.Ln,
                              reads=[ms], writes=[ms], tag=tg3)
                        bk.op("se", "activation", ms[:], ms[:], AF.Exp,
                              scale=-0.5, reads=[ms], writes=[ms], tag=tg3)
                        msb = bass.AP(ms, 0, [[ms.shape[1], 128], [1, H], [0, 64]])
                        bk.op("ve", "tensor_tensor",
                              tmp2[:].rearrange("p (h d) -> p h d", d=64),
                              ro[:].rearrange("p (h d) -> p h d", d=64),
                              msb, ALU.mult, reads=[ro, ms], writes=[tmp2],
                              tag=tg3)
                        wr = pers[wrep]
                        wrb = bass.AP(wr, 0, [[wr.shape[1], 128], [0, H], [1, 64]])
                        bk.op("ve", "tensor_tensor",
                              qro[:].rearrange("p (h d) -> p h d", d=64),
                              tmp2[:].rearrange("p (h d) -> p h d", d=64),
                              wrb, ALU.mult, reads=[tmp2, wr], writes=[qro],
                              tag=tg3)
                        ti = t0 // 128 + k
                        bk.dma("sy", dd[dst][ti * 128:(ti + 1) * 128, :],
                               qro[:], reads=[qro], writes=[(dd[dst], ti)],
                               tag=tg3, grp="f_qk")
                bk.barrier()
            with ExitStack() as s2:
                sb2 = lambda n, sh, d: s2.enter_context(
                    nc.sbuf_tensor(f"{n}_{hf}", sh, d))
                v_sb2 = sb2("f2_v", [128, NK * H * 65], BF16)
                qT = sb2("f2_qT", [128, TH], BF16)
                kT = sb2("f2_kT", [128, TH], BF16)
                P_sb = [sb2(f"f2_P{i}", [128, 512], BF16) for i in range(2)]
                rl = sb2("f2_rl", [128, 1], F32)
                ao = sb2("f2_ao", [128, NK * D], BF16)
                psS = [s2.enter_context(nc.psum_tensor(f"f2_psS{i}_{hf}",
                       [128, 512], F32)) for i in range(2)]
                psO = [s2.enter_context(nc.psum_tensor(f"f2_psO{i}_{hf}",
                       [128, 65], F32)) for i in range(4)]
                VW = H * 65
                for k in range(NK):
                    bk.dma("sy", v_sb2[:, k * VW:(k + 1) * VW],
                           dd["vro_h"][(t0 // 128 + k) * 128:
                                       (t0 // 128 + k + 1) * 128, :],
                           reads=[(dd["vro_h"], t0 // 128 + k)],
                           writes=[(v_sb2, k)], tag=f"F2v{hf}", grp="f_qt")
                for hp in range(8):
                    tgh = f"F2{hf}p{hp}"
                    bk.dma("se", qT[:],
                           dd["qro_h"][t0:t0 + TH, hp * 128:(hp + 1) * 128],
                           transpose=True,
                           reads=[(dd["qro_h"], t0 // 128 + j)
                                  for j in range(NK)],
                           writes=[qT], tag=tgh, grp="f_qt")
                    bk.dma("se", kT[:],
                           dd["kro_h"][t0:t0 + TH, hp * 128:(hp + 1) * 128],
                           transpose=True,
                           reads=[(dd["kro_h"], t0 // 128 + j)
                                  for j in range(NK)],
                           writes=[kT], tag=tgh, grp="f_qt")
                    for hsub in range(2):
                        h = hp * 2 + hsub
                        o = hsub * 64
                        os_ = slice(o, o + 64)
                        for Q in range(4):
                            qs = slice(Q * 512, (Q + 1) * 512)
                            NJ = 4 * Q + 4

                            def emit_S(j, tg):
                                js = slice(j * 128, (j + 1) * 128)
                                bk.op("pe", "matmul", psS[j % 2][:], kT[os_, js],
                                      qT[os_, qs], start=True, stop=True,
                                      reads=[qT, kT], writes=[psS[j % 2]], tag=tg)
                                bk.op("se", "activation", P_sb[j % 2][:],
                                      psS[j % 2][:], AF.Exp, scale=0.125,
                                      bias=negC[:, 0:1],
                                      reads=[psS[j % 2], negC],
                                      writes=[P_sb[j % 2]], tag=tg)
                                if j >= 4 * Q:
                                    qt_d = j - 4 * Q
                                    bk.op("ve", "tensor_tensor",
                                          P_sb[j % 2][:, qt_d * 128:(qt_d + 1) * 128],
                                          P_sb[j % 2][:, qt_d * 128:(qt_d + 1) * 128],
                                          pers["mask_incl"][:], ALU.mult,
                                          reads=[P_sb[j % 2], pers["mask_incl"]],
                                          writes=[P_sb[j % 2]], tag=tg)

                            def emit_O(j, tg):
                                for qt_ in range(4):
                                    gq = 4 * Q + qt_
                                    if gq < j:
                                        continue
                                    pO = psO[qt_]
                                    first = (j == 0)
                                    bk.op("pe", "matmul", pO[:, 0:65],
                                          P_sb[j % 2][:, qt_ * 128:(qt_ + 1) * 128],
                                          v_sb2[:, j * VW + h * 65:
                                                j * VW + h * 65 + 65],
                                          start=first, stop=(j == gq),
                                          reads=[P_sb[j % 2], (v_sb2, j)],
                                          writes=[pO], accum=not first, tag=tg)

                            for j in range(NJ):
                                tg = f"{tgh}h{hsub}Q{Q}j{j}"
                                emit_S(j, tg)
                                if j > 0:
                                    emit_O(j - 1, tg)
                            emit_O(NJ - 1, f"{tgh}h{hsub}Q{Q}jz")
                            for qt_ in range(4):
                                tg = f"{tgh}h{hsub}Q{Q}e{qt_}"
                                pO = psO[qt_]
                                bk.op("ve", "reciprocal", rl[:],
                                      pO[:, 64:65], reads=[pO], writes=[rl],
                                      tag=tg)
                                gq = 4 * Q + qt_
                                bk.op("ve", "tensor_single_scalar",
                                      ao[:, gq * D + h * 64:
                                         gq * D + h * 64 + 64],
                                      pO[:, 0:64], rl[:, 0:1], ALU.mult,
                                      reads=[pO, rl],
                                      writes=[(ao, (gq, h))], tag=tg)
                for k in range(NK):
                    ti = t0 // 128 + k
                    bk.dma("sy", dd["ao_h"][ti * 128:(ti + 1) * 128, :],
                           ao[:, k * D:(k + 1) * D],
                           reads=[(ao, (k, hh)) for hh in range(H)],
                           writes=[(dd["ao_h"], ti)], tag=f"F2o{hf}",
                           grp="f_ao")
                bk.barrier()
        with ExitStack() as s3:
            sb3 = lambda n, sh, d: s3.enter_context(nc.sbuf_tensor(n, sh, d))
            woT = [sb3(f"f3_woT{c}", [128, D], BF16) for c in range(8)]
            for c in range(8):
                bk.dma("se", woT[c][:], dd["wout_h"][:, c * 128:(c + 1) * 128],
                       transpose=True, reads=[dd["wout_h"]], writes=[woT[c]],
                       tag="F3w", grp="f_w")
            aoT = [sb3(f"f3_aoT{c}", [128, 512], BF16) for c in range(8)]
            xrt = sb3("f3_xr", [128, D], F32)
            g1t = sb3("f3_g1", [128, D], F32)
            ot = sb3("f3_o", [128, D], F32)
            pw = [s3.enter_context(nc.psum_tensor(f"f3_pw{i}", [128, 512],
                                                  F32)) for i in range(2)]
            for q in range(T // 512):
                q0 = q * 512
                for c in range(8):
                    bk.dma("se", aoT[c][:],
                           dd["ao_h"][q0:q0 + 512, c * 128:(c + 1) * 128],
                           transpose=True,
                           reads=[(dd["ao_h"], q0 // 128 + j)
                                  for j in range(4)],
                           writes=[aoT[c]], tag=f"F3a{q}", grp="f_at")
                for kq in range(4):
                    ti = q * 4 + kq
                    tg = f"F3t{ti}"
                    bk.dma("sy", xrt[:], dd["x_res"][ti * 128:(ti + 1) * 128, :],
                           reads=[(dd["x_res"], ti)], writes=[xrt], tag=tg,
                           grp="f_in")
                    bk.dma("sy", g1t[:], dd["g1"][ti * 128:(ti + 1) * 128, :],
                           reads=[(dd["g1"], ti)], writes=[g1t], tag=tg,
                           grp="f_in")
                    for b2 in range(2):
                        pp = pw[b2]
                        ds_ = slice(b2 * 512, (b2 + 1) * 512)
                        for c in range(8):
                            bk.op("pe", "matmul", pp[:],
                                  aoT[c][:, kq * 128:(kq + 1) * 128],
                                  woT[c][:, ds_], start=(c == 0), stop=(c == 7),
                                  reads=[aoT[c], woT[c]], writes=[pp],
                                  accum=(c > 0), tag=tg)
                        bk.op("ve", "tensor_mul", g1t[:, ds_], pp[:],
                              g1t[:, ds_], reads=[pp, g1t], writes=[g1t],
                              tag=tg)
                        bk.op("ve", "tensor_add", ot[:, ds_], g1t[:, ds_],
                              xrt[:, ds_], reads=[g1t, xrt], writes=[(ot, b2)],
                              tag=tg)
                    bk.dma("sy", dd["y_out"][ti * 128:(ti + 1) * 128, :],
                           ot[:], reads=[(ot, 0), (ot, 1)],
                           writes=[(dd["y_out"], ti)], tag=tg, grp="f_y")
        _end_stage(cx, "stF")



def _prep_inputs(kw):
    x = np.ascontiguousarray(np.asarray(kw["L0_attn_inputs"], dtype=np.float32))
    full = {
        "ln1_s": kw["L0_attn_ln1_scale"], "ln1_b": kw["L0_attn_ln1_bias"],
        "wq": np.asarray(kw["L0_attn_wq"]).reshape(D, D),
        "wk": np.asarray(kw["L0_attn_wk"]).reshape(D, D),
        "wv": np.asarray(kw["L0_attn_wv"]).reshape(D, D),
        "wo": np.asarray(kw["L0_attn_wo"]).reshape(D, D),
        "ln2_s": kw["L0_attn_ln2_scale"], "ln2_b": kw["L0_attn_ln2_bias"],
        "w1": kw["L0_attn_w1"], "b1": kw["L0_attn_b1"],
        "w2": kw["L0_attn_w2"], "b2": kw["L0_attn_b2"],
        "gate_w": kw["L0_ffn_gate_w"], "gate_b": kw["L0_ffn_gate_b"],
        "w13": kw["L0_ffn_w13"], "w2m": kw["L0_ffn_w2"],
        "w13s": kw["L0_ffn_w13_s"], "w2s": kw["L0_ffn_w2_s"],
        "wqkv": kw["L1_attn_w_qkv"], "wout": kw["L1_attn_w_out"],
        "qnw": kw["L1_attn_q_norm_w"], "knw": kw["L1_attn_k_norm_w"],
        "wg": kw["L1_ffn_w_gate"], "wu": kw["L1_ffn_w_up"],
        "wd": kw["L1_ffn_w_down"],
    }
    ids = np.ascontiguousarray(np.asarray(kw["L0_ffn_input_ids"], dtype=np.int32))
    full = {k: np.ascontiguousarray(np.asarray(v, dtype=np.float32))
            for k, v in full.items()}
    full["input_ids"] = ids
    in_maps = []
    for c in range(N_CORES):
        m = dict(full)
        m["x_in"] = np.ascontiguousarray(
            x[c * BC:(c + 1) * BC].reshape(T, D))
        in_maps.append(m)
    return in_maps


def kernel(**kw):
    in_maps = _prep_inputs(kw)
    nc = build_program()
    res = run_bass_kernel_spmd(nc, in_maps, core_ids=list(range(N_CORES)))
    out = np.concatenate(
        [r["y_out"].reshape(BC, S, D) for r in res.results], axis=0)
    return out.astype(np.float32)


# version 09
# speedup vs baseline: 1.0537x; 1.0064x over previous
import math
from contextlib import ExitStack

import numpy as np

import concourse.bass as bass
import concourse.mybir as mybir
from concourse.bass_utils import run_bass_kernel_spmd

F32 = mybir.dt.float32
BF16 = mybir.dt.bfloat16
AF = mybir.ActivationFunctionType
ALU = mybir.AluOpType
AX = mybir.AxisListType

N_CORES = 8
B, S, D, H, DH, MLP = 16, 2048, 1024, 16, 64, 4096
BC = B // N_CORES
T = BC * S
NT = T // 128
NE = 16
IM = 512
I1 = 2816
EPS0 = 1e-5
EPS1 = 1e-6
KEPS = 1e-3
RMS_EPS = 1e-6
FP4_MIDS = [0.25, 0.75, 1.25, 1.75, 2.5, 3.5, 5.0]
FP4_STEP = [0.5, 0.5, 0.5, 0.5, 1.0, 1.0, 2.0]
GELU_C = 0.7978845608028654
GELU_A = 0.044715
TEST_NO_B2 = False
TEST_NO_GELU = False



_ENGINES = {
    "pe": "tensor",
    "ve": "vector",
    "se": "scalar",
    "po": "gpsimd",
    "sy": "sync",
}


class _Op:
    __slots__ = ("fn", "args", "kwargs", "kind", "marked", "waits", "sub",
                 "emitted", "tag", "grp", "mval", "grp_sem")

    def __init__(self, fn, args, kwargs, kind, sub, tag=""):
        self.fn = fn
        self.args = args
        self.kwargs = kwargs
        self.kind = kind
        self.marked = kind == "d"
        self.waits = []
        self.sub = sub
        self.emitted = False
        self.tag = tag


class Tok:
    __slots__ = ("stream", "kind", "sub", "op", "clock", "grp")

    def __init__(self, stream, kind, sub, op, clock, grp=None):
        self.stream = stream
        self.kind = kind
        self.sub = sub
        self.op = op
        self.clock = clock
        self.grp = grp


class _Stream:
    def __init__(self, name):
        self.name = name
        self.ops = []
        self.nsub = {"c": 0, "d": 0}
        self.clock = {}
        self.last_tok = None
        self.last_by_kind = {}
        self.marked_prefix = {"c": [0], "d": [0]}
        self.emitted_sub = {"c": 0, "d": 0}


def _key(x):
    if isinstance(x, tuple):
        base, sub = x
        return (_key(base)[0], sub)
    if isinstance(x, bass.AP):
        return (x.tensor.name, None)
    return (x.name, None)


class BK:

    def __init__(self, nc, ctx):
        self.nc = nc
        self.ctx = ctx
        self.streams = {n: _Stream(n) for n in _ENGINES}
        self.sems = {}
        for n in _ENGINES:
            self.sems[(n, "c")] = ctx.enter_context(nc.semaphore(f"p_{n}"))
        self.grp_sem = {}
        self.grp_count = {}
        self.grp_ck = {}
        self.grp_queue = {}
        self.last_w = {}
        self.readers = {}
        self.psum_acc = {}

    SEM_ROLL = 30000

    def _grp(self, grp, inc):
        if grp not in self.grp_sem:
            self.grp_sem[grp] = self.ctx.enter_context(
                self.nc.semaphore(f"g_{grp}"))
            self.grp_count[grp] = 0
            self.grp_raw = getattr(self, "grp_raw", {})
            self.grp_gen = getattr(self, "grp_gen", {})
            self.grp_raw[grp] = 0
            self.grp_gen[grp] = 0
            self.grp_inc = getattr(self, "grp_inc", {})
            self.grp_inc[grp] = inc
        return self.grp_sem[grp]

    def _make_ck(self, grp):
        qn = self.grp_queue[grp]
        qst = self.streams[qn]
        cur = self.grp_count[grp]
        wo = _Op(None, (), {}, "w", 0, f"ck_{grp}")
        wo.grp = None
        wo.waits.append(("S", self.grp_sem[grp],
                         self.grp_inc[grp] * self.grp_raw[grp]))
        qst.ops.append(wo)
        cktok = self.op(qn, "nop", kind="c", mark=True, tag=f"ck_{grp}{cur}")
        cktok.clock[("D", grp)] = cur
        qst.clock[("D", grp)] = cur
        self.grp_ck[grp] = (cur, cktok)
        return self.grp_ck[grp]

    def _maybe_roll(self, grp):
        inc = self.grp_inc[grp]
        if (self.grp_raw[grp] + 1) * inc > self.SEM_ROLL:
            self._make_ck(grp)
            self.grp_gen[grp] += 1
            self.grp_sem[grp] = self.ctx.enter_context(
                self.nc.semaphore(f"g_{grp}_{self.grp_gen[grp]}"))
            self.grp_raw[grp] = 0

    def _dep(self, op, st, tok):
        if tok is None or tok.op is op:
            return
        cl = st.clock
        if tok.grp is not None:
            grp = tok.grp
            if cl.get(("D", grp), 0) >= tok.sub:
                return
            ck = self.grp_ck.get(grp)
            if ck is None or ck[0] < tok.sub:
                ck = self._make_ck(grp)
            self._dep(op, st, ck[1])
            cl[("D", tok.grp)] = max(cl.get(("D", tok.grp), 0), ck[0])
            return
        k = (tok.stream, tok.kind)
        if cl.get(k, 0) >= tok.sub:
            return
        if tok.op.emitted and not tok.op.marked:
            raise RuntimeError(
                f"late dependency on already-emitted unmarked op "
                f"{tok.stream}/{tok.kind}/{tok.sub} {tok.op.fn}")
        tok.op.marked = True
        op.waits.append(tok)
        for kk, vv in tok.clock.items():
            if cl.get(kk, 0) < vv:
                cl[kk] = vv
        cl[k] = max(cl.get(k, 0), tok.sub)

    def op(self, stream, fn, *args, reads=(), writes=(), kind="c",
           accum=False, mark=False, tag="", grp=None, **kwargs):
        st = self.streams[stream]
        st.nsub[kind] += 1
        if kind == "d":
            grp = grp or stream
            self._grp(grp, 16)
            self.grp_queue[grp] = stream
            self._maybe_roll(grp)
        elif stream == "po" and fn != "nop":
            grp = grp or "poQ"
            self._grp(grp, 1)
            self.grp_queue[grp] = stream
            self._maybe_roll(grp)
        else:
            grp = None
        o = _Op(fn, args, kwargs, kind, st.nsub[kind], tag)
        o.grp = grp
        if mark:
            o.marked = True
        psum_names = set()
        for r in list(reads) + list(writes):
            if isinstance(r, bass.AP) and isinstance(
                    r.tensor, bass.PSumTensorHandle):
                psum_names.add(r.tensor.name)
            elif isinstance(r, tuple) and isinstance(
                    r[0], bass.PSumTensorHandle):
                psum_names.add(r[0].name)
            elif isinstance(r, bass.PSumTensorHandle):
                psum_names.add(r.name)
        for r in reads:
            name, sub = _key(r)
            d = self.last_w.get(name)
            if d:
                if sub is None:
                    for tok in d.values():
                        self._dep(o, st, tok)
                else:
                    for s2 in (sub, "*"):
                        if s2 in d:
                            self._dep(o, st, d[s2])
        for w in writes:
            name, sub = _key(w)
            d = self.last_w.get(name)
            if d and not accum:
                if sub is None:
                    for tok in d.values():
                        self._dep(o, st, tok)
                else:
                    for s2 in (sub, "*"):
                        if s2 in d:
                            self._dep(o, st, d[s2])
            for rsub, tok in self.readers.get(name, []):
                if accum:
                    continue
                if sub is None or rsub is None or rsub == sub:
                    self._dep(o, st, tok)
        if psum_names and stream != "pe":
            for name in psum_names:
                for (astream, tok2) in self.psum_acc.get(name, []):
                    if astream != stream:
                        self._dep(o, st, tok2)
        tok = Tok(stream, kind, o.sub, o, dict(st.clock), grp)
        if grp is not None:
            self.grp_count[grp] += 1
            self.grp_raw[grp] += 1
            tok.sub = self.grp_count[grp]
            o.grp_sem = self.grp_sem[grp]
        else:
            tok.clock[(stream, kind)] = o.sub
        for r in reads:
            name, sub = _key(r)
            self.readers.setdefault(name, []).append((sub, tok))
        for w in writes:
            name, sub = _key(w)
            sub = "*" if sub is None else sub
            d = self.last_w.setdefault(name, {})
            if sub == "*":
                d.clear()
            d[sub] = tok
            if not accum:
                self.readers[name] = [
                    (rs, rt) for (rs, rt) in self.readers.get(name, [])
                    if not (sub == "*" or rs is None or rs == sub)]
        if psum_names and stream != "pe":
            for name in psum_names:
                lst = [e for e in self.psum_acc.get(name, [])
                       if e[0] == stream]
                lst.append((stream, tok))
                self.psum_acc[name] = lst
        st.ops.append(o)
        st.last_tok = tok
        st.last_by_kind[kind] = tok
        return tok

    def dma(self, queue, out, in_, reads=(), writes=(), tag="", grp=None, **kw):
        return self.op(queue, "dma_start", out, in_, kind="d",
                       reads=list(reads) + [in_], writes=list(writes) + [out],
                       tag=tag, grp=grp, **kw)

    def barrier(self):
        for grp in list(self.grp_sem):
            ck = self.grp_ck.get(grp)
            if ck is None or ck[0] < self.grp_count[grp]:
                self._make_ck(grp)
        toks = [t for st in self.streams.values()
                for t in st.last_by_kind.values()]
        for n, st in self.streams.items():
            o = _Op(None, (), {}, "w", 0, "barrier")
            o.grp = None
            for t in toks:
                self._dep(o, st, t)
            if o.waits:
                st.ops.append(o)

    def flush(self, blk):
        for st in self.streams.values():
            base = st.emitted_sub.get("mc", 0)
            for o in st.ops:
                if (o.kind == "c" and o.marked and o.grp is None):
                    base += 1
                o.mval = base
            st.emitted_sub["mc"] = base
            assert base < self.SEM_ROLL, f"compute sem overflow on {st.name}" 
        for name, st in self.streams.items():
            if not st.ops:
                continue
            ops = st.ops
            sems = self.sems

            def emit(eng, ops=ops, st=st, sems=sems, self=self):
                def wsem(t):
                    if isinstance(t, tuple):
                        return t[1], t[2]
                    return sems[(t.stream, "c")], t.op.mval
                for o in ops:
                    attach = None
                    if o.kind == "w":
                        for t in o.waits:
                            eng.wait_ge(*wsem(t))
                        o.emitted = True
                        continue
                    if o.waits:
                        attach = o.waits[0]
                        for t in o.waits[1:]:
                            eng.wait_ge(*wsem(t))
                    inst = getattr(eng, o.fn)(*o.args, **o.kwargs)
                    if attach is not None:
                        inst._wait_ge(*wsem(attach))
                    if o.grp is not None:
                        inst.then_inc(o.grp_sem, self.grp_inc[o.grp])
                    elif o.marked:
                        inst.then_inc(sems[(st.name, "c")], 1)
                    o.emitted = True

            getattr(blk, _ENGINES[name])(emit)
            st.ops = []



class Ctx:

    def __init__(self, nc, bk, blk, g):
        self.nc = nc
        self.bk = bk
        self.blk = blk
        self.g = g
        self.dram = {}
        self.pers = {}


def _declare_dram(nc, dbg):
    dd = {}

    def ext(name, shape, dt=F32):
        dd[name] = nc.dram_tensor(name, shape, dt, kind="ExternalInput")

    def scratch(name, shape, dt):
        kind = "ExternalOutput" if name in dbg else "Internal"
        dd[name] = nc.dram_tensor(name, shape, dt, kind=kind)

    ext("x_in", [T, D])
    ext("input_ids", [2, 512], mybir.dt.int32)
    ext("ln1_s", [D]); ext("ln1_b", [D]); ext("ln2_s", [D]); ext("ln2_b", [D])
    ext("wq", [D, D]); ext("wk", [D, D]); ext("wv", [D, D]); ext("wo", [D, D])
    ext("w1", [D, MLP]); ext("b1", [MLP]); ext("w2", [MLP, D]); ext("b2", [D])
    ext("gate_w", [NE, D]); ext("gate_b", [NE])
    ext("w13", [NE, 2 * IM, D]); ext("w2m", [NE, D, IM])
    ext("w13s", [2 * IM, D]); ext("w2s", [D, IM])
    ext("wqkv", [3 * D, D]); ext("wout", [D, D])
    ext("qnw", [DH]); ext("knw", [DH])
    ext("wg", [I1, D]); ext("wu", [I1, D]); ext("wd", [D, I1])
    dd["y_out"] = nc.dram_tensor("y_out", [T, D], F32, kind="ExternalOutput")

    scratch("wq_h", [D, D], BF16); scratch("wk_h", [D, D], BF16)
    scratch("wv_h", [D, D], BF16); scratch("wo_h", [D, D], BF16)
    scratch("w1_h", [D, MLP], BF16); scratch("w2_h", [MLP, D], BF16)
    scratch("w13_h", [NE, 2 * IM, D], BF16); scratch("w2m_h", [NE, D, IM], BF16)
    scratch("w13s_h", [2 * IM, D], BF16); scratch("w2s_h", [D, IM], BF16)
    scratch("wqkv_h", [3 * D, D], BF16); scratch("wout_h", [D, D], BF16)
    scratch("gate_w_h", [NE, D], BF16)
    scratch("wgq_h", [I1, D], BF16); scratch("wuq_h", [I1, D], BF16)
    scratch("wdq_h", [D, I1], BF16)

    scratch("xn0", [T, D], F32)
    scratch("xn0_h", [T, D], BF16)
    scratch("xln1_h", [T, D], BF16)
    scratch("x1", [T, D], F32)
    scratch("xln2_h", [T, D], BF16)
    scratch("g0", [T, D], F32)
    scratch("hm_h", [T, IM], BF16)
    scratch("x_res", [T, D], F32)
    scratch("xn1_h", [T, D], BF16)
    scratch("xq_h", [T, D], BF16)
    scratch("qro_h", [T, D], BF16)
    scratch("kro_h", [T, D], BF16)
    scratch("ao_h", [T, D], BF16)
    scratch("vro_h", [T, H * 65], BF16)
    scratch("hq_h", [T, I1], BF16)
    scratch("g1", [T, D], F32)
    return dd


def build_program(dbg=(), stages=("prep", "A", "D", "B", "C", "G", "F")):
    nc = bass.Bass("TRN2", target_bir_lowering=False)
    dd = _declare_dram(nc, set(dbg))
    g = ExitStack()
    with g:
        bk = BK(nc, g)
        blk = g.enter_context(nc.Block())
        cx = Ctx(nc, bk, blk, g)
        cx.dram = dd
        stage_prep(cx, "prep" in stages)
        if "A" in stages:
            stage_A(cx)
        if "D" in stages:
            stage_D(cx)
        if "B" in stages:
            stage_B(cx)
        if "C" in stages:
            stage_C(cx)
        if "E" in stages:
            stage_E(cx)
        if "G" in stages:
            stage_G(cx)
        if "F" in stages:
            stage_F(cx)
        bk.barrier()
        bk.flush(blk)
    return nc


def _end_stage(cx, label=None):
    cx.bk.barrier()
    if label:
        with cx.nc.named_scope(label):
            cx.bk.flush(cx.blk)
    else:
        cx.bk.flush(cx.blk)
    cx.bk.last_w.clear()
    cx.bk.readers.clear()



def stage_prep(cx, do_casts):
    bk, nc, dd, g = cx.bk, cx.nc, cx.dram, cx.g

    if do_casts:
        casts = [("wq", "wq_h"), ("wk", "wk_h"), ("wv", "wv_h"),
                 ("wo", "wo_h"), ("w1", "w1_h"), ("w2", "w2_h"),
                 ("w13s", "w13s_h"), ("w2s", "w2s_h"),
                 ("wqkv", "wqkv_h"), ("wout", "wout_h"), ("gate_w", "gate_w_h")]
        for e in range(NE):
            casts.append((("w13", e), ("w13_h", e)))
            casts.append((("w2m", e), ("w2m_h", e)))
        for src, dst in casts:
            if isinstance(src, tuple):
                s_ap = dd[src[0]][src[1]]
                d_ap = dd[dst[0]][dst[1]]
                rd, wr = (dd[src[0]], src[1]), (dd[dst[0]], dst[1])
            else:
                s_ap, d_ap = dd[src][:], dd[dst][:]
                rd, wr = dd[src], dd[dst]
            bk.op("po", "dma_start", d_ap, s_ap, kind="d",
                  reads=[rd], writes=[wr], tag=f"cast_{src}")

    pers = cx.pers
    ident1 = g.enter_context(nc.sbuf_tensor("ident1", [1, 1], F32))
    bk.op("po", "memset", ident1[:], 1.0, writes=[ident1])
    vecs = [("ln1_s", D), ("ln1_b", D), ("ln2_s", D), ("ln2_b", D),
            ("b1", MLP), ("b2", D)]
    for vec, n in vecs:
        pers[vec] = g.enter_context(
            nc.sbuf_tensor(f"pp_{vec}", [128, n // 128], F32))
    with nc.sbuf_tensor("vrow", [1, MLP], F32) as vrow, \
         nc.psum_tensor("vps", [128, MLP // 128], F32) as vps:
        for vec, n in vecs:
            a = n // 128
            t = pers[vec]
            bk.dma("sy", vrow[:, 0:n], dd[vec][None, :], writes=[vrow],
                   tag=f"vec_{vec}")
            for c in range(a):
                bk.op("pe", "transpose", vps[:, c:c + 1],
                      vrow[:, c * 128:(c + 1) * 128], ident1[:],
                      reads=[vrow, ident1], writes=[(vps, c)], tag=f"vec_{vec}")
            bk.op("ve", "tensor_copy", t[:, 0:a], vps[:, 0:a],
                  reads=[(vps, cc) for cc in range(a)], writes=[t],
                  mark=True, tag=f"vec_{vec}")
        bk.barrier()
        bk.flush(cx.blk)
    ones1 = g.enter_context(nc.sbuf_tensor("ones1", [1, 128], BF16))
    bk.op("po", "memset", ones1[:], 1.0, writes=[ones1])
    brows = [("gate_b", NE), ("qnw", DH), ("knw", DH)]
    for vec, n in brows:
        pers[vec] = g.enter_context(nc.sbuf_tensor(f"rep_{vec}", [128, n], F32))
    with nc.sbuf_tensor("brow_b", [1, DH], BF16) as row, \
         nc.sbuf_tensor("brow_f", [1, DH], F32) as rowf, \
         nc.psum_tensor("brow_ps", [128, DH], F32) as ps:
        for vec, n in brows:
            bk.dma("sy", rowf[:, 0:n], dd[vec][None, :], writes=[rowf],
                   tag=f"row_{vec}", grp="p_vec")
            bk.op("ve", "tensor_copy", row[:, 0:n], rowf[:, 0:n],
                  reads=[rowf], writes=[row])
            bk.op("pe", "matmul", ps[:, 0:n], ones1[:], row[:, 0:n],
                  start=True, stop=True, reads=[ones1, row], writes=[ps])
            bk.op("ve", "tensor_copy", pers[vec][:, 0:n], ps[:, 0:n],
                  reads=[ps], writes=[pers[vec]], mark=True)
        bk.barrier()
        bk.flush(cx.blk)
    mask = g.enter_context(nc.sbuf_tensor("mask_incl", [128, 128], F32))
    bk.op("po", "iota", mask[:], [[1, 128]], base=0, channel_multiplier=-1,
          allow_small_or_imprecise_dtypes=True, writes=[mask])
    bk.op("po", "tensor_scalar", mask[:], mask[:], 0.0, None, ALU.is_ge,
          reads=[mask], writes=[mask])
    pers["mask_incl"] = mask
    identB = g.enter_context(nc.sbuf_tensor("identB", [128, 128], BF16))
    iotaB = g.enter_context(nc.sbuf_tensor("iotaB", [128, 128], F32))
    bk.op("po", "iota", iotaB[:], [[1, 128]], base=0, channel_multiplier=-1,
          allow_small_or_imprecise_dtypes=True, writes=[iotaB])
    bk.op("po", "tensor_scalar", identB[:], iotaB[:], 0.0, None, ALU.is_equal,
          reads=[iotaB], writes=[identB])
    pers["identB"] = identB
    onec = g.enter_context(nc.sbuf_tensor("onec", [128, 1], BF16))
    bk.op("po", "memset", onec[:], 1.0, writes=[onec])
    pers["onec"] = onec
    onesP = g.enter_context(nc.sbuf_tensor("onesP", [128, 64], BF16))
    bk.op("po", "memset", onesP[:], 1.0, writes=[onesP])
    pers["onesP"] = onesP
    onesF = g.enter_context(nc.sbuf_tensor("onesF", [1, 128], F32))
    bk.op("po", "memset", onesF[:], 1.0, writes=[onesF])
    pers["onesF"] = onesF
    b2row2 = g.enter_context(nc.sbuf_tensor("b2row2", [1, D], F32))
    with nc.sbuf_tensor("b2row", [1, D], F32) as b2row:
        bk.dma("sy", b2row[:], dd["b2"][None, :], writes=[b2row], tag="b2row",
               grp="p_vec")
        bk.op("ve", "tensor_scalar_mul", b2row2[:], b2row[:], 2.0,
              reads=[b2row], writes=[b2row2], mark=True)
        bk.barrier()
        bk.flush(cx.blk)
    pers["b2row2"] = b2row2
    _end_stage(cx)



def _ln_tile(bk, sb, se_eps, x_ap, out_f32, out_bf, bufs, tag):
    negm, sq, vs, v1, rstd = bufs
    ve, se = "ve", "se"
    bk.op(ve, "reduce_sum", negm[:], x_ap, axis=AX.X, negate=True,
          reads=[x_ap], writes=[negm], tag=tag)
    bk.op(ve, "tensor_scalar_mul", negm[:], negm[:], 1.0 / D,
          reads=[negm], writes=[negm], tag=tag)
    bk.op(se, "activation", sq[:], x_ap, AF.Square, bias=negm[:, 0:1],
          accum_out=vs[:, 0:1], reads=[x_ap, negm], writes=[sq, vs], tag=tag)
    bk.op(ve, "tensor_scalar", v1[:], vs[:], 1.0 / D, se_eps, ALU.mult, ALU.add,
          reads=[vs], writes=[v1], tag=tag)
    bk.op(se, "activation", rstd[:], v1[:], AF.Ln, reads=[v1], writes=[rstd],
          tag=tag)
    bk.op(se, "activation", rstd[:], rstd[:], AF.Exp, scale=-0.5,
          reads=[rstd], writes=[rstd], tag=tag)
    outs = []
    if out_f32 is not None:
        bk.op(ve, "tensor_scalar", out_f32[:], x_ap, negm[:, 0:1], rstd[:, 0:1],
              ALU.add, ALU.mult, reads=[x_ap, negm, rstd], writes=[out_f32],
              tag=tag)
    if out_bf is not None:
        src = out_f32 if out_f32 is not None else None
        if src is not None:
            bk.op(ve, "tensor_copy", out_bf[:], src[:], reads=[src],
                  writes=[out_bf], tag=tag)
        else:
            bk.op(ve, "tensor_scalar", out_bf[:], x_ap, negm[:, 0:1],
                  rstd[:, 0:1], ALU.add, ALU.mult,
                  reads=[x_ap, negm, rstd], writes=[out_bf], tag=tag)


def stage_A(cx):
    bk, nc, dd = cx.bk, cx.nc, cx.dram
    with ExitStack() as st:
        sb = lambda n, sh, d: st.enter_context(nc.sbuf_tensor(n, sh, d))
        NB = 3
        xt = [sb(f"a_x{i}", [128, D], F32) for i in range(NB)]
        xn = [sb(f"a_xn{i}", [128, D], F32) for i in range(NB)]
        xnb = [sb(f"a_xnb{i}", [128, D], BF16) for i in range(NB)]
        xlb = [sb(f"a_xlb{i}", [128, D], BF16) for i in range(NB)]
        sq = [sb(f"a_sq{i}", [128, D], F32) for i in range(NB)]
        cols = [[sb(f"a_c{i}_{j}", [128, 1], F32) for j in range(4)]
                for i in range(NB)]
        for t in range(NT):
            i = t % NB
            x_ap = dd["x_in"][t * 128:(t + 1) * 128, :]
            bk.dma("sy", xt[i][:], x_ap, tag=f"A{t}", grp="a_in")
            negm, vs, v1, rstd = cols[i]
            _ln_tile(bk, sb, EPS0, xt[i][:], xn[i], xnb[i],
                     (negm, sq[i], vs, v1, rstd), f"A{t}")
            bk.dma("sy", dd["xn0"][t * 128:(t + 1) * 128, :], xn[i][:],
                   reads=[xn[i]], writes=[(dd["xn0"], t)], tag=f"A{t}",
                   grp="a_out")
            bk.dma("sy", dd["xn0_h"][t * 128:(t + 1) * 128, :], xnb[i][:],
                   reads=[xnb[i]], writes=[(dd["xn0_h"], t)], tag=f"A{t}",
                   grp="a_out")
            _ln_tile(bk, sb, EPS1, xn[i][:], None, xlb[i],
                     (negm, sq[i], vs, v1, rstd), f"A{t}l")
            bk.dma("sy", dd["xln1_h"][t * 128:(t + 1) * 128, :], xlb[i][:],
                   reads=[xlb[i]], writes=[(dd["xln1_h"], t)], tag=f"A{t}",
                   grp="a_out")
        _end_stage(cx, "stA")


def stage_D(cx):
    bk, nc, dd, pers = cx.bk, cx.nc, cx.dram, cx.pers
    TH = T // 2
    NK = TH // 128
    with ExitStack() as st:
        sb = lambda n, sh, d: st.enter_context(nc.sbuf_tensor(n, sh, d))
        ps_ = lambda n, sh: st.enter_context(nc.psum_tensor(n, sh, F32))
        gwn = sb("d_gwn", [16, D], BF16)
        gwT = sb("d_gwT", [128, 8 * NE], BF16)
        psg = st.enter_context(nc.psum_tensor("d_psg", [128, NE], BF16))
        bk.dma("sy", gwn[:], dd["gate_w_h"][:], tag="gwT", grp="d_x")
        for c in range(8):
            bk.op("pe", "transpose", psg[:], gwn[:, c * 128:(c + 1) * 128],
                  pers["identB"][0:16, 0:16], reads=[gwn, pers["identB"]],
                  writes=[psg], tag="gwT")
            bk.op("ve", "tensor_copy", gwT[:, c * NE:(c + 1) * NE], psg[:],
                  reads=[psg], writes=[(gwT, c)], tag="gwT")
        xT = [sb(f"d_xT{c}", [128, TH], BF16) for c in range(8)]
        w13T = [[sb(f"d_w13T{p}_{c}", [128, 2 * IM], BF16) for c in range(8)]
                for p in range(2)]
        w2T = [[sb(f"d_w2T{p}_{c}", [128, D], BF16) for c in range(4)]
               for p in range(2)]
        hT = [[sb(f"d_hT{pp_}_{c}", [128, 512], BF16) for c in range(4)]
              for pp_ in range(2)]
        y_sb = sb("d_y", [128, NK * D], F32)
        comb = sb("d_comb", [128, NK * NE], F32)
        S_sb = sb("d_S", [128, NE], F32)
        biased = sb("d_bias", [128, NE], F32)
        eq = sb("d_eq", [128, NE], F32)
        tmp = sb("d_tmp", [128, NE], F32)
        mx = sb("d_mx", [128, 1], F32)
        ssum = sb("d_ssum", [128, 1], F32)
        sg = [sb(f"d_sg{p}", [128, 512], F32) for p in range(2)]
        hbf = [sb(f"d_hbf{p}", [128, 512], BF16) for p in range(2)]
        g0t = sb("d_g0t", [128, D], F32)
        psc = ps_("d_psc", [128, NE])
        pm = [ps_(f"d_pm{i}", [128, 512]) for i in range(4)]
        py = [ps_(f"d_py{i}", [128, 512]) for i in range(2)]
        for hf in range(2):
            t0 = hf * TH
            for c in range(8):
                bk.dma("se", xT[c][:],
                       dd["xn0_h"][t0:t0 + TH, c * 128:(c + 1) * 128],
                       transpose=True,
                       reads=[(dd["xn0_h"], t) for t in range(hf * NK, (hf + 1) * NK)],
                       writes=[xT[c]], tag=f"D{hf}xT", grp="d_x")
            for k in range(NK):
                kk = slice(k * 128, (k + 1) * 128)
                for c in range(8):
                    bk.op("pe", "matmul", psc[:], xT[c][:, kk],
                          gwT[:, c * NE:(c + 1) * NE],
                          start=(c == 0), stop=(c == 7),
                          reads=[xT[c], (gwT, c)], writes=[psc],
                          accum=(c > 0), tag=f"D{hf}r{k}")
                bk.op("se", "activation", S_sb[:], psc[:], AF.Sigmoid,
                      reads=[psc], writes=[S_sb], tag=f"D{hf}r{k}")
                bk.op("ve", "tensor_add", biased[:], S_sb[:], pers["gate_b"][:, 0:NE],
                      reads=[S_sb, pers["gate_b"]], writes=[biased], tag=f"D{hf}r{k}")
                cslice = slice(k * NE, (k + 1) * NE)
                for r in range(4):
                    bk.op("ve", "reduce_max", mx[:], biased[:], axis=AX.X,
                          reads=[biased], writes=[mx], tag=f"D{hf}r{k}")
                    bk.op("ve", "tensor_single_scalar", eq[:], biased[:],
                          mx[:, 0:1], ALU.is_equal,
                          reads=[biased, mx], writes=[eq], tag=f"D{hf}r{k}")
                    if r == 0:
                        bk.op("ve", "tensor_mul", comb[:, cslice], eq[:], S_sb[:],
                              reads=[eq, S_sb], writes=[(comb, k)], tag=f"D{hf}r{k}")
                    else:
                        bk.op("ve", "tensor_mul", tmp[:], eq[:], S_sb[:],
                              reads=[eq, S_sb], writes=[tmp], tag=f"D{hf}r{k}")
                        bk.op("ve", "tensor_add", comb[:, cslice], comb[:, cslice],
                              tmp[:], reads=[(comb, k), tmp], writes=[(comb, k)],
                              tag=f"D{hf}r{k}")
                    if r < 3:
                        bk.op("ve", "tensor_scalar_mul", tmp[:], eq[:], -1e9,
                              reads=[eq], writes=[tmp], tag=f"D{hf}r{k}")
                        bk.op("ve", "tensor_add", biased[:], biased[:], tmp[:],
                              reads=[biased, tmp], writes=[biased], tag=f"D{hf}r{k}")
                bk.op("ve", "reduce_sum", ssum[:], comb[:, cslice], axis=AX.X,
                      reads=[(comb, k)], writes=[ssum], tag=f"D{hf}r{k}")
                bk.op("ve", "reciprocal", ssum[:], ssum[:],
                      reads=[ssum], writes=[ssum], tag=f"D{hf}r{k}")
                bk.op("ve", "tensor_scalar_mul", ssum[:], ssum[:], 2.5,
                      reads=[ssum], writes=[ssum], tag=f"D{hf}r{k}")
                bk.op("ve", "tensor_single_scalar", comb[:, cslice], comb[:, cslice],
                      ssum[:, 0:1], ALU.mult,
                      reads=[(comb, k), ssum], writes=[(comb, k)], tag=f"D{hf}r{k}")
            pend = [None]
            for e in range(NE + 1):
                p = e % 2
                shared = e == NE
                w13src = dd["w13s_h"][:] if shared else dd["w13_h"][e]
                w2src = dd["w2s_h"][:] if shared else dd["w2m_h"][e]
                w13key = dd["w13s_h"] if shared else (dd["w13_h"], e)
                w2key = dd["w2s_h"] if shared else (dd["w2m_h"], e)
                for c in range(8):
                    bk.dma("se", w13T[p][c][:], w13src[:, c * 128:(c + 1) * 128],
                           transpose=True, reads=[w13key], writes=[w13T[p][c]],
                           tag=f"D{hf}e{e}w", grp=f"d_w{p}")
                for c in range(4):
                    bk.dma("se", w2T[p][c][:], w2src[:, c * 128:(c + 1) * 128],
                           transpose=True, reads=[w2key], writes=[w2T[p][c]],
                           tag=f"D{hf}e{e}w", grp=f"d_w{p}")
                def emit_mm1(q):
                    for kq in range(4):
                        k = q * 4 + kq
                        kk = slice(k * 128, (k + 1) * 128)
                        pg, pu = pm[2 * (k % 2)], pm[2 * (k % 2) + 1]
                        tg = f"D{hf}e{e}k{k}"
                        for half_o in range(2):
                            dst = pg if half_o == 0 else pu
                            for c in range(8):
                                bk.op("pe", "matmul", dst[:],
                                      xT[c][:, kk],
                                      w13T[p][c][:, half_o * IM:(half_o + 1) * IM],
                                      start=(c == 0), stop=(c == 7),
                                      reads=[xT[c], w13T[p][c]], writes=[dst],
                                      accum=(c > 0), tag=tg)
                        bk.op("se", "activation", sg[k % 2][:], pg[:], AF.Silu,
                              reads=[pg], writes=[sg[k % 2]], tag=tg)
                        if shared:
                            bk.op("ve", "scalar_tensor_tensor", hbf[k % 2][:],
                                  sg[k % 2][:], 1.0, pu[:], ALU.mult, ALU.mult,
                                  reads=[sg[k % 2], pu], writes=[hbf[k % 2]],
                                  tag=tg)
                        else:
                            bk.op("ve", "scalar_tensor_tensor", hbf[k % 2][:],
                                  sg[k % 2][:], comb[:, k * NE + e:k * NE + e + 1],
                                  pu[:], ALU.mult, ALU.mult,
                                  reads=[sg[k % 2], pu, (comb, k)],
                                  writes=[hbf[k % 2]], tag=tg)
                        bk.dma("sy", dd["hm_h"][t0 + k * 128:t0 + (k + 1) * 128, :],
                               hbf[k % 2][:], reads=[hbf[k % 2]],
                               writes=[(dd["hm_h"], (hf, k))],
                               tag=tg, grp="d_hs")

                def emit_hload(q):
                    q0 = t0 + q * 512
                    for c in range(4):
                        bk.dma("se", hT[q % 2][c][:],
                               dd["hm_h"][q0:q0 + 512, c * 128:(c + 1) * 128],
                               transpose=True,
                               reads=[(dd["hm_h"], (hf, q * 4 + j)) for j in range(4)],
                               writes=[hT[q % 2][c]], tag=f"D{hf}e{e}q{q}",
                               grp="d_ht")

                def emit_mm2(q, e2, p2):
                    for kq in range(4):
                        k = q * 4 + kq
                        tg = f"D{hf}e{e2}y{k}"
                        for b in range(2):
                            pyb = py[b]
                            for c in range(4):
                                bk.op("pe", "matmul", pyb[:],
                                      hT[q % 2][c][:, kq * 128:(kq + 1) * 128],
                                      w2T[p2][c][:, b * 512:(b + 1) * 512],
                                      start=(c == 0), stop=(c == 3),
                                      reads=[hT[q % 2][c], w2T[p2][c]], writes=[pyb],
                                      accum=(c > 0), tag=tg)
                            ys = y_sb[:, k * D + b * 512:k * D + (b + 1) * 512]
                            if e2 == 0:
                                bk.op("ve", "tensor_copy", ys, pyb[:],
                                      reads=[pyb], writes=[(y_sb, (k, b))],
                                      tag=tg)
                            else:
                                bk.op("ve", "tensor_add", ys, ys, pyb[:],
                                      reads=[pyb, (y_sb, (k, b))],
                                      writes=[(y_sb, (k, b))], tag=tg)

                for q in range(NK // 4):
                    emit_mm1(q)
                    emit_hload(q)
                    if pend[0] is not None:
                        emit_mm2(*pend[0])
                    pend[0] = (q, e, p)
            if pend[0] is not None:
                emit_mm2(*pend[0])
                pend[0] = None
            for k in range(NK):
                bk.op("se", "activation", g0t[:], y_sb[:, k * D:(k + 1) * D],
                      AF.Silu, reads=[(y_sb, (k, 0)), (y_sb, (k, 1))],
                      writes=[g0t], tag=f"D{hf}g{k}")
                bk.dma("sy", dd["g0"][t0 + k * 128:t0 + (k + 1) * 128, :], g0t[:],
                       reads=[g0t], writes=[(dd["g0"], hf * NK + k)],
                       tag=f"D{hf}g{k}", grp="d_gs")
        _end_stage(cx, "stD")


def stage_B(cx):
    bk, nc, dd, pers = cx.bk, cx.nc, cx.dram, cx.pers
    TH = T // 2
    NK = TH // 128
    identB = pers["identB"]
    with ExitStack() as st:
        sb = lambda n, sh, d: st.enter_context(nc.sbuf_tensor(n, sh, d))
        ps_ = lambda n, sh, d=F32: st.enter_context(nc.psum_tensor(n, sh, d))
        phiq = [sb(f"b_phiq{c}", [128, TH], BF16) for c in range(8)]
        phik = [sb(f"b_phik{c}", [128, TH], BF16) for c in range(8)]
        v_sb = sb("b_v", [128, NK * D], BF16)
        atn = [sb(f"b_atn{c}", [128, TH], BF16) for c in range(8)]
        state_f = sb("b_stf", [128, 65], F32)
        state_b = sb("b_stb", [128, 65], BF16)
        Am = [sb(f"b_Am{i}", [128, 128], BF16) for i in range(2)]
        kj = [sb(f"b_kj{i}", [128, 64], BF16) for i in range(2)]
        recip_f = sb("b_recf", [128, 128], F32)
        recip_b = sb("b_recb", [128, 128], BF16)
        bb = sb("b_bb", [128, 128], F32)

        for hf in range(2):
            t0 = hf * TH
            with ExitStack() as stq:
                psq = [stq.enter_context(nc.psum_tensor(
                    f"b1_psq{hf}_{i}", [128, 512], F32)) for i in range(2)]
                xlT = [stq.enter_context(nc.sbuf_tensor(f"b_xlT{hf}_{c}", [128, 512], BF16))
                       for c in range(8)]
                wq_sb = [stq.enter_context(nc.sbuf_tensor(f"b_wq{hf}_{c}", [128, D], BF16))
                         for c in range(8)]
                wk_sb = [stq.enter_context(nc.sbuf_tensor(f"b_wk{hf}_{c}", [128, D], BF16))
                         for c in range(8)]
                wv_sb = [stq.enter_context(nc.sbuf_tensor(f"b_wv{hf}_{c}", [128, D], BF16))
                         for c in range(8)]
                for c in range(8):
                    bk.dma("sy", wq_sb[c][:], dd["wq_h"][c * 128:(c + 1) * 128, :],
                           reads=[dd["wq_h"]], tag="Bw", grp="b_w")
                    bk.dma("sy", wk_sb[c][:], dd["wk_h"][c * 128:(c + 1) * 128, :],
                           reads=[dd["wk_h"]], tag="Bw", grp="b_w")
                    bk.dma("sy", wv_sb[c][:], dd["wv_h"][c * 128:(c + 1) * 128, :],
                           reads=[dd["wv_h"]], tag="Bw", grp="b_w")
                for qtr in range(4):
                    q0 = t0 + qtr * 512
                    for c in range(8):
                        bk.dma("se", xlT[c][:],
                               dd["xln1_h"][q0:q0 + 512, c * 128:(c + 1) * 128],
                               transpose=True,
                               reads=[(dd["xln1_h"], q0 // 128 + j) for j in range(4)],
                               writes=[xlT[c]], tag=f"B1x{hf}{qtr}", grp="b_x")
                        bk.op("ve", "tensor_scalar", xlT[c][:], xlT[c][:],
                              pers["ln1_s"][:, c:c + 1],
                              pers["ln1_b"][:, c:c + 1], ALU.mult, ALU.add,
                              reads=[xlT[c], pers["ln1_s"], pers["ln1_b"]],
                              writes=[xlT[c]], tag=f"B1a{hf}{qtr}")
                    for dst, wsb, nm in ((phiq, wq_sb, "q"), (phik, wk_sb, "k")):
                        for dc in range(8):
                            pq = psq[dc % 2]
                            for c in range(8):
                                bk.op("pe", "matmul", pq[:],
                                      wsb[c][:, dc * 128:(dc + 1) * 128],
                                      xlT[c][:], start=(c == 0), stop=(c == 7),
                                      reads=[wsb[c], xlT[c]], writes=[pq],
                                      accum=(c > 0), tag=f"B1{nm}{hf}{qtr}{dc}")
                            bk.op("ve", "tensor_scalar",
                                  dst[dc][:, qtr * 512:(qtr + 1) * 512], pq[:],
                                  0.0, KEPS, ALU.max, ALU.add,
                                  reads=[pq], writes=[(dst[dc], qtr)],
                                  tag=f"B1{nm}e{hf}{qtr}{dc}")
                    for j in range(4):
                        tt = qtr * 4 + j
                        for b2 in range(2):
                            pv = psq[b2]
                            for c in range(8):
                                bk.op("pe", "matmul", pv[:],
                                      xlT[c][:, j * 128:(j + 1) * 128],
                                      wv_sb[c][:, b2 * 512:(b2 + 1) * 512],
                                      start=(c == 0), stop=(c == 7),
                                      reads=[xlT[c], wv_sb[c]], writes=[pv],
                                      accum=(c > 0), tag=f"B1v{hf}{tt}")
                            bk.op("ve", "tensor_copy",
                                  v_sb[:, tt * D + b2 * 512:tt * D + (b2 + 1) * 512],
                                  pv[:], reads=[pv], writes=[(v_sb, (tt, b2))],
                                  tag=f"B1ve{hf}{tt}")
            bk.barrier()
            st2 = ExitStack()
            psA = [st2.enter_context(nc.psum_tensor(f"b2_psA{i}_{hf}",
                   [128, 128], F32)) for i in range(2)]
            psT1 = st2.enter_context(nc.psum_tensor(f"b2_psT_{hf}",
                   [128, 64], BF16))
            psT = [psT1, psT1]
            psN = [st2.enter_context(nc.psum_tensor(f"b2_psN{i}_{hf}",
                   [128, 128], F32)) for i in range(2)]
            psDen = st2.enter_context(nc.psum_tensor(f"b2_psDen_{hf}",
                                                     [128, 128], F32))
            psB = st2.enter_context(nc.psum_tensor(f"b2_psB_{hf}",
                                                   [128, 128], F32))
            psD = st2.enter_context(nc.psum_tensor(f"b2_psD_{hf}",
                                                   [128, 65], F32))
            for h in range(H):
                o = (h % 2) * 64
                P2 = h // 2
                os_ = slice(o, o + 64)
                bk.op("ve", "memset", state_f[os_, :], 0.0,
                      writes=[(state_f, o)], tag=f"B2z{hf}{h}")
                bk.op("ve", "memset", state_b[os_, :], 0.0,
                      writes=[(state_b, o)], tag=f"B2z{hf}{h}")
                for c in range(NK):
                    cs = slice(c * 128, (c + 1) * 128)
                    pq_ap = phiq[P2][os_, cs]
                    pk_ap = phik[P2][os_, cs]
                    v_c = v_sb[:, c * D + h * 64:c * D + h * 64 + 64]
                    tg = f"B2{hf}h{h}c{c}"
                    bk.op("pe", "matmul", psA[c % 2][:], pk_ap, pq_ap, start=True,
                          stop=True, reads=[(phik[P2], c), (phiq[P2], c)],
                          writes=[psA[c % 2]], tag=tg)
                    bk.op("ve", "tensor_tensor", Am[c % 2][:], psA[c % 2][:],
                          pers["mask_incl"][:], ALU.mult,
                          reads=[psA[c % 2], pers["mask_incl"]], writes=[Am[c % 2]], tag=tg)
                    bk.op("pe", "transpose", psT[c % 2][:], pk_ap,
                          identB[os_, os_], reads=[(phik[P2], c), identB],
                          writes=[psT[c % 2]], tag=tg)
                    bk.op("se", "copy", kj[c % 2][:], psT[c % 2][:],
                          reads=[psT[c % 2]], writes=[kj[c % 2]], tag=tg)
                    bk.op("pe", "matmul", psN[c % 2][os_, :], state_b[os_, 0:64],
                          pq_ap, start=True, stop=False, tile_position=(o, o),
                          reads=[(state_b, o), (phiq[P2], c)], writes=[psN[c % 2]],
                          tag=tg)
                    bk.op("pe", "matmul", psN[c % 2][os_, :], v_c, Am[c % 2][:],
                          start=False, stop=True, tile_position=(0, o),
                          reads=[(v_sb, (c, 0)), (v_sb, (c, 1)), Am[c % 2]],
                          writes=[psN[c % 2]], accum=True, tag=tg)
                    bk.op("pe", "matmul", psDen[o:o + 1, :],
                          state_b[os_, 64:65], pq_ap, start=True, stop=False,
                          tile_position=(o, o),
                          reads=[(state_b, o), (phiq[P2], c)], writes=[psDen],
                          tag=tg)
                    bk.op("pe", "matmul", psDen[o:o + 1, :], pers["onec"][:],
                          Am[c % 2][:], start=False, stop=True, tile_position=(0, o),
                          reads=[pers["onec"], Am[c % 2]], writes=[psDen], accum=True,
                          tag=tg)
                    bk.op("pe", "matmul", psD[os_, 0:64], kj[c % 2][:], v_c,
                          start=True, stop=True, tile_position=(0, o),
                          reads=[kj[c % 2], (v_sb, (c, 0)), (v_sb, (c, 1))],
                          writes=[psD], tag=tg)
                    bk.op("pe", "matmul", psD[os_, 64:65], kj[c % 2][:],
                          pers["onec"][:], start=True, stop=True,
                          tile_position=(0, o), reads=[kj[c % 2], pers["onec"]],
                          writes=[psD], accum=True, tag=tg)
                    bk.op("ve", "tensor_add", state_f[os_, :], state_f[os_, :],
                          psD[os_, :], reads=[(state_f, o), psD],
                          writes=[(state_f, o)], tag=tg)
                    bk.op("se", "copy", state_b[os_, :], state_f[os_, :],
                          reads=[(state_f, o)], writes=[(state_b, o)], tag=tg)
                    bk.op("ve", "reciprocal", recip_f[o:o + 1, :],
                          psDen[o:o + 1, :], reads=[psDen],
                          writes=[recip_f], tag=tg)
                    bk.op("se", "copy", recip_b[o:o + 1, :],
                          recip_f[o:o + 1, :], reads=[recip_f],
                          writes=[recip_b], tag=tg)
                    bk.op("pe", "matmul", psB[os_, :],
                          pers["onesP"][o:o + 1, :], recip_b[o:o + 1, :],
                          start=True, stop=True, tile_position=(o, o),
                          reads=[pers["onesP"], recip_b], writes=[psB], tag=tg)
                    bk.op("se", "copy", bb[os_, :], psB[os_, :],
                          reads=[psB], writes=[bb], tag=tg)
                    bk.op("ve", "tensor_tensor", atn[P2][os_, cs], psN[c % 2][os_, :],
                          bb[os_, :], ALU.mult, reads=[psN[c % 2], bb],
                          writes=[(atn[P2], (o, c))], tag=tg)
            bk.barrier()
            st2.close()
            with ExitStack() as st3:
                psq = [st3.enter_context(nc.psum_tensor(
                    f"b3_psq{hf}_{i}", [128, 512], F32)) for i in range(2)]
                wo_sb = [st3.enter_context(
                    nc.sbuf_tensor(f"b_wo{hf}_{c}", [128, D], BF16))
                    for c in range(8)]
                for c in range(8):
                    bk.dma("sy", wo_sb[c][:],
                           dd["wo_h"][c * 128:(c + 1) * 128, :],
                           reads=[dd["wo_h"]], tag="Bw", grp="b_w")
                x1sb = st3.enter_context(nc.sbuf_tensor(f"b3_x1_{hf}", [128, D], F32))
                xnsb = st3.enter_context(nc.sbuf_tensor(f"b3_xn_{hf}", [128, D], F32))
                xl2b = st3.enter_context(nc.sbuf_tensor(f"b3_xl2_{hf}", [128, D], BF16))
                sq3 = st3.enter_context(nc.sbuf_tensor(f"b3_sq_{hf}", [128, D], F32))
                cols3 = [st3.enter_context(nc.sbuf_tensor(f"b3_c{hf}_{j}", [128, 1], F32))
                         for j in range(4)]
                for tt in range(NK):
                    tg = f"B3{hf}t{tt}"
                    ta = t0 + tt * 128
                    bk.dma("sy", xnsb[:], dd["xn0"][ta:ta + 128, :],
                           reads=[(dd["xn0"], ta // 128)], writes=[xnsb],
                           tag=tg, grp="b_x")
                    for b2 in range(2):
                        pw = psq[b2]
                        for c in range(8):
                            bk.op("pe", "matmul", pw[:],
                                  atn[c][:, tt * 128:(tt + 1) * 128],
                                  wo_sb[c][:, b2 * 512:(b2 + 1) * 512],
                                  start=(c == 0), stop=(c == 7),
                                  reads=[(atn[c], (0, tt)), (atn[c], (64, tt)),
                                         wo_sb[c]],
                                  writes=[pw], accum=(c > 0), tag=tg)
                        bk.op("ve", "tensor_add",
                              x1sb[:, b2 * 512:(b2 + 1) * 512], pw[:],
                              xnsb[:, b2 * 512:(b2 + 1) * 512],
                              reads=[pw, xnsb], writes=[x1sb], tag=tg)
                    bk.dma("sy", dd["x1"][ta:ta + 128, :], x1sb[:],
                           reads=[x1sb], writes=[(dd["x1"], ta // 128)],
                           tag=tg, grp="b_out")
                    negm, vs, v1c, rstd = cols3
                    _ln_tile(bk, None, EPS1, x1sb[:], None, xl2b,
                             (negm, sq3, vs, v1c, rstd), tg)
                    bk.dma("sy", dd["xln2_h"][ta:ta + 128, :], xl2b[:],
                           reads=[xl2b], writes=[(dd["xln2_h"], ta // 128)],
                           tag=tg, grp="b_out")
            bk.barrier()
        _end_stage(cx, "stB")


def stage_C(cx):
    bk, nc, dd, pers = cx.bk, cx.nc, cx.dram, cx.pers
    NB = T // 512
    NC1, NC2 = MLP // 128, 8
    with ExitStack() as st:
        sb = lambda n, sh, d: st.enter_context(nc.sbuf_tensor(n, sh, d))
        ps_ = lambda n, sh: st.enter_context(nc.psum_tensor(n, sh, F32))
        w1_sb = [[sb(f"c_w1{pp_}_{c}", [128, 512], BF16) for c in range(8)]
                 for pp_ in range(2)]
        w2_sb = [sb(f"c_w2{c}", [128, D], BF16) for c in range(32)]
        hT = [sb(f"c_hT{c}", [128, 512], BF16) for c in range(32)]
        xl2T = [sb(f"c_xl2T{c}", [128, 512], BF16) for c in range(8)]
        tsb = [sb(f"c_t{i}", [128, 512], F32) for i in range(2)]
        usb = [sb(f"c_u{i}", [128, 512], F32) for i in range(2)]
        csb = [sb(f"c_c{i}", [128, 512], F32) for i in range(2)]
        x1t = [sb(f"c_x1t{i}", [128, D], F32) for i in range(2)]
        g0t = [sb(f"c_g0t{i}", [128, D], F32) for i in range(2)]
        xint = [sb(f"c_xint{i}", [128, D], F32) for i in range(2)]
        e_xn = sb("e_xn", [128, D], F32)
        e_xnb = sb("e_xnb", [128, D], BF16)
        e_xqb = sb("e_xqb", [128, D], BF16)
        e_sq = sb("e_sq", [128, D], F32)
        e_cols = [sb(f"e_c{j}", [128, 1], F32) for j in range(4)]
        e_qb = (sb("e_qa", [128, D], F32), sb("e_qp1", [128, D], F32),
                sb("e_qp2", [128, D], F32), sb("e_qc7", [128, D], F32),
                sb("e_qbm", [128, D // 16], F32), sb("e_qsc", [128, D // 16], F32))
        pg = [ps_(f"c_pg{i}", [128, 512]) for i in range(2)]
        py = [ps_(f"c_py{i}", [128, 512]) for i in range(2)]
        for c in range(32):
            bk.dma("sy", w2_sb[c][:], dd["w2_h"][c * 128:(c + 1) * 128, :],
                   reads=[dd["w2_h"]], tag="Cw2", grp="c_w")
        for blk in range(NB):
            t0 = blk * 512
            for c in range(8):
                bk.dma("se", xl2T[c][:],
                       dd["xln2_h"][t0:t0 + 512, c * 128:(c + 1) * 128],
                       transpose=True,
                       reads=[(dd["xln2_h"], t0 // 128 + j) for j in range(4)],
                       writes=[xl2T[c]], tag=f"C{blk}x", grp="c_x")
                bk.op("ve", "tensor_scalar", xl2T[c][:], xl2T[c][:],
                      pers["ln2_s"][:, c:c + 1], pers["ln2_b"][:, c:c + 1],
                      ALU.mult, ALU.add,
                      reads=[xl2T[c], pers["ln2_s"], pers["ln2_b"]],
                      writes=[xl2T[c]], tag=f"C{blk}x")
            for dc in range(NC1):
                tg = f"C{blk}h{dc}"
                gidx = dc // 4
                gp = gidx % 2
                if dc % 4 == 0:
                    for c in range(8):
                        bk.dma("sy", w1_sb[gp][c][:],
                               dd["w1_h"][c * 128:(c + 1) * 128,
                                          gidx * 512:(gidx + 1) * 512],
                               reads=[dd["w1_h"]], writes=[w1_sb[gp][c]],
                               tag=tg, grp="c_w1")
                pp = pg[dc % 2]
                for c in range(8):
                    bk.op("pe", "matmul", pp[:],
                          w1_sb[gp][c][:, (dc % 4) * 128:(dc % 4 + 1) * 128],
                          xl2T[c][:],
                          start=(c == 0), stop=(c == 7),
                          reads=[w1_sb[gp][c], xl2T[c]], writes=[pp],
                          accum=(c > 0), tag=tg)
                i = dc % 2
                b1c = pers["b1"][:, dc:dc + 1]
                bk.op("ve", "tensor_scalar", tsb[i][:], pp[:], b1c, None,
                      ALU.add, reads=[pp, pers["b1"]], writes=[tsb[i]], tag=tg)
                bk.op("se", "activation", usb[i][:], tsb[i][:], AF.Square,
                      reads=[tsb[i]], writes=[usb[i]], tag=tg)
                bk.op("ve", "tensor_scalar", usb[i][:], usb[i][:], GELU_A, 1.0,
                      ALU.mult, ALU.add, reads=[usb[i]], writes=[usb[i]], tag=tg)
                bk.op("ve", "tensor_tensor", csb[i][:], tsb[i][:], usb[i][:],
                      ALU.mult, reads=[tsb[i], usb[i]], writes=[csb[i]], tag=tg)
                bk.op("se", "activation", usb[i][:], csb[i][:], AF.Tanh,
                      scale=GELU_C, reads=[csb[i]], writes=[usb[i]], tag=tg)
                bk.op("ve", "scalar_tensor_tensor", hT[dc][:], usb[i][:], 1.0,
                      tsb[i][:], ALU.add, ALU.mult,
                      reads=[usb[i], tsb[i]], writes=[hT[dc]], tag=tg)
            for j in range(4):
                tt = t0 + j * 128
                ti = tt // 128
                jp = j % 2
                tg = f"C{blk}y{j}"
                bk.dma("sy", x1t[jp][:], dd["x1"][tt:tt + 128, :],
                       reads=[(dd["x1"], ti)], writes=[x1t[jp]], tag=tg, grp="c_in")
                bk.dma("sy", g0t[jp][:], dd["g0"][tt:tt + 128, :],
                       reads=[(dd["g0"], ti)], writes=[g0t[jp]], tag=tg, grp="c_in")
                bk.dma("sy", xint[jp][:], dd["x_in"][tt:tt + 128, :],
                       writes=[xint[jp]], tag=tg, grp="c_in")
                for b2 in range(2):
                    pp = py[b2]
                    ds_ = slice(b2 * 512, (b2 + 1) * 512)
                    bk.op("pe", "matmul", pp[:], pers["onesF"][:],
                          pers["b2row2"][0:1, ds_], start=True, stop=False,
                          reads=[pers["onesF"], pers["b2row2"]], writes=[pp],
                          tag=tg)
                    for c in range(32):
                        bk.op("pe", "matmul", pp[:],
                              hT[c][:, j * 128:(j + 1) * 128],
                              w2_sb[c][:, ds_], start=False, stop=(c == 31),
                              reads=[hT[c], w2_sb[c]], writes=[pp],
                              accum=True, tag=tg)
                    bk.op("ve", "scalar_tensor_tensor", x1t[jp][:, ds_], pp[:], 0.5,
                          x1t[jp][:, ds_], ALU.mult, ALU.add,
                          reads=[pp, x1t[jp]], writes=[x1t[jp]], tag=tg)
                    bk.op("ve", "tensor_mul", g0t[jp][:, ds_], x1t[jp][:, ds_],
                          g0t[jp][:, ds_], reads=[x1t[jp], g0t[jp]],
                          writes=[g0t[jp]], tag=tg)
                    bk.op("ve", "tensor_add", xint[jp][:, ds_], g0t[jp][:, ds_],
                          xint[jp][:, ds_], reads=[g0t[jp], xint[jp]],
                          writes=[(xint[jp], b2)], tag=tg)
                bk.dma("sy", dd["x_res"][tt:tt + 128, :], xint[jp][:],
                       reads=[(xint[jp], 0), (xint[jp], 1)],
                       writes=[(dd["x_res"], ti)], tag=tg, grp="c_out")
                negm, vs, v1, rstd = e_cols
                _ln_tile(bk, None, EPS0, xint[jp][:], e_xn, e_xnb,
                         (negm, e_sq, vs, v1, rstd), tg + "e")
                bk.dma("sy", dd["xn1_h"][tt:tt + 128, :], e_xnb[:],
                       reads=[e_xnb], writes=[(dd["xn1_h"], ti)], tag=tg,
                       grp="c_out")
                emit_fp4(bk, pers, e_xn[:], e_xqb[:], e_qb, D, tg + "q")
                bk.dma("sy", dd["xq_h"][tt:tt + 128, :], e_xqb[:],
                       reads=[e_xqb], writes=[(dd["xq_h"], ti)], tag=tg,
                       grp="c_out")
        _end_stage(cx, "stC")




def emit_fp4(bk, pers, x_ap, out_ap, bufs, N, tag):
    absx, p1, p2, c7, bm, sc = bufs
    NB = N // 16
    a3 = lambda t: t[:, 0:N].rearrange("p (b i) -> p b i", i=16)
    scb = bass.AP(sc, 0, [[sc.shape[1], 128], [1, NB], [0, 16]])
    bk.op("se", "activation", absx[:, 0:N], x_ap, AF.Abs,
          reads=[x_ap], writes=[absx], tag=tag)
    bk.op("ve", "reduce_max", bm[:, 0:NB].rearrange("p (b o) -> p b o", o=1),
          a3(absx), axis=AX.X, reads=[absx], writes=[bm], tag=tag)
    bk.op("ve", "tensor_scalar", sc[:, 0:NB], bm[:, 0:NB], 1.0 / 6.0, 1e-12,
          ALU.mult, ALU.max, reads=[bm], writes=[sc], tag=tag)
    bk.op("ve", "reciprocal", bm[:, 0:NB], sc[:, 0:NB],
          reads=[sc], writes=[bm], tag=tag)
    rsb = bass.AP(bm, 0, [[bm.shape[1], 128], [1, NB], [0, 16]])
    bk.op("ve", "tensor_tensor", a3(absx), a3(absx), rsb, ALU.mult,
          reads=[absx, bm], writes=[absx], tag=tag)
    bk.op("ve", "tensor_scalar", p1[:, 0:N], absx[:, 0:N], 0.25, None,
          ALU.is_gt, reads=[absx], writes=[p1], tag=tag)
    for t in (0.75, 1.25, 1.75):
        bk.op("ve", "scalar_tensor_tensor", p1[:, 0:N], absx[:, 0:N], t,
              p1[:, 0:N], ALU.is_gt, ALU.add, reads=[absx, p1], writes=[p1],
              tag=tag)
    bk.op("ve", "tensor_scalar", p2[:, 0:N], absx[:, 0:N], 2.5, None,
          ALU.is_gt, reads=[absx], writes=[p2], tag=tag)
    bk.op("ve", "scalar_tensor_tensor", p2[:, 0:N], absx[:, 0:N], 3.5,
          p2[:, 0:N], ALU.is_gt, ALU.add, reads=[absx, p2], writes=[p2],
          tag=tag)
    bk.op("ve", "tensor_scalar", c7[:, 0:N], absx[:, 0:N], 5.0, 2.0,
          ALU.is_gt, ALU.mult, reads=[absx], writes=[c7], tag=tag)
    bk.op("ve", "tensor_add", p2[:, 0:N], p2[:, 0:N], c7[:, 0:N],
          reads=[p2, c7], writes=[p2], tag=tag)
    bk.op("ve", "scalar_tensor_tensor", p1[:, 0:N], p1[:, 0:N], 0.5,
          p2[:, 0:N], ALU.mult, ALU.add, reads=[p1, p2], writes=[p1], tag=tag)
    bk.op("se", "sign", c7[:, 0:N], x_ap, reads=[x_ap], writes=[c7], tag=tag)
    bk.op("ve", "tensor_mul", p1[:, 0:N], p1[:, 0:N], c7[:, 0:N],
          reads=[p1, c7], writes=[p1], tag=tag)
    bk.op("ve", "tensor_tensor",
          out_ap.rearrange("p (b i) -> p b i", i=16), a3(p1),
          bass.AP(sc, 0, [[sc.shape[1], 128], [1, NB], [0, 16]]), ALU.mult,
          reads=[p1, sc], writes=[out_ap], tag=tag)


def stage_E(cx):
    bk, nc, dd, pers = cx.bk, cx.nc, cx.dram, cx.pers
    with ExitStack() as st:
        sb = lambda n, sh, d: st.enter_context(nc.sbuf_tensor(n, sh, d))
        NBUF = 2
        xt = [sb(f"e_x{i}", [128, D], F32) for i in range(NBUF)]
        xn = [sb(f"e_xn{i}", [128, D], F32) for i in range(NBUF)]
        xnb = [sb(f"e_xnb{i}", [128, D], BF16) for i in range(NBUF)]
        xqb = [sb(f"e_xqb{i}", [128, D], BF16) for i in range(NBUF)]
        sq = sb("e_sq", [128, D], F32)
        cols = [sb(f"e_c{j}", [128, 1], F32) for j in range(4)]
        qb = (sb("e_qa", [128, D], F32), sb("e_qp1", [128, D], F32),
              sb("e_qp2", [128, D], F32), sb("e_qc7", [128, D], F32),
              sb("e_qbm", [128, D // 16], F32), sb("e_qsc", [128, D // 16], F32))
        for t in range(NT):
            i = t % NBUF
            tg = f"E{t}"
            bk.dma("sy", xt[i][:], dd["x_res"][t * 128:(t + 1) * 128, :],
                   reads=[(dd["x_res"], t)], writes=[xt[i]], tag=tg,
                   grp="e_in")
            negm, vs, v1, rstd = cols
            _ln_tile(bk, None, EPS0, xt[i][:], xn[i], xnb[i],
                     (negm, sq, vs, v1, rstd), tg)
            bk.dma("sy", dd["xn1_h"][t * 128:(t + 1) * 128, :], xnb[i][:],
                   reads=[xnb[i]], writes=[(dd["xn1_h"], t)], tag=tg,
                   grp="e_out")
            emit_fp4(bk, pers, xn[i][:], xqb[i][:], qb, D, tg)
            bk.dma("sy", dd["xq_h"][t * 128:(t + 1) * 128, :], xqb[i][:],
                   reads=[xqb[i]], writes=[(dd["xq_h"], t)], tag=tg,
                   grp="e_out")
        _end_stage(cx, "stE")


def stage_G(cx):
    bk, nc, dd, pers = cx.bk, cx.nc, cx.dram, cx.pers
    TH = T // 2
    NK = TH // 128
    NB6 = (I1 + 511) // 512
    with ExitStack() as st:
        sb = lambda n, sh, d: st.enter_context(nc.sbuf_tensor(n, sh, d))
        ps_ = lambda n, sh: st.enter_context(nc.psum_tensor(n, sh, F32))
        with ExitStack() as stq:
            wt = [stq.enter_context(nc.sbuf_tensor(f"g_wt{i}", [128, I1], F32))
                  for i in range(2)]
            wq = [stq.enter_context(nc.sbuf_tensor(f"g_wq{i}", [128, I1], BF16))
                  for i in range(2)]
            qb = (stq.enter_context(nc.sbuf_tensor("g_qa", [128, I1], F32)),
                  stq.enter_context(nc.sbuf_tensor("g_qp1", [128, I1], F32)),
                  stq.enter_context(nc.sbuf_tensor("g_qp2", [128, I1], F32)),
                  stq.enter_context(nc.sbuf_tensor("g_qc7", [128, I1], F32)),
                  stq.enter_context(nc.sbuf_tensor("g_qbm", [128, I1 // 16], F32)),
                  stq.enter_context(nc.sbuf_tensor("g_qsc", [128, I1 // 16], F32)))
            jobs = []
            for w_, wq_, rows, cols_ in (("wg", "wgq_h", I1, D),
                                         ("wu", "wuq_h", I1, D),
                                         ("wd", "wdq_h", D, I1)):
                for r0 in range(0, rows, 128):
                    jobs.append((w_, wq_, r0, cols_))
            for ji, (w_, wq_, r0, cols_) in enumerate(jobs):
                i = ji % 2
                tg = f"Gq{w_}{r0}"
                bk.dma("sy", wt[i][:, 0:cols_], dd[w_][r0:r0 + 128, :],
                       reads=[(dd[w_], r0)], writes=[wt[i]], tag=tg,
                       grp="g_wq")
                emit_fp4(bk, pers, wt[i][:, 0:cols_], wq[i][:, 0:cols_],
                         qb, cols_, tg)
                bk.dma("sy", dd[wq_][r0:r0 + 128, :], wq[i][:, 0:cols_],
                       reads=[wq[i]], writes=[(dd[wq_], r0)], tag=tg,
                       grp="g_wqs")
            bk.barrier()
        wgT = [sb(f"g_wgT{c}", [128, I1], BF16) for c in range(8)]
        wuT = [sb(f"g_wuT{c}", [128, I1], BF16) for c in range(8)]
        wdT = [sb(f"g_wdT{c}", [128, D], BF16) for c in range(22)]
        for c in range(8):
            bk.dma("se", wgT[c][:], dd["wgq_h"][:, c * 128:(c + 1) * 128],
                   transpose=True,
                   reads=[(dd["wgq_h"], r0) for r0 in range(0, I1, 128)],
                   writes=[wgT[c]], tag="GwT", grp="g_w")
            bk.dma("se", wuT[c][:], dd["wuq_h"][:, c * 128:(c + 1) * 128],
                   transpose=True,
                   reads=[(dd["wuq_h"], r0) for r0 in range(0, I1, 128)],
                   writes=[wuT[c]], tag="GwT", grp="g_w")
        for c in range(22):
            bk.dma("se", wdT[c][:], dd["wdq_h"][:, c * 128:(c + 1) * 128],
                   transpose=True,
                   reads=[(dd["wdq_h"], r0) for r0 in range(0, D, 128)],
                   writes=[wdT[c]], tag="GwT", grp="g_w")
        xqT = [sb(f"g_xqT{c}", [128, 512], BF16) for c in range(8)]
        hT = [[sb(f"g_hT{pp_}_{c}", [128, 512], BF16) for c in range(22)]
              for pp_ in range(2)]
        sg_1 = sb("g_sg", [128, 512], F32)
        sg = [sg_1, sg_1]
        hf_1 = sb("g_hf", [128, 512], F32)
        hf = [hf_1, hf_1]
        hq = [sb(f"g_hq{i}", [128, 512], BF16) for i in range(2)]
        g1t = sb("g_g1t", [128, D], F32)
        qb2_1 = (sb("g_q2a", [128, 512], F32), sb("g_q2p1", [128, 512], F32),
                 sg_1, sb("g_q2c7", [128, 512], F32),
                 sb("g_q2bm", [128, 32], F32), sb("g_q2sc", [128, 32], F32))
        qb2 = [qb2_1, qb2_1]
        pg = [ps_(f"g_pg{i}", [128, 512]) for i in range(2)]
        pu = [ps_(f"g_pu{i}", [128, 512]) for i in range(2)]
        py = [ps_(f"g_py{i}", [128, 512]) for i in range(2)]
        def g_mm1(hf_, q):
            t0 = hf_ * TH
            for kq in range(4):
                k = q * 4 + kq
                kk = slice(kq * 128, (kq + 1) * 128)
                tg = f"G{hf_}k{k}"
                for blkb in range(NB6):
                    bw = min(512, I1 - blkb * 512)
                    bs = slice(blkb * 512, blkb * 512 + bw)
                    for c in range(8):
                        bk.op("pe", "matmul", pg[blkb % 2][:, 0:bw], xqT[c][:, kk],
                              wgT[c][:, bs], start=(c == 0), stop=(c == 7),
                              reads=[xqT[c], wgT[c]], writes=[pg[blkb % 2]],
                              accum=(c > 0), tag=tg)
                    for c in range(8):
                        bk.op("pe", "matmul", pu[blkb % 2][:, 0:bw], xqT[c][:, kk],
                              wuT[c][:, bs], start=(c == 0), stop=(c == 7),
                              reads=[xqT[c], wuT[c]], writes=[pu[blkb % 2]],
                              accum=(c > 0), tag=tg)
                    bk.op("se", "activation", sg[blkb % 2][:, 0:bw],
                          pg[blkb % 2][:, 0:bw],
                          AF.Silu, reads=[pg[blkb % 2]], writes=[sg[blkb % 2]],
                          tag=tg)
                    bk.op("ve", "tensor_mul", hf[blkb % 2][:, 0:bw],
                          sg[blkb % 2][:, 0:bw],
                          pu[blkb % 2][:, 0:bw], reads=[sg[blkb % 2], pu[blkb % 2]],
                          writes=[hf[blkb % 2]], tag=tg)
                    emit_fp4(bk, pers, hf[blkb % 2][:, 0:bw], hq[blkb % 2][:, 0:bw],
                             qb2[blkb % 2], bw, tg)
                    bk.dma("sy", dd["hq_h"][t0 + k * 128:t0 + (k + 1) * 128,
                                            blkb * 512:blkb * 512 + bw],
                           hq[blkb % 2][:, 0:bw], reads=[hq[blkb % 2]],
                           writes=[(dd["hq_h"], (hf_, k, blkb))], tag=tg,
                           grp="g_hs")

        def g_hload(hf_, q):
            t0 = hf_ * TH
            q0 = t0 + q * 512
            for c in range(22):
                bk.dma("se", hT[q % 2][c][:],
                       dd["hq_h"][q0:q0 + 512, c * 128:(c + 1) * 128],
                       transpose=True,
                       reads=[(dd["hq_h"], (hf_, q * 4 + j, bb))
                              for j in range(4) for bb in range(NB6)],
                       writes=[hT[q % 2][c]], tag=f"G{hf_}q{q}", grp="g_ht")

        def g_mm2(hf_, q):
            t0 = hf_ * TH
            for kq in range(4):
                k = q * 4 + kq
                ti = t0 // 128 + k
                tg = f"G{hf_}y{k}"
                for b2 in range(2):
                    pp = py[b2]
                    for c in range(22):
                        bk.op("pe", "matmul", pp[:],
                              hT[q % 2][c][:, kq * 128:(kq + 1) * 128],
                              wdT[c][:, b2 * 512:(b2 + 1) * 512],
                              start=(c == 0), stop=(c == 21),
                              reads=[hT[q % 2][c], wdT[c]], writes=[pp],
                              accum=(c > 0), tag=tg)
                    bk.op("se", "activation", g1t[:, b2 * 512:(b2 + 1) * 512],
                          pp[:], AF.Silu, reads=[pp], writes=[(g1t, b2)],
                          tag=tg)
                bk.dma("sy", dd["g1"][ti * 128:(ti + 1) * 128, :], g1t[:],
                       reads=[(g1t, 0), (g1t, 1)],
                       writes=[(dd["g1"], ti)], tag=tg, grp="g_gs")

        pendg = None
        for hf_ in range(2):
            t0 = hf_ * TH
            for q in range(NK // 4):
                qq0 = t0 + q * 512
                for c in range(8):
                    bk.dma("se", xqT[c][:],
                           dd["xq_h"][qq0:qq0 + 512, c * 128:(c + 1) * 128],
                           transpose=True,
                           reads=[(dd["xq_h"], qq0 // 128 + j) for j in range(4)],
                           writes=[xqT[c]], tag=f"G{hf_}x{q}", grp="g_x")
                g_mm1(hf_, q)
                g_hload(hf_, q)
                if pendg is not None:
                    g_mm2(*pendg)
                pendg = (hf_, q)
        if pendg is not None:
            g_mm2(*pendg)
        _end_stage(cx, "stG")


def stage_F(cx):
    bk, nc, dd, pers = cx.bk, cx.nc, cx.dram, cx.pers
    TH = T // 2
    NK = TH // 128
    R2PI = 1.0 / (2.0 * math.pi)
    RND_C = 12582912.0
    with ExitStack() as st:
        sb = lambda n, sh, d: st.enter_context(nc.sbuf_tensor(n, sh, d))
        ps_ = lambda n, sh: st.enter_context(nc.psum_tensor(n, sh, F32))
        cosT = sb("f_cos", [128, NK * 32], F32)
        sinT = sb("f_sin", [128, NK * 32], F32)
        with ExitStack() as s0:
            ivrow = s0.enter_context(nc.sbuf_tensor("f_ivr", [1, 32], F32))
            ivrep = s0.enter_context(nc.sbuf_tensor("f_ivrep", [128, 32], F32))
            ivb = s0.enter_context(nc.sbuf_tensor("f_ivb", [1, 32], BF16))
            poscol = s0.enter_context(nc.sbuf_tensor("f_pos", [128, 1], F32))
            ang = s0.enter_context(nc.sbuf_tensor("f_ang", [128, 32], F32))
            tmpa = s0.enter_context(nc.sbuf_tensor("f_tmpa", [128, 32], F32))
            hpi = s0.enter_context(nc.sbuf_tensor("f_hpi", [128, 1], F32))
            psr = s0.enter_context(nc.psum_tensor("f_psr", [128, 32], F32))
            bk.op("po", "iota", ivrow[:], [[1, 32]], base=0,
                  channel_multiplier=0, allow_small_or_imprecise_dtypes=True,
                  writes=[ivrow], tag="Frt")
            bk.op("po", "memset", hpi[:], float(math.pi / 2.0), writes=[hpi],
                  tag="Frt")
            bk.op("se", "activation", ivrow[:], ivrow[:], AF.Exp,
                  scale=float(-math.log(10000.0) / 32.0),
                  reads=[ivrow], writes=[ivrow], tag="Frt")
            bk.op("ve", "tensor_copy", ivb[:], ivrow[:], reads=[ivrow],
                  writes=[ivb], tag="Frt")
            bk.op("pe", "matmul", psr[:], pers["onesF"][:].rearrange(
                  "a b -> a b"), ivrow[:], start=True, stop=True,
                  reads=[pers["onesF"], ivrow], writes=[psr], tag="Frt")
            bk.op("ve", "tensor_copy", ivrep[:], psr[:], reads=[psr],
                  writes=[ivrep], tag="Frt")
            for t in range(NK):
                tg = f"Frt{t}"
                cs = slice(t * 32, (t + 1) * 32)
                bk.op("po", "iota", poscol[:], [[0, 1]], base=t * 128,
                      channel_multiplier=1,
                      allow_small_or_imprecise_dtypes=True,
                      writes=[poscol], tag=tg)
                bk.op("ve", "tensor_single_scalar", ang[:], ivrep[:],
                      poscol[:, 0:1], ALU.mult, reads=[ivrep, poscol],
                      writes=[ang], tag=tg)
                bk.op("ve", "tensor_scalar", tmpa[:], ang[:], R2PI, RND_C,
                      ALU.mult, ALU.add, reads=[ang], writes=[tmpa], tag=tg)
                bk.op("ve", "tensor_scalar", tmpa[:], tmpa[:], -RND_C, None,
                      ALU.add, reads=[tmpa], writes=[tmpa], tag=tg)
                bk.op("ve", "scalar_tensor_tensor", ang[:], tmpa[:],
                      float(-2.0 * math.pi), ang[:], ALU.mult, ALU.add,
                      reads=[tmpa, ang], writes=[ang], tag=tg)
                bk.op("ve", "tensor_scalar", ang[:], ang[:],
                      float(math.pi), float(-math.pi), ALU.min, ALU.max,
                      reads=[ang], writes=[ang], tag=tg)
                bk.op("se", "activation", sinT[:, cs], ang[:], AF.Sin,
                      reads=[ang], writes=[(sinT, t)], tag=tg)
                bk.op("se", "activation", tmpa[:], ang[:], AF.Abs,
                      reads=[ang], writes=[tmpa], tag=tg)
                bk.op("se", "activation", cosT[:, cs], tmpa[:], AF.Sin,
                      scale=-1.0, bias=hpi[:, 0:1],
                      reads=[tmpa, hpi], writes=[(cosT, t)], tag=tg)
        bk.barrier()
        negC = sb("f_negC", [128, 1], F32)
        bk.op("po", "memset", negC[:], -20.0, writes=[negC], tag="FnegC")
        wqkvT = [sb(f"f_wqkvT{c}", [128, 3 * D], BF16) for c in range(8)]
        for c in range(8):
            bk.dma("se", wqkvT[c][:], dd["wqkv_h"][:, c * 128:(c + 1) * 128],
                   transpose=True, reads=[dd["wqkv_h"]], writes=[wqkvT[c]],
                   tag="FwT", grp="f_w")
        for hf in range(2):
            t0 = hf * TH
            with ExitStack() as s1:
                sb1 = lambda n, sh, d: s1.enter_context(
                    nc.sbuf_tensor(f"{n}_{hf}", sh, d))
                xnT = [sb1(f"f_xnT{c}", [128, TH], BF16) for c in range(8)]
                v_sb = sb1("f_v", [128, H * 65], BF16)
                ro = sb1("f_ro", [128, D], F32)
                tmp = sb1("f_tmp", [128, 512], F32)
                tmp2 = sb1("f_tmp2", [128, D], F32)
                ms = sb1("f_ms", [128, H], F32)
                qro = sb1("f_qro", [128, D], BF16)
                pq = [s1.enter_context(nc.psum_tensor(f"f_pq{i}_{hf}",
                      [128, 512], F32)) for i in range(6)]
                for c in range(8):
                    bk.dma("se", xnT[c][:],
                           dd["xn1_h"][t0:t0 + TH, c * 128:(c + 1) * 128],
                           transpose=True,
                           reads=[(dd["xn1_h"], t0 // 128 + j)
                                  for j in range(NK)],
                           writes=[xnT[c]], tag=f"F1x{hf}", grp="f_x")
                for k in range(NK):
                    kk = slice(k * 128, (k + 1) * 128)
                    tg = f"F1{hf}k{k}"
                    for part in range(6):
                        pp = pq[part]
                        for c in range(8):
                            bk.op("pe", "matmul", pp[:], xnT[c][:, kk],
                                  wqkvT[c][:, part * 512:(part + 1) * 512],
                                  start=(c == 0), stop=(c == 7),
                                  reads=[xnT[c], wqkvT[c]], writes=[pp],
                                  accum=(c > 0), tag=tg)
                    if k == 0:
                        bk.op("ve", "memset",
                              bass.AP(v_sb, 64, [[H * 65, 128], [65, H], [1, 1]]),
                              1.0, writes=[(v_sb, "ones")], tag=tg)
                    for b2 in range(2):
                        bk.op("ve", "tensor_copy",
                              bass.AP(v_sb, b2 * 8 * 65,
                                      [[H * 65, 128], [65, 8], [1, 64]]),
                              pq[4 + b2][:].rearrange("p (h d) -> p h d", d=64),
                              reads=[pq[4 + b2], (v_sb, "ones")],
                              writes=[(v_sb, b2)], tag=tg)
                    bk.dma("sy", dd["vro_h"][(t0 // 128 + k) * 128:
                                             (t0 // 128 + k + 1) * 128, :],
                           v_sb[:], reads=[(v_sb, 0), (v_sb, 1), (v_sb, "ones")],
                           writes=[(dd["vro_h"], t0 // 128 + k)], tag=tg,
                           grp="f_qk")
                    for (pa, pb, wrep, dst) in ((0, 1, "qnw", "qro_h"),
                                                (2, 3, "knw", "kro_h")):
                        cs = slice(k * 32, (k + 1) * 32)
                        c3 = bass.AP(cosT, k * 32,
                                     [[cosT.shape[1], 128], [0, 8], [1, 32]])
                        s3 = bass.AP(sinT, k * 32,
                                     [[sinT.shape[1], 128], [0, 8], [1, 32]])
                        for half_i, pp in ((0, pq[pa]), (1, pq[pb])):
                            p3a = pp[:].rearrange("p (h d) -> p h d", d=64)
                            q1 = pp[:].rearrange(
                                "p (h e d) -> p (h e) d", e=2, d=32
                            )
                            ro3 = ro[:, half_i * 512:(half_i + 1) * 512]
                            rr = lambda t_, off: bass.AP(
                                t_.tensor, t_.offset + off,
                                [[t_.ap[0][0], 128], [64, 8], [1, 32]])
                            q1v = bass.AP(pp, 0, [[512, 128], [64, 8], [1, 32]])
                            q2v = bass.AP(pp, 32, [[512, 128], [64, 8], [1, 32]])
                            o1 = bass.AP(ro, half_i * 512,
                                         [[ro.shape[1], 128], [64, 8], [1, 32]])
                            o2 = bass.AP(ro, half_i * 512 + 32,
                                         [[ro.shape[1], 128], [64, 8], [1, 32]])
                            t1 = bass.AP(tmp, 0, [[512, 128], [64, 8], [1, 32]])
                            tg2 = tg + f"r{pa}{half_i}"
                            bk.op("ve", "tensor_tensor", o1, q1v, c3, ALU.mult,
                                  reads=[pp, cosT], writes=[ro], tag=tg2)
                            bk.op("ve", "tensor_tensor", t1, q2v, s3, ALU.mult,
                                  reads=[pp, sinT], writes=[tmp], tag=tg2)
                            bk.op("ve", "tensor_sub", o1, o1, t1,
                                  reads=[ro, tmp], writes=[ro], tag=tg2)
                            bk.op("ve", "tensor_tensor", o2, q2v, c3, ALU.mult,
                                  reads=[pp, cosT], writes=[ro], tag=tg2)
                            bk.op("ve", "tensor_tensor", t1, q1v, s3, ALU.mult,
                                  reads=[pp, sinT], writes=[tmp], tag=tg2)
                            bk.op("ve", "tensor_add", o2, o2, t1,
                                  reads=[ro, tmp], writes=[ro], tag=tg2)
                        tg3 = tg + f"n{pa}"
                        bk.op("ve", "tensor_mul", tmp2[:], ro[:], ro[:],
                              reads=[ro], writes=[tmp2], tag=tg3)
                        bk.op("ve", "reduce_sum",
                              ms[:].rearrange("p (h o) -> p h o", o=1),
                              tmp2[:].rearrange("p (h d) -> p h d", d=64),
                              axis=AX.X, reads=[tmp2], writes=[ms], tag=tg3)
                        bk.op("ve", "tensor_scalar", ms[:], ms[:], 1.0 / 64,
                              RMS_EPS, ALU.mult, ALU.add, reads=[ms],
                              writes=[ms], tag=tg3)
                        bk.op("se", "activation", ms[:], ms[:], AF.Ln,
                              reads=[ms], writes=[ms], tag=tg3)
                        bk.op("se", "activation", ms[:], ms[:], AF.Exp,
                              scale=-0.5, reads=[ms], writes=[ms], tag=tg3)
                        msb = bass.AP(ms, 0, [[ms.shape[1], 128], [1, H], [0, 64]])
                        bk.op("ve", "tensor_tensor",
                              tmp2[:].rearrange("p (h d) -> p h d", d=64),
                              ro[:].rearrange("p (h d) -> p h d", d=64),
                              msb, ALU.mult, reads=[ro, ms], writes=[tmp2],
                              tag=tg3)
                        wr = pers[wrep]
                        wrb = bass.AP(wr, 0, [[wr.shape[1], 128], [0, H], [1, 64]])
                        bk.op("ve", "tensor_tensor",
                              qro[:].rearrange("p (h d) -> p h d", d=64),
                              tmp2[:].rearrange("p (h d) -> p h d", d=64),
                              wrb, ALU.mult, reads=[tmp2, wr], writes=[qro],
                              tag=tg3)
                        ti = t0 // 128 + k
                        bk.dma("sy", dd[dst][ti * 128:(ti + 1) * 128, :],
                               qro[:], reads=[qro], writes=[(dd[dst], ti)],
                               tag=tg3, grp="f_qk")
                bk.barrier()
            with ExitStack() as s2:
                sb2 = lambda n, sh, d: s2.enter_context(
                    nc.sbuf_tensor(f"{n}_{hf}", sh, d))
                v_sb2 = sb2("f2_v", [128, NK * H * 65], BF16)
                qT = sb2("f2_qT", [128, TH], BF16)
                kT = sb2("f2_kT", [128, TH], BF16)
                P_sb = [sb2(f"f2_P{i}", [128, 512], BF16) for i in range(2)]
                rl = sb2("f2_rl", [128, 1], F32)
                ao = sb2("f2_ao", [128, NK * D], BF16)
                psS = [s2.enter_context(nc.psum_tensor(f"f2_psS{i}_{hf}",
                       [128, 512], F32)) for i in range(2)]
                psO = [s2.enter_context(nc.psum_tensor(f"f2_psO{i}_{hf}",
                       [128, 65], F32)) for i in range(4)]
                VW = H * 65
                for k in range(NK):
                    bk.dma("sy", v_sb2[:, k * VW:(k + 1) * VW],
                           dd["vro_h"][(t0 // 128 + k) * 128:
                                       (t0 // 128 + k + 1) * 128, :],
                           reads=[(dd["vro_h"], t0 // 128 + k)],
                           writes=[(v_sb2, k)], tag=f"F2v{hf}", grp="f_qt")
                for hp in range(8):
                    tgh = f"F2{hf}p{hp}"
                    bk.dma("se", qT[:],
                           dd["qro_h"][t0:t0 + TH, hp * 128:(hp + 1) * 128],
                           transpose=True,
                           reads=[(dd["qro_h"], t0 // 128 + j)
                                  for j in range(NK)],
                           writes=[qT], tag=tgh, grp="f_qt")
                    bk.dma("se", kT[:],
                           dd["kro_h"][t0:t0 + TH, hp * 128:(hp + 1) * 128],
                           transpose=True,
                           reads=[(dd["kro_h"], t0 // 128 + j)
                                  for j in range(NK)],
                           writes=[kT], tag=tgh, grp="f_qt")
                    for hsub in range(2):
                        h = hp * 2 + hsub
                        o = hsub * 64
                        os_ = slice(o, o + 64)
                        for Q in range(4):
                            qs = slice(Q * 512, (Q + 1) * 512)
                            NJ = 4 * Q + 4

                            def emit_S(j, tg):
                                js = slice(j * 128, (j + 1) * 128)
                                bk.op("pe", "matmul", psS[j % 2][:], kT[os_, js],
                                      qT[os_, qs], start=True, stop=True,
                                      reads=[qT, kT], writes=[psS[j % 2]], tag=tg)
                                bk.op("se", "activation", P_sb[j % 2][:],
                                      psS[j % 2][:], AF.Exp, scale=0.125,
                                      bias=negC[:, 0:1],
                                      reads=[psS[j % 2], negC],
                                      writes=[P_sb[j % 2]], tag=tg)
                                if j >= 4 * Q:
                                    qt_d = j - 4 * Q
                                    bk.op("ve", "tensor_tensor",
                                          P_sb[j % 2][:, qt_d * 128:(qt_d + 1) * 128],
                                          P_sb[j % 2][:, qt_d * 128:(qt_d + 1) * 128],
                                          pers["mask_incl"][:], ALU.mult,
                                          reads=[P_sb[j % 2], pers["mask_incl"]],
                                          writes=[P_sb[j % 2]], tag=tg)

                            def emit_O(j, tg):
                                for qt_ in range(4):
                                    gq = 4 * Q + qt_
                                    if gq < j:
                                        continue
                                    pO = psO[qt_]
                                    first = (j == 0)
                                    bk.op("pe", "matmul", pO[:, 0:65],
                                          P_sb[j % 2][:, qt_ * 128:(qt_ + 1) * 128],
                                          v_sb2[:, j * VW + h * 65:
                                                j * VW + h * 65 + 65],
                                          start=first, stop=(j == gq),
                                          reads=[P_sb[j % 2], (v_sb2, j)],
                                          writes=[pO], accum=not first, tag=tg)

                            for j in range(NJ):
                                tg = f"{tgh}h{hsub}Q{Q}j{j}"
                                emit_S(j, tg)
                                if j > 0:
                                    emit_O(j - 1, tg)
                            emit_O(NJ - 1, f"{tgh}h{hsub}Q{Q}jz")
                            for qt_ in range(4):
                                tg = f"{tgh}h{hsub}Q{Q}e{qt_}"
                                pO = psO[qt_]
                                bk.op("ve", "reciprocal", rl[:],
                                      pO[:, 64:65], reads=[pO], writes=[rl],
                                      tag=tg)
                                gq = 4 * Q + qt_
                                bk.op("ve", "tensor_single_scalar",
                                      ao[:, gq * D + h * 64:
                                         gq * D + h * 64 + 64],
                                      pO[:, 0:64], rl[:, 0:1], ALU.mult,
                                      reads=[pO, rl],
                                      writes=[(ao, (gq, h))], tag=tg)
                for k in range(NK):
                    ti = t0 // 128 + k
                    bk.dma("sy", dd["ao_h"][ti * 128:(ti + 1) * 128, :],
                           ao[:, k * D:(k + 1) * D],
                           reads=[(ao, (k, hh)) for hh in range(H)],
                           writes=[(dd["ao_h"], ti)], tag=f"F2o{hf}",
                           grp="f_ao")
                bk.barrier()
        with ExitStack() as s3:
            sb3 = lambda n, sh, d: s3.enter_context(nc.sbuf_tensor(n, sh, d))
            woT = [sb3(f"f3_woT{c}", [128, D], BF16) for c in range(8)]
            for c in range(8):
                bk.dma("se", woT[c][:], dd["wout_h"][:, c * 128:(c + 1) * 128],
                       transpose=True, reads=[dd["wout_h"]], writes=[woT[c]],
                       tag="F3w", grp="f_w")
            aoT = [sb3(f"f3_aoT{c}", [128, 512], BF16) for c in range(8)]
            xrt = sb3("f3_xr", [128, D], F32)
            g1t = sb3("f3_g1", [128, D], F32)
            ot = sb3("f3_o", [128, D], F32)
            pw = [s3.enter_context(nc.psum_tensor(f"f3_pw{i}", [128, 512],
                                                  F32)) for i in range(2)]
            for q in range(T // 512):
                q0 = q * 512
                for c in range(8):
                    bk.dma("se", aoT[c][:],
                           dd["ao_h"][q0:q0 + 512, c * 128:(c + 1) * 128],
                           transpose=True,
                           reads=[(dd["ao_h"], q0 // 128 + j)
                                  for j in range(4)],
                           writes=[aoT[c]], tag=f"F3a{q}", grp="f_at")
                for kq in range(4):
                    ti = q * 4 + kq
                    tg = f"F3t{ti}"
                    bk.dma("sy", xrt[:], dd["x_res"][ti * 128:(ti + 1) * 128, :],
                           reads=[(dd["x_res"], ti)], writes=[xrt], tag=tg,
                           grp="f_in")
                    bk.dma("sy", g1t[:], dd["g1"][ti * 128:(ti + 1) * 128, :],
                           reads=[(dd["g1"], ti)], writes=[g1t], tag=tg,
                           grp="f_in")
                    for b2 in range(2):
                        pp = pw[b2]
                        ds_ = slice(b2 * 512, (b2 + 1) * 512)
                        for c in range(8):
                            bk.op("pe", "matmul", pp[:],
                                  aoT[c][:, kq * 128:(kq + 1) * 128],
                                  woT[c][:, ds_], start=(c == 0), stop=(c == 7),
                                  reads=[aoT[c], woT[c]], writes=[pp],
                                  accum=(c > 0), tag=tg)
                        bk.op("ve", "tensor_mul", g1t[:, ds_], pp[:],
                              g1t[:, ds_], reads=[pp, g1t], writes=[g1t],
                              tag=tg)
                        bk.op("ve", "tensor_add", ot[:, ds_], g1t[:, ds_],
                              xrt[:, ds_], reads=[g1t, xrt], writes=[(ot, b2)],
                              tag=tg)
                    bk.dma("sy", dd["y_out"][ti * 128:(ti + 1) * 128, :],
                           ot[:], reads=[(ot, 0), (ot, 1)],
                           writes=[(dd["y_out"], ti)], tag=tg, grp="f_y")
        _end_stage(cx, "stF")



def _prep_inputs(kw):
    x = np.ascontiguousarray(np.asarray(kw["L0_attn_inputs"], dtype=np.float32))
    full = {
        "ln1_s": kw["L0_attn_ln1_scale"], "ln1_b": kw["L0_attn_ln1_bias"],
        "wq": np.asarray(kw["L0_attn_wq"]).reshape(D, D),
        "wk": np.asarray(kw["L0_attn_wk"]).reshape(D, D),
        "wv": np.asarray(kw["L0_attn_wv"]).reshape(D, D),
        "wo": np.asarray(kw["L0_attn_wo"]).reshape(D, D),
        "ln2_s": kw["L0_attn_ln2_scale"], "ln2_b": kw["L0_attn_ln2_bias"],
        "w1": kw["L0_attn_w1"], "b1": kw["L0_attn_b1"],
        "w2": kw["L0_attn_w2"], "b2": kw["L0_attn_b2"],
        "gate_w": kw["L0_ffn_gate_w"], "gate_b": kw["L0_ffn_gate_b"],
        "w13": kw["L0_ffn_w13"], "w2m": kw["L0_ffn_w2"],
        "w13s": kw["L0_ffn_w13_s"], "w2s": kw["L0_ffn_w2_s"],
        "wqkv": kw["L1_attn_w_qkv"], "wout": kw["L1_attn_w_out"],
        "qnw": kw["L1_attn_q_norm_w"], "knw": kw["L1_attn_k_norm_w"],
        "wg": kw["L1_ffn_w_gate"], "wu": kw["L1_ffn_w_up"],
        "wd": kw["L1_ffn_w_down"],
    }
    ids = np.ascontiguousarray(np.asarray(kw["L0_ffn_input_ids"], dtype=np.int32))
    full = {k: np.ascontiguousarray(np.asarray(v, dtype=np.float32))
            for k, v in full.items()}
    full["input_ids"] = ids
    in_maps = []
    for c in range(N_CORES):
        m = dict(full)
        m["x_in"] = np.ascontiguousarray(
            x[c * BC:(c + 1) * BC].reshape(T, D))
        in_maps.append(m)
    return in_maps


def kernel(**kw):
    in_maps = _prep_inputs(kw)
    nc = build_program()
    res = run_bass_kernel_spmd(nc, in_maps, core_ids=list(range(N_CORES)))
    out = np.concatenate(
        [r["y_out"].reshape(BC, S, D) for r in res.results], axis=0)
    return out.astype(np.float32)
